# version 03
# baseline (speedup 1.0000x reference)
import contextlib
import math
import numpy as np
import ml_dtypes
import concourse.bass as bass
import concourse.mybir as mybir
from concourse.bass_utils import run_bass_kernel_spmd

F32 = mybir.dt.float32
BF16 = mybir.dt.bfloat16
I32 = mybir.dt.int32
AF = mybir.ActivationFunctionType
ALU = mybir.AluOpType
AXL = mybir.AxisListType
BF16NP = ml_dtypes.bfloat16

T = 4096
D = 1024
TC = 512
NCH = T // TC
ENGINES = ("sync", "tensor", "vector", "scalar", "gpsimd")
QSEM = {"sync": "Qld", "tensor": "Qt", "vector": "Qv", "scalar": "Qs", "gpsimd": "Qg"}


class Prog:

    def __init__(self, nc, ctx):
        self.nc = nc
        self.ctx = ctx
        self.items = {e: [] for e in ENGINES}
        self.sems = {}
        self.tot = {}
        for s in QSEM.values():
            self.sems[s] = ctx.enter_context(nc.semaphore(s))
            self.tot[s] = 0

    def wait(self, eng, tok):
        if tok is None:
            return
        name, val = tok
        if name.startswith("D_") and val != self.tot[name]:
            raise AssertionError(
                f"unsound DMA wait: stream {name} token {val} < issued {self.tot[name]}")
        if val > 0:
            self.items[eng].append(("wait", name, val))

    def waits(self, eng, toks):
        for t in toks:
            self.wait(eng, t)

    def op(self, eng, fn, inc=False):
        amt = 0
        name = QSEM[eng]
        if inc:
            amt = 16 if eng in ("sync", "gpsimd") and getattr(fn, "_is_dma", False) else 1
            self.tot[name] += amt
        self.items[eng].append(("op", fn, name if inc else None, amt))
        return (name, self.tot[name]) if inc else None

    def dma(self, eng, out, in_, stream=None):
        name = "D_" + (stream if stream is not None else eng)
        if name not in self.sems:
            self.sems[name] = self.ctx.enter_context(self.nc.semaphore(name))
            self.tot[name] = 0
        self.tot[name] += 16
        def fn(e):
            return e.dma_start(out=out, in_=in_)
        self.items[eng].append(("op", fn, name, 16))
        return (name, self.tot[name])

    def mark(self, eng):
        name = QSEM[eng]
        return (name, self.tot[name])

    def barrier(self, producer_engs=None, consumer_engs=None):
        producer_engs = producer_engs or ENGINES
        consumer_engs = consumer_engs or ENGINES
        toks = [self.mark(e) for e in producer_engs]
        toks += [(n, self.tot[n]) for n in self.sems if n.startswith("D_")]
        for ce in consumer_engs:
            for t in toks:
                self.wait(ce, t)

    def emit(self):
        block = self.ctx.enter_context(self.nc.Block())
        for eng_name in ENGINES:
            items = self.items[eng_name]
            if not items:
                continue
            def mk(items=items):
                def body(eng):
                    for it in items:
                        if it[0] == "wait":
                            eng.wait_ge(self.sems[it[1]], it[2])
                        else:
                            _, fn, sem, amt = it
                            inst = fn(eng)
                            if sem is not None:
                                assert inst is not None, "op with inc returned None"
                                inst.then_inc(self.sems[sem], amt)
                return body
            getattr(block, eng_name)(mk())


def ceil_div(a, b):
    return -(-a // b)


def _gget(guards, i):
    g = guards[i]
    return g[0] if isinstance(g, list) else g


def _gset(guards, i, tok):
    if isinstance(guards[i], list):
        guards[i][0] = tok
    else:
        guards[i] = tok



def emit_ln_stats(p, s_in, ntile, dfree, s_mu, s_rstd, s_sq, eps,
                  wait_vec=(), wait_sc=()):
    p.waits("scalar", wait_sc)
    for i in range(ntile):
        p.wait("scalar", getattr(p, "_sq_tok", None))
        p._sq_tok = p.op("scalar",
                         lambda e, i=i: e.activation(s_sq[:], s_in[:, i, :], AF.Square,
                                                     accum_out=s_mu[:, ntile + i:ntile + i + 1]),
                         inc=True)
    sq_tok = p.mark("scalar")
    p.waits("vector", wait_vec)
    red_tok = None
    for i in range(ntile):
        red_tok = p.op("vector",
                       lambda e, i=i: e.reduce_sum(s_mu[:, i:i + 1], s_in[:, i, :], AXL.X),
                       inc=(i == ntile - 1))
    p.wait("vector", red_tok)
    p.wait("vector", sq_tok)
    summ = s_mu[:, 0:ntile]
    msq = s_mu[:, ntile:2 * ntile]
    var = s_mu[:, 2 * ntile:3 * ntile]
    mu = s_mu[:, 3 * ntile:4 * ntile]
    t = p.op("vector", lambda e: e.tensor_tensor(var, summ, summ, ALU.mult), inc=True)
    p.wait("vector", t)
    t = p.op("vector", lambda e: e.scalar_tensor_tensor(
        var, msq, float(dfree), var, ALU.mult, ALU.subtract), inc=True)
    p.wait("vector", t)
    var_tok = p.op("vector", lambda e: e.tensor_scalar(
        var, var, 1.0 / (dfree * dfree), eps, ALU.mult, ALU.add), inc=True)
    mu_tok = p.op("vector", lambda e: e.tensor_scalar_mul(mu, summ, 1.0 / dfree),
                  inc=True)
    p.wait("scalar", var_tok)
    t = p.op("scalar", lambda e: e.activation(var, var, AF.Ln), inc=True)
    p.wait("scalar", t)
    rstd_tok = p.op("scalar",
                    lambda e: e.activation(s_rstd[:, 0:ntile], var, AF.Exp, scale=-0.5),
                    inc=True)
    return rstd_tok, mu_tok


def emit_ln_apply(p, s_in, s_out, ntile, s_mu, s_rstd, rstd_tok, mu_tok=None):
    p.wait("vector", rstd_tok)
    p.wait("vector", mu_tok)
    tok = None
    for i in range(ntile):
        tok = p.op("vector",
                   lambda e, i=i: e.tensor_scalar(s_out[:, i, :], s_in[:, i, :],
                                                  s_mu[:, 3 * ntile + i:3 * ntile + i + 1],
                                                  s_rstd[:, i:i + 1],
                                                  ALU.subtract, ALU.mult),
                   inc=(i == ntile - 1))
    return tok



def emit_dequant(p, s_q, s_sc, s_qf, q_dram, scales_dram, K, N, dest_sbuf=None,
                 dest_dram=None, s_w=None, wait_first=()):
    nk = K // 128
    last = None
    for kc in range(nk):
        if kc == 0:
            p.waits("sync", wait_first)
        else:
            p.wait("sync", p.mark("vector"))
        p.dma("sync", s_q[:, 0:N], q_dram[kc * 128:(kc + 1) * 128, :], stream="dq")
        lv = p.dma("sync", s_sc[:, 0:N],
                   scales_dram[kc:kc + 1, :].partition_broadcast(128).squeeze(1),
                   stream="dq")
        p.wait("vector", lv)
        p.op("vector", lambda e: e.tensor_copy(s_qf[:, 0:N], s_q[:, 0:N]))
        if dest_sbuf is not None:
            last = p.op("vector",
                        lambda e, kc=kc: e.scalar_tensor_tensor(
                            dest_sbuf[:, kc, :], s_qf[:, 0:N], -128.0, s_sc[:, 0:N],
                            ALU.add, ALU.mult), inc=True)
        else:
            if kc >= 1:
                p.wait("vector", last)
            vtok = p.op("vector",
                        lambda e: e.scalar_tensor_tensor(
                            s_w[:, 0:N], s_qf[:, 0:N], -128.0, s_sc[:, 0:N],
                            ALU.add, ALU.mult), inc=True)
            p.wait("gpsimd", vtok)
            last = p.dma("gpsimd", dest_dram[kc * 128:(kc + 1) * 128, :], s_w[:, 0:N], stream="dqst")
    return last


def build_ffn_block(p, sb, ps, h, dbg=None):
    nc = p.nc
    FF = 4096
    NT = TC // 128
    NKC1, NNO1 = D // 128, FF // 128
    NKC2, NNO2 = FF // 128, D // 128
    w2bf = nc.dram_tensor("ffn_w2bf", [FF, D], BF16).ap()
    ybf = nc.dram_tensor("ffn_ybf", [FF, T], BF16).ap()

    s_id16 = sb("f_id16", [128, 128], BF16)
    s_id32 = sb("f_id32", [128, 128], F32)
    s_gb = sb("f_gbeta", [128, 2 * D], F32)
    s_b1 = sb("f_fc1b", [128, NNO1], F32)
    s_b2 = sb("f_fc2b", [128, NNO2], F32)
    p.dma("sync", s_id16[:], h["ident16"], stream="cst")
    p.dma("sync", s_id32[:], h["ident32"], stream="cst")
    p.dma("sync", s_gb[:, 0:D], h["ln_gamma"].partition_broadcast(128).squeeze(1), stream="cst")
    p.dma("sync", s_gb[:, D:2 * D], h["ln_beta"].partition_broadcast(128).squeeze(1), stream="cst")
    p.dma("sync", s_b1[:], h["fc1_bias"], stream="cst")
    cst = p.dma("sync", s_b2[:, 0:NNO2], h["fc2_bias"], stream="cst")

    s_w = sb("f_w", [128, 32, 1024], BF16)
    w1v = s_w[:].rearrange("p a b -> p (a b)").rearrange("p (kc n) -> p kc n", kc=NKC1)
    w2v = s_w[:]

    with contextlib.ExitStack() as dqctx:
        s_q = dqctx.enter_context(nc.sbuf_tensor("dq_q", [128, FF], I32))
        s_sc = dqctx.enter_context(nc.sbuf_tensor("dq_sc", [128, FF], F32))
        s_qf = dqctx.enter_context(nc.sbuf_tensor("dq_qf", [128, FF], F32))
        s_wst = dqctx.enter_context(nc.sbuf_tensor("dq_w", [128, FF], BF16))
        dq1 = emit_dequant(p, s_q, s_sc, s_qf, h["fc1_q"], h["fc1_scales"], D, FF,
                           dest_sbuf=w1v)
        dq2 = emit_dequant(p, s_q, s_sc, s_qf, h["fc2_q"], h["fc2_scales"], FF, D,
                           dest_dram=w2bf, s_w=s_wst, wait_first=[dq1])
    p.barrier()

    s_x = sb("f_x", [128, NT, D], F32)
    s_z1 = sb("f_z1", [128, NT, D], F32)
    s_t = sb("f_t", [128, NT, D], F32)
    s_sq = sb("f_sq", [128, D], F32)
    s_mu = sb("f_mu", [128, 4 * NT], F32)
    s_rstd = sb("f_rstd", [128, NT], F32)
    s_mu2 = sb("f_mu2", [128, 4 * NT], F32)
    s_rstd2 = sb("f_rstd2", [128, NT], F32)
    s_h16 = sb("f_h16", [128, NT, D], BF16)
    s_hF = sb("f_hF", [128, D // 128, TC], BF16)
    s_y16 = sb("f_y16", [128, FF // 128, TC], BF16)
    s_t2 = sb("f_t2", [128, 2, TC], F32)
    p_tr = [ps(f"f_ptr{i}", [128, TC], F32) for i in range(2)]
    p_mm = [ps(f"f_pmm{i}", [128, TC], F32) for i in range(2)]
    pmm_g = [None, None]
    ptr_g = [None, None]

    def ln1(c, war_tok):
        p.wait("sync", war_tok)
        lv = p.dma("sync", s_x[:],
                   h["xin"][c * TC:(c + 1) * TC, :].rearrange("(i p) d -> p i d", p=128),
                   stream="xld")
        rstd_tok, mu_tok = emit_ln_stats(p, s_x, NT, D, s_mu, s_rstd, s_sq, 1e-5,
                                    wait_vec=[lv], wait_sc=[lv])
        z_tok = emit_ln_apply(p, s_x, s_z1, NT, s_mu, s_rstd, rstd_tok, mu_tok)
        return lv, z_tok

    def ln2gb(z_tok):
        rstd_tok, mu_tok = emit_ln_stats(p, s_z1, NT, D, s_mu2, s_rstd2, s_sq, 1e-5,
                                    wait_vec=[], wait_sc=[z_tok])
        emit_ln_apply(p, s_z1, s_t, NT, s_mu2, s_rstd2, rstd_tok, mu_tok)
        gb = s_gb[:, 0:D].unsqueeze(1).broadcast_to((128, NT, D))
        bb = s_gb[:, D:2 * D].unsqueeze(1).broadcast_to((128, NT, D))
        p.op("vector", lambda e: e.tensor_tensor(s_t[:], s_t[:], gb, ALU.mult))
        return p.op("vector", lambda e: e.tensor_tensor(s_h16[:], s_t[:], bb, ALU.add),
                    inc=True)

    prev_consumed = None
    prev_store = None
    for c in range(NCH):
        lv, z_tok = ln1(c, prev_store)
        if prev_consumed is not None:
            p.wait("vector", prev_consumed)
        hv = ln2gb(z_tok)
        p.wait("tensor", hv)
        ev_tok = ptr_g
        hf_toks = []
        for j in range(D // 128):
            bt = j % 2
            p.wait("tensor", ev_tok[bt])
            tr = None
            for i in range(NT):
                tr = p.op("tensor",
                          lambda e, i=i, j=j, bt=bt: e.transpose(
                              p_tr[bt][:].bitcast(BF16)[:, i * 128:(i + 1) * 128],
                              s_h16[:, i, j * 128:(j + 1) * 128], s_id16[:]),
                          inc=(i == NT - 1))
            p.wait("vector", tr)
            ev_tok[bt] = p.op("vector",
                              lambda e, j=j, bt=bt: e.tensor_copy(
                                  s_hF[:, j, :], p_tr[bt][:].bitcast(BF16)[:, 0:TC]),
                              inc=True)
            hf_toks.append(ev_tok[bt])
        if dbg is not None and c == 0:
            p.wait("gpsimd", p.mark("vector"))
            p.dma("gpsimd", dbg["z1"], s_z1[:], stream="dbg")
            p.dma("gpsimd", dbg["h16"], s_h16[:], stream="dbg")
            p.dma("gpsimd", dbg["hF"], s_hF[:], stream="dbg")
        p.waits("tensor", hf_toks[-2:])
        mv_tok = pmm_g
        for no in range(NNO1):
            alt = no % 2
            p.wait("tensor", mv_tok[alt])
            mm = None
            for kc in range(NKC1):
                mm = p.op("tensor",
                          lambda e, no=no, kc=kc, alt=alt: e.matmul(
                              p_mm[alt][:], w1v[:, kc, no * 128:(no + 1) * 128],
                              s_hF[:, kc, :], start=(kc == 0), stop=(kc == NKC1 - 1)),
                          inc=(kc == NKC1 - 1))
            p.wait("scalar", mm)
            mv_tok[alt] = p.op("scalar",
                               lambda e, no=no, alt=alt: e.activation(
                                   s_y16[:, no, :], p_mm[alt][:], AF.Silu,
                                   bias=s_b1[:, no:no + 1]), inc=True)
        mm_all = p.mark("tensor")
        p.wait("gpsimd", p.mark("scalar"))
        if dbg is not None and c == 0:
            p.dma("gpsimd", dbg["y16"], s_y16[:], stream="dbg")
        for q in range(4):
            prev_store = p.dma(
                "gpsimd",
                ybf[q * FF // 4:(q + 1) * FF // 4, c * TC:(c + 1) * TC].rearrange(
                    "(kc p) t -> p kc t", p=128),
                s_y16[:, q * 8:(q + 1) * 8, :], stream="yst")
        prev_consumed = mm_all
    passA_mm = p.mark("tensor")
    passA_st = prev_store

    p.wait("sync", passA_mm)
    wld2 = p.dma("sync", s_w[:], w2bf.rearrange("(kc p) n -> p kc n", p=128), stream="wld")
    s_yF = s_y16

    prev_done = passA_st
    out_tok = None
    for c in range(NCH):
        p.wait("sync", prev_done)
        yld = None
        for q in range(4):
            yld = p.dma("sync",
                        s_yF[:, q * 8:(q + 1) * 8, :],
                        ybf[q * FF // 4:(q + 1) * FF // 4, c * TC:(c + 1) * TC].rearrange(
                            "(kc p) t -> p kc t", p=128), stream="yld")
        lv, z_tok = ln1(c, out_tok)
        p.wait("tensor", yld)
        if c == 0:
            p.wait("tensor", wld2)
        mv_tok = pmm_g
        tr_tok = [None, None]
        pv_tok = ptr_g
        for no in range(NNO2):
            alt = no % 2
            p.wait("tensor", mv_tok[alt])
            mm = None
            for kc in range(NKC2):
                mm = p.op("tensor",
                          lambda e, no=no, kc=kc, alt=alt: e.matmul(
                              p_mm[alt][:], w2v[:, kc, no * 128:(no + 1) * 128],
                              s_yF[:, kc, :], start=(kc == 0), stop=(kc == NKC2 - 1)),
                          inc=(kc == NKC2 - 1))
            p.wait("vector", mm)
            p.wait("vector", tr_tok[alt])
            mv_tok[alt] = p.op("vector",
                               lambda e, no=no, alt=alt: e.tensor_scalar(
                                   s_t2[:, no % 2, :], p_mm[alt][:],
                                   s_b2[:, no:no + 1], 0.5, ALU.add, ALU.mult),
                               inc=True)
            p.wait("tensor", mv_tok[alt])
            if no == 0:
                p.wait("tensor", z_tok)
            p.wait("tensor", pv_tok[alt])
            ptrf = p_tr[alt][:]
            tr = None
            for i in range(NT):
                tr = p.op("tensor",
                          lambda e, i=i, alt=alt, ptrf=ptrf: e.transpose(
                              ptrf[:, i * 128:(i + 1) * 128],
                              s_t2[:, alt, i * 128:(i + 1) * 128], s_id32[:]),
                          inc=(i == NT - 1))
            tr_tok[alt] = tr
            p.wait("vector", tr)
            pv_tok[alt] = p.op(
                "vector",
                lambda e, no=no, ptrf=ptrf: e.tensor_tensor(
                    s_t[:, :, no * 128:(no + 1) * 128],
                    ptrf.rearrange("p (i w) -> p i w", i=NT),
                    s_z1[:, :, no * 128:(no + 1) * 128], ALU.add),
                inc=True)
        pv = p.mark("vector")
        if dbg is not None and c == 0:
            p.wait("gpsimd", pv)
            p.dma("gpsimd", dbg["inner"], s_t[:], stream="dbg")
        rstd_tok, mu_tok = emit_ln_stats(p, s_t, NT, D, s_mu2, s_rstd2, s_sq, 1e-5,
                                    wait_vec=[], wait_sc=[pv])
        emit_ln_apply(p, s_t, s_t, NT, s_mu2, s_rstd2, rstd_tok, mu_tok)
        ov = p.op("vector", lambda e: e.tensor_tensor(s_t[:], s_t[:], s_x[:], ALU.add),
                  inc=True)
        p.wait("gpsimd", ov)
        out_tok = p.dma("gpsimd",
                        h["xout"][c * TC:(c + 1) * TC, :].rearrange("(i p) d -> p i d", p=128),
                        s_t[:], stream="xst")
        prev_done = p.mark("tensor")
    p.wait("gpsimd", out_tok)
    return out_tok



def _ident_np():
    return np.eye(128, dtype=BF16NP), np.eye(128, dtype=np.float32)


def build_ffn_nc(debug=False):
    nc = bass.Bass("TRN2", target_bir_lowering=False, debug=False)
    FF = 4096
    h = {}
    h["xin"] = nc.dram_tensor("xin", [T, D], F32, kind="ExternalInput").ap()
    h["xout"] = nc.dram_tensor("xout", [T, D], F32, kind="ExternalOutput").ap()
    h["ln_gamma"] = nc.dram_tensor("ln_gamma", [1, D], F32, kind="ExternalInput").ap()
    h["ln_beta"] = nc.dram_tensor("ln_beta", [1, D], F32, kind="ExternalInput").ap()
    h["fc1_q"] = nc.dram_tensor("fc1_q", [D, FF], I32, kind="ExternalInput").ap()
    h["fc1_scales"] = nc.dram_tensor("fc1_scales", [D // 128, FF], F32, kind="ExternalInput").ap()
    h["fc1_bias"] = nc.dram_tensor("fc1_bias", [128, FF // 128], F32, kind="ExternalInput").ap()
    h["fc2_q"] = nc.dram_tensor("fc2_q", [FF, D], I32, kind="ExternalInput").ap()
    h["fc2_scales"] = nc.dram_tensor("fc2_scales", [FF // 128, D], F32, kind="ExternalInput").ap()
    h["fc2_bias"] = nc.dram_tensor("fc2_bias", [128, D // 128], F32, kind="ExternalInput").ap()
    h["ident16"] = nc.dram_tensor("ident16", [128, 128], BF16, kind="ExternalInput").ap()
    h["ident32"] = nc.dram_tensor("ident32", [128, 128], F32, kind="ExternalInput").ap()
    dbg = None
    if debug:
        dbg = dict(
            z1=nc.dram_tensor("dbg_z1", [128, 4, D], F32, kind="ExternalOutput").ap(),
            h16=nc.dram_tensor("dbg_h16", [128, 4, D], F32, kind="ExternalOutput").ap(),
            hF=nc.dram_tensor("dbg_hF", [128, 8, TC], F32, kind="ExternalOutput").ap(),
            y16=nc.dram_tensor("dbg_y16", [128, 32, TC], F32, kind="ExternalOutput").ap(),
            inner=nc.dram_tensor("dbg_inner", [128, 4, D], F32, kind="ExternalOutput").ap(),
        )
    with contextlib.ExitStack() as ctx:
        p = Prog(nc, ctx)
        def sb(name, shape, dt):
            return ctx.enter_context(nc.sbuf_tensor(name, shape, dt))
        def ps(name, shape, dt):
            return ctx.enter_context(nc.psum_tensor(name, shape, dt))
        build_ffn_block(p, sb, ps, h, dbg)
        p.emit()
    return nc


def ffn_in_map(x, gamma, beta, fc1_q, fc1_scales, fc1_bias, fc2_q, fc2_scales, fc2_bias):
    id16, id32 = _ident_np()
    return dict(
        xin=np.ascontiguousarray(x, np.float32),
        ln_gamma=np.ascontiguousarray(gamma.reshape(1, D), np.float32),
        ln_beta=np.ascontiguousarray(beta.reshape(1, D), np.float32),
        fc1_q=np.ascontiguousarray(fc1_q, np.int32),
        fc1_scales=np.ascontiguousarray(fc1_scales, np.float32),
        fc1_bias=np.ascontiguousarray(fc1_bias.reshape(32, 128).T, np.float32),
        fc2_q=np.ascontiguousarray(fc2_q, np.int32),
        fc2_scales=np.ascontiguousarray(fc2_scales, np.float32),
        fc2_bias=np.ascontiguousarray(fc2_bias.reshape(8, 128).T, np.float32),
        ident16=id16, ident32=id32,
    )



def tr_tm2fm(p, src, dst, p_tr, s_id16, nj, guards, gidx=(0, 1), nt=None):
    nt = nt if nt is not None else TC // 128
    for j in range(nj):
        bt = j % 2
        p.wait("tensor", _gget(guards, gidx[bt]))
        tr = None
        for i in range(nt):
            tr = p.op("tensor",
                      lambda e, i=i, j=j, bt=bt: e.transpose(
                          p_tr[bt][:].bitcast(BF16)[:, i * 128:(i + 1) * 128],
                          src[:, i, j * 128:(j + 1) * 128], s_id16[:]),
                      inc=(i == nt - 1))
        p.wait("vector", tr)
        _gset(guards, gidx[bt],
              p.op("vector",
                   lambda e, j=j, bt=bt, nt=nt: e.tensor_copy(
                       dst[:, j, :], p_tr[bt][:].bitcast(BF16)[:, 0:nt * 128]),
                   inc=True))
    return [_gget(guards, gidx[0]), _gget(guards, gidx[1])]


def tr_fm2tm(p, src, dst, p_tr, s_id16, nj, guards, gidx=(0, 1), nt=None):
    nt = nt if nt is not None else TC // 128
    for i in range(nt):
        bt = i % 2
        p.wait("tensor", _gget(guards, gidx[bt]))
        tr = None
        for j in range(nj):
            tr = p.op("tensor",
                      lambda e, i=i, j=j, bt=bt: e.transpose(
                          p_tr[bt][:].bitcast(BF16)[:, j * 128:(j + 1) * 128],
                          src[:, j, i * 128:(i + 1) * 128], s_id16[:]),
                      inc=(j == nj - 1))
        p.wait("vector", tr)
        _gset(guards, gidx[bt],
              p.op("vector",
                   lambda e, i=i, bt=bt, nj=nj: e.tensor_copy(
                       dst[:, i, :], p_tr[bt][:].bitcast(BF16)[:, 0:nj * 128]),
                   inc=True))
    return [_gget(guards, gidx[0]), _gget(guards, gidx[1])]


def build_latt_block(p, sb, ps, h, dbg=None):
    nc = p.nc
    NT = TC // 128
    NH, HD = 16, 64
    NJ = D // 128
    qbf = nc.dram_tensor("latt_qbf", [D, T], BF16).ap()

    s_wq = sb("a_wq", [128, NJ, D], BF16)
    s_wk = sb("a_wk", [128, NJ, D], BF16)
    s_wv = sb("a_wv", [128, NJ, D], BF16)
    s_wp = sb("a_wp", [128, NJ, D], BF16)
    s_b = sb("a_b", [128, 4 * NJ], F32)
    s_id16 = sb("a_id16", [128, 128], BF16)
    s_id32 = sb("a_id32", [128, 128], F32)
    wtoks = []
    for nm, s in (("Wq", s_wq), ("Wk", s_wk), ("Wv", s_wv), ("Wp", s_wp)):
        wtoks.append(p.dma("gpsimd", s[:], h[nm].rearrange("(kc p) n -> p kc n", p=128), stream="wld"))
    p.dma("sync", s_b[:, 0:NJ], h["bq"], stream="cst")
    p.dma("sync", s_b[:, NJ:2 * NJ], h["bk"], stream="cst")
    p.dma("sync", s_b[:, 2 * NJ:3 * NJ], h["bv"], stream="cst")
    p.dma("sync", s_b[:, 3 * NJ:4 * NJ], h["bp"], stream="cst")
    p.dma("sync", s_id16[:], h["ident16"], stream="cst")
    cst = p.dma("sync", s_id32[:], h["ident32"], stream="cst")
    p.wait("tensor", wtoks[-1])
    for e in ("tensor", "scalar", "vector"):
        p.wait(e, cst)

    s_x = sb("a_x", [128, NT, D], F32)
    s_z16 = sb("a_z16", [128, NT, D], BF16)
    s_zF = sb("a_zF", [128, NJ, TC], BF16)
    s_sq = sb("a_sq", [128, D], F32)
    s_mu = sb("a_mu", [128, 4 * NT], F32)
    s_rstd = sb("a_rstd", [128, NT], F32)
    s_q16 = sb("a_q16", [128, NJ, TC], BF16)
    s_k16 = sb("a_k16", [128, NJ, TC], BF16)
    s_v16 = sb("a_v16", [128, NJ, TC], BF16)
    s_kT = sb("a_kT", [128, NT, D], BF16)
    s_vT = sb("a_vT", [128, NT, D], BF16)
    s_u = sb("a_u", [128, TC], F32)
    s_tm = sb("a_tm", [128, TC], F32)
    s_kv = sb("a_kv", [128, NJ * HD], BF16)
    s_kv32 = sb("a_kv32", [128, NJ * HD], F32)
    s_kvbd = sb("a_kvbd", [128, NJ * 128], BF16)
    s_zero = sb("a_zero", [128, TC], BF16)
    s_o = sb("a_o", [128, NJ, TC], BF16)
    s_oT = sb("a_oT", [128, 2, TC], F32)
    s_t = sb("a_t", [128, NT, D], F32)
    p_mm = [ps(f"a_pmm{i}", [128, TC], F32) for i in range(2)]
    p_tr = [ps(f"a_ptr{i}", [128, TC], F32) for i in range(2)]
    p_kv = ps("a_pkv", [128, NJ * HD], F32)

    def ln_z16(c, war_tok):
        p.wait("sync", war_tok)
        lv = p.dma("sync", s_x[:],
                   h["xin"][c * TC:(c + 1) * TC, :].rearrange("(i p) d -> p i d", p=128),
                   stream="xld")
        rstd_tok, mu_tok = emit_ln_stats(p, s_x, NT, D, s_mu, s_rstd, s_sq, 1e-5,
                                         wait_vec=[lv], wait_sc=[lv])
        return emit_ln_apply(p, s_x, s_z16, NT, s_mu, s_rstd, rstd_tok, mu_tok)

    pmm_g = [None, None]
    ptr_g = [None, None]
    state = {"prev_exp": None}

    def proj(wv, bcols, dst16, elu):
        evac = pmm_g
        prev_exp = state["prev_exp"]
        tok = None
        for no in range(NJ):
            alt = no % 2
            p.wait("tensor", evac[alt])
            mm = None
            for kc in range(NJ):
                mm = p.op("tensor",
                          lambda e, no=no, kc=kc, alt=alt: e.matmul(
                              p_mm[alt][:], wv[:, kc, no * 128:(no + 1) * 128],
                              s_zF[:, kc, :], start=(kc == 0), stop=(kc == NJ - 1)),
                          inc=(kc == NJ - 1))
            p.wait("vector", mm)
            if elu:
                evac[alt] = p.op("vector",
                                 lambda e, no=no, alt=alt: e.tensor_scalar_add(
                                     s_u[:], p_mm[alt][:], bcols[:, no:no + 1]),
                                 inc=True)
                p.wait("vector", prev_exp)
                t1 = p.op("vector", lambda e: e.tensor_scalar_min(s_tm[:], s_u[:], 0.0),
                          inc=True)
                p.wait("scalar", t1)
                prev_exp = p.op("scalar",
                                lambda e: e.activation(s_tm[:], s_tm[:], AF.Exp),
                                inc=True)
                p.wait("vector", prev_exp)
                tok = p.op("vector",
                           lambda e, no=no: e.scalar_tensor_tensor(
                               dst16[:, no, :], s_u[:], 0.0, s_tm[:],
                               ALU.max, ALU.add), inc=True)
            else:
                evac[alt] = tok = p.op(
                    "vector",
                    lambda e, no=no, alt=alt: e.tensor_scalar_add(
                        dst16[:, no, :], p_mm[alt][:], bcols[:, no:no + 1]),
                    inc=True)
        state["prev_exp"] = prev_exp
        return tok

    zz = p.op("vector", lambda e: e.memset(s_zero[:], 0.0), inc=True)
    zb = p.op("vector", lambda e: e.memset(s_kvbd[:], 0.0), inc=True)
    p.wait("tensor", zz)

    war = None
    kv_started = False
    kv_consumed = None
    store_q = None
    for c in range(NCH):
        ztok = ln_z16(c, war)
        p.wait("tensor", ztok)
        ev = tr_tm2fm(p, s_z16, s_zF, p_tr, s_id16, NJ, ptr_g)
        p.waits("tensor", ev)
        p.wait("tensor", store_q)
        qtok = proj(s_wq, s_b[:, 0:NJ], s_q16, elu=True)
        p.wait("gpsimd", qtok)
        if dbg is not None and c == dbg.get("_kv_tap_chunk", 0):
            p.dma("gpsimd", dbg["q"], s_q16[:], stream="dbg")
            p.dma("gpsimd", dbg["zF"], s_zF[:], stream="dbg")
        store_q = p.dma("gpsimd", qbf[:, c * TC:(c + 1) * TC].rearrange(
            "(kc p) t -> p kc t", p=128), s_q16[:], stream="qst")
        ktok = proj(s_wk, s_b[:, NJ:2 * NJ], s_k16, elu=True)
        vtok = proj(s_wv, s_b[:, 2 * NJ:3 * NJ], s_v16, elu=False)
        p.wait("tensor", ktok)
        p.wait("tensor", vtok)
        tr_fm2tm(p, s_k16, s_kT, p_tr, s_id16, NJ, ptr_g)
        evv = tr_fm2tm(p, s_v16, s_vT, p_tr, s_id16, NJ, ptr_g)
        p.waits("tensor", evv)
        p.wait("tensor", kv_consumed)
        p.op("tensor", lambda e: e.matmul(p_kv[:], s_id16[:], s_zero[:],
                                          start=True, stop=False,
                                          skip_group_check=True))
        for i in range(NT):
            for hh in range(NH):
                po = (hh % 2) * 64
                fo = (hh // 2) * HD
                p.op("tensor",
                     lambda e, i=i, hh=hh, po=po, fo=fo: e.matmul(
                         p_kv[po:po + 64, fo:fo + HD],
                         s_kT[:, i, hh * HD:(hh + 1) * HD],
                         s_vT[:, i, hh * HD:(hh + 1) * HD],
                         start=False, stop=False, skip_group_check=True))
        kvm = p.op("tensor", lambda e: e.matmul(p_kv[:], s_id16[:], s_zero[:],
                                                start=False, stop=True,
                                                skip_group_check=True), inc=True)
        p.wait("vector", kvm)
        if not kv_started:
            kv_consumed = p.op("vector", lambda e: e.tensor_copy(s_kv32[:], p_kv[:]),
                               inc=True)
        else:
            kv_consumed = p.op("vector", lambda e: e.tensor_tensor(
                s_kv32[:], s_kv32[:], p_kv[:], ALU.add), inc=True)
        kv_started = True
        war = p.mark("tensor")
    p.wait("vector", kv_consumed)
    kv16 = p.op("vector", lambda e: e.tensor_copy(s_kv[:], s_kv32[:]), inc=True)
    p.wait("vector", zb)
    kvbd_src_e = s_kv[0:64, :].rearrange("p (j w) -> p j w", j=NJ)
    kvbd_dst_e = s_kvbd[0:64, :].rearrange("p (j w) -> p j w", w=128)[:, :, 0:64]
    kvbd_src_o = s_kv[64:128, :].rearrange("p (j w) -> p j w", j=NJ)
    kvbd_dst_o = s_kvbd[64:128, :].rearrange("p (j w) -> p j w", w=128)[:, :, 64:128]
    p.op("vector", lambda e: e.tensor_copy(kvbd_dst_e, kvbd_src_e))
    kv16 = p.op("vector", lambda e: e.tensor_copy(kvbd_dst_o, kvbd_src_o), inc=True)
    if dbg is not None:
        p.wait("gpsimd", kv16)
        p.dma("gpsimd", dbg["kv"], s_kv[:], stream="dbg")

    war = [store_q]
    out_tok = None
    for c in range(NCH):
        p.waits("sync", war)
        qld = p.dma("sync", s_q16[:], qbf[:, c * TC:(c + 1) * TC].rearrange(
            "(kc p) t -> p kc t", p=128), stream="qld")
        lv = p.dma("sync", s_x[:],
                   h["xin"][c * TC:(c + 1) * TC, :].rearrange("(i p) d -> p i d", p=128),
                   stream="xld")
        p.wait("tensor", qld)
        if c == 0:
            p.wait("tensor", kv16)
        o_evac = pmm_g
        for j in range(NJ):
            alt = j % 2
            p.wait("tensor", o_evac[alt])
            mm = p.op("tensor",
                      lambda e, j=j, alt=alt: e.matmul(
                          p_mm[alt][:], s_kvbd[:, j * 128:(j + 1) * 128],
                          s_q16[:, j, :], start=True, stop=True),
                      inc=True)
            p.wait("vector", mm)
            o_evac[alt] = p.op("vector",
                               lambda e, j=j, alt=alt: e.tensor_copy(
                                   s_o[:, j, :], p_mm[alt][:]), inc=True)
        o_done = p.mark("vector")
        if dbg is not None and c == 0:
            p.wait("gpsimd", o_done)
            p.dma("gpsimd", dbg["o"], s_o[:], stream="dbg")
        p.wait("tensor", o_done)
        mv = pmm_g
        pv = ptr_g
        trv = [None, None]
        for no in range(NJ):
            alt = no % 2
            p.wait("tensor", mv[alt])
            mm = None
            for kc in range(NJ):
                mm = p.op("tensor",
                          lambda e, no=no, kc=kc, alt=alt: e.matmul(
                              p_mm[alt][:], s_wp[:, kc, no * 128:(no + 1) * 128],
                              s_o[:, kc, :], start=(kc == 0), stop=(kc == NJ - 1)),
                          inc=(kc == NJ - 1))
            p.wait("vector", mm)
            p.wait("vector", trv[alt])
            mv[alt] = p.op("vector",
                           lambda e, no=no, alt=alt: e.tensor_scalar_add(
                               s_oT[:, alt, :], p_mm[alt][:],
                               s_b[:, 3 * NJ + no:3 * NJ + no + 1]), inc=True)
            p.wait("tensor", mv[alt])
            p.wait("tensor", pv[alt])
            tr = None
            for i in range(NT):
                tr = p.op("tensor",
                          lambda e, i=i, alt=alt: e.transpose(
                              p_tr[alt][:, i * 128:(i + 1) * 128],
                              s_oT[:, alt, i * 128:(i + 1) * 128], s_id32[:]),
                          inc=(i == NT - 1))
            trv[alt] = tr
            p.wait("vector", tr)
            pv[alt] = p.op("vector",
                           lambda e, no=no, alt=alt: e.tensor_copy(
                               s_t[:, :, no * 128:(no + 1) * 128],
                               p_tr[alt][:].rearrange("p (i w) -> p i w", i=NT)),
                           inc=True)
        pvm = p.mark("vector")
        rstd_tok, mu_tok = emit_ln_stats(p, s_t, NT, D, s_mu, s_rstd, s_sq, 1e-5,
                                         wait_vec=[], wait_sc=[pvm])
        emit_ln_apply(p, s_t, s_t, NT, s_mu, s_rstd, rstd_tok, mu_tok)
        ov = p.op("vector", lambda e: e.tensor_tensor(s_t[:], s_t[:], s_x[:], ALU.add),
                  inc=True)
        p.wait("gpsimd", ov)
        out_tok = p.dma("gpsimd",
                        h["xout"][c * TC:(c + 1) * TC, :].rearrange("(i p) d -> p i d", p=128),
                        s_t[:], stream="xst")
        war = [p.mark("tensor"), ov]
    p.wait("gpsimd", out_tok)
    return out_tok


def build_latt_nc(debug=False):
    nc = bass.Bass("TRN2", target_bir_lowering=False, debug=False)
    h = {}
    dbg = None
    h["xin"] = nc.dram_tensor("xin", [T, D], F32, kind="ExternalInput").ap()
    h["xout"] = nc.dram_tensor("xout", [T, D], F32, kind="ExternalOutput").ap()
    for nm in ("Wq", "Wk", "Wv", "Wp"):
        h[nm] = nc.dram_tensor(nm, [D, D], F32, kind="ExternalInput").ap()
    for nm in ("bq", "bk", "bv", "bp"):
        h[nm] = nc.dram_tensor(nm, [128, D // 128], F32, kind="ExternalInput").ap()
    h["ident16"] = nc.dram_tensor("ident16", [128, 128], BF16, kind="ExternalInput").ap()
    h["ident32"] = nc.dram_tensor("ident32", [128, 128], F32, kind="ExternalInput").ap()
    if debug:
        dbg = dict(
            q=nc.dram_tensor("dbg_q", [128, 8, TC], F32, kind="ExternalOutput").ap(),
            zF=nc.dram_tensor("dbg_zF", [128, 8, TC], F32, kind="ExternalOutput").ap(),
            kT=nc.dram_tensor("dbg_kT", [128, 4, D], F32, kind="ExternalOutput").ap(),
            vT=nc.dram_tensor("dbg_vT", [128, 4, D], F32, kind="ExternalOutput").ap(),
            kv=nc.dram_tensor("dbg_kv", [128, 8 * 64], F32, kind="ExternalOutput").ap(),
            o=nc.dram_tensor("dbg_o", [128, 8, TC], F32, kind="ExternalOutput").ap(),
            k16=nc.dram_tensor("dbg_k16", [128, 8, TC], F32, kind="ExternalOutput").ap(),
            v16=nc.dram_tensor("dbg_v16", [128, 8, TC], F32, kind="ExternalOutput").ap(),
        )
    with contextlib.ExitStack() as ctx:
        p = Prog(nc, ctx)
        def sb(name, shape, dt):
            return ctx.enter_context(nc.sbuf_tensor(name, shape, dt))
        def ps(name, shape, dt):
            return ctx.enter_context(nc.psum_tensor(name, shape, dt))
        build_latt_block(p, sb, ps, h, dbg)
        p.emit()
    return nc


def latt_in_map(x, Wq, bq, Wk, bk, Wv, bv, Wp, bp):
    id16, id32 = _ident_np()
    def bcol(b):
        return np.ascontiguousarray(b.reshape(8, 128).T, np.float32)
    return dict(
        xin=np.ascontiguousarray(x, np.float32),
        Wq=np.ascontiguousarray(Wq, np.float32), bq=bcol(bq),
        Wk=np.ascontiguousarray(Wk, np.float32), bk=bcol(bk),
        Wv=np.ascontiguousarray(Wv, np.float32), bv=bcol(bv),
        Wp=np.ascontiguousarray(Wp, np.float32), bp=bcol(bp),
        ident16=id16, ident32=id32,
    )



LW = 512
LH, LHD = 8, 64
INNER = 512
HID = 2730
W1S = [128] * 42 + [42, 42]
W1OFF = np.cumsum([0] + W1S).tolist()
W2S = [128] * 21 + [42]
W2OFF = np.cumsum([0] + W2S).tolist()


def build_lffn_block(p, sb, ps, h, dbg=None, max_pass=99):
    nc = p.nc
    NT = TC // 128
    NJ = D // 128
    xi = nc.dram_tensor("lf_xi", [D, T], F32).ap()

    s_id16 = sb("l_id16", [128, 128], BF16)
    s_id32 = sb("l_id32", [128, 128], F32)
    s_rot = sb("l_rot", [128, 128], BF16)
    s_qk = sb("l_qk", [128, 4], F32)
    s_gb = sb("l_gb", [128, 4 * NJ], F32)
    s_b2 = sb("l_b2", [128, 2 * NJ], F32)
    s_b1 = sb("l_b1", [128, 2 * 44], F32)
    p.dma("sync", s_id16[:], h["ident16"], stream="cst")
    p.dma("sync", s_id32[:], h["ident32"], stream="cst")
    p.dma("sync", s_rot[:], h["rot128"], stream="cst")
    p.dma("sync", s_qk[:], h["qkscale"], stream="cst")
    p.dma("sync", s_gb[:], h["lngb"], stream="cst")
    p.dma("sync", s_b2[:], h["lb2"], stream="cst")
    cst = p.dma("sync", s_b1[:], h["lb1"], stream="cst")
    for e in ("tensor", "scalar", "vector"):
        p.wait(e, cst)

    s_sq = sb("l_sq", [128, D], F32)
    s_mu = sb("l_mu", [128, 4 * NT], F32)
    s_rstd = sb("l_rstd", [128, NT], F32)
    s_selh = sb("l_selh", [128, 4 * 128], BF16)
    s_b8 = sb("l_b8", [128, 4 * 128], BF16)
    s_ones = sb("l_ones", [128, 128], F32)
    s_ones16 = sb("l_ones16", [128, 128], BF16)
    p_a = [ps(f"l_pa{i}", [128, TC], F32) for i in range(2)]
    p_b = [ps(f"l_pb{i}", [128, TC], F32) for i in range(2)]
    p_c = ps("l_pc", [128, TC], F32)
    p_d = ps("l_pd", [128, TC], F32)
    p_e = ps("l_pe", [128, TC], F32)
    p_f = ps("l_pf", [128, TC], F32)
    p.dma("sync", s_selh[:].rearrange("p (a n) -> p a n", a=4),
          h["selh"].rearrange("(a p) n -> p a n", p=128), stream="cst")
    p.dma("sync", s_b8[0:8, :], h["b8"], stream="cst")
    p.dma("sync", s_ones[:], h["onesf"], stream="cst")
    cst2 = p.dma("sync", s_ones16[:], h["ones16"], stream="cst")
    for e in ("tensor", "scalar", "vector"):
        p.wait(e, cst2)
    h_selh = [s_selh[:, jj * 128:(jj + 1) * 128] for jj in range(4)]
    h_b8 = [s_b8[0:8, jj * 128:(jj + 1) * 128] for jj in range(4)]

    pa_g = [None, None]
    pb_g = [None, None]
    pc_g = [None]
    pd_g = [None]
    pe_g = [None]
    pf_g = [None]

    def ln_stats_apply_bf16(lv, dst16):
        rstd_tok, mu_tok = emit_ln_stats(p, s_x, NT, D, s_mu, s_rstd, s_sq, 1e-5,
                                         wait_vec=[lv], wait_sc=[lv])
        return emit_ln_apply(p, s_x, dst16, NT, s_mu, s_rstd, rstd_tok, mu_tok)

    def _pass0(p0ctx):
      s_x = p0ctx.enter_context(nc.sbuf_tensor("p0_x", [128, NT, D], F32))
      s_xiF = p0ctx.enter_context(nc.sbuf_tensor("p0_xiF", [128, NJ, TC + 1], F32))
      war = []
      if True:
        for c in range(NCH):
            p.waits("sync", war)
            lv = p.dma("sync", s_x[:],
                       h["xin"][c * TC:(c + 1) * TC, :].rearrange("(i p) d -> p i d", p=128),
                       stream="xld")
            rstd_tok, mu_tok = emit_ln_stats(p, s_x, NT, D, s_mu, s_rstd, s_sq, 1e-5,
                                             wait_vec=[lv], wait_sc=[lv])
            ztok = emit_ln_apply(p, s_x, s_x, NT, s_mu, s_rstd, rstd_tok, mu_tok)
            p.wait("tensor", ztok)
            for j in range(NJ):
                bt = j % 2
                p.wait("tensor", pa_g[bt])
                tr = None
                for i in range(NT):
                    tr = p.op("tensor",
                              lambda e, i=i, j=j, bt=bt: e.transpose(
                                  p_a[bt][:, i * 128:(i + 1) * 128],
                                  s_x[:, i, j * 128:(j + 1) * 128], s_id32[:]),
                              inc=(i == NT - 1))
                p.wait("vector", tr)
                pa_g[bt] = p.op("vector",
                                lambda e, j=j, bt=bt: e.tensor_copy(s_xiF[:, j, 1:TC + 1],
                                                                    p_a[bt][:]),
                                inc=True)
            ev = p.mark("vector")
            p.wait("gpsimd", ev)
            st = p.dma("gpsimd", xi[:, c * TC:(c + 1) * TC].rearrange(
                "(j p) t -> p j t", p=128), s_xiF[:, :, 1:TC + 1], stream="xist")
            if dbg is not None and "xi0" in dbg:
                st = p.dma("gpsimd", dbg["xi0"][:, c * TC:(c + 1) * TC].rearrange(
                    "(j p) t -> p j t", p=128), s_xiF[:, :, 1:TC + 1], stream="xist")
            war = [st, p.mark("tensor")]
    with contextlib.ExitStack() as _p0ctx:
        _pass0(_p0ctx)
    p.barrier()

    if max_pass < 1:
        return None

    def _passA(l):
        with contextlib.ExitStack() as pc:
            def qsb(name, shape, dt):
                return pc.enter_context(nc.sbuf_tensor(name + str(l), shape, dt))
            s_q2 = qsb("l_q2", [128, 4, TC], BF16)
            s_xiF = qsb("l_xiF", [128, NJ, TC + 1], F32)
            s_xi16 = qsb("l_xi16", [128, NJ, TC + 1], BF16)
            s_q = qsb("l_q", [128, 4, TC], BF16)
            s_k = qsb("l_k", [128, 4, TC], BF16)
            s_v = qsb("l_v", [128, 4, TC], BF16)
            s_kp = qsb("l_kp", [128, 4, TC], BF16)
            s_vp = qsb("l_vp", [128, 4, TC], BF16)
            s_vT = qsb("l_vT", [128, NT, INNER], BF16)
            s_vTp = qsb("l_vTp", [128, NT, INNER], BF16)
            s_qf = qsb("l_qf", [128, 4, TC], F32)
            s_n2 = qsb("l_n2", [128, 2 * TC], F32)
            s_n16 = qsb("l_n16", [128, 2 * TC], BF16)
            s_tab = qsb("l_tab", [128, 4, TC], F32)
            s_sim = qsb("l_sim", [128, 2 * LW], F32)
            s_prob = qsb("l_prob", [128, 2 * LW], BF16)
            s_pT = qsb("l_pT", [128, 2 * LW], BF16)
            s_row = qsb("l_row", [128, 8], F32)
            s_oT = qsb("l_oT", [128, NT, INNER], BF16)
            s_oF = qsb("l_oF", [128, 4, TC], BF16)
            s_w3 = qsb("l_w3", [128, NJ, 1536], BF16)
            s_wo = qsb("l_wo", [128, 4, D], BF16)
            s_u1 = qsb("l_u1", [128, TC], F32)
            s_u2 = qsb("l_u2", [128, TC], F32)
            s_u2b = qsb("l_u2b", [128, TC], BF16)
            s_u3 = qsb("l_u3", [128, TC], F32)
            s_mask = qsb("l_mask", [128, 8, LW], F32)
            p.dma("sync", s_mask[:], h["lmask"].rearrange("(a p) w -> p a w", p=128),
                  stream="cst")
            wld = p.dma("gpsimd", s_w3[:],
                        h["wqkv"][l].rearrange("(kc p) n -> p kc n", p=128), stream="wld")
            wld = p.dma("gpsimd", s_wo[:],
                        h["wo"][l].rearrange("(kc p) n -> p kc n", p=128), stream="wld")
            p.wait("tensor", wld)
            qs_col = s_qk[:, 2 * l:2 * l + 1]
            ks_col = s_qk[:, 2 * l + 1:2 * l + 2]
            cur, prv = 0, 1
            kbufs = [s_k, s_kp]
            vbufs = [s_v, s_vp]
            vTbufs = [s_vT, s_vTp]
            xist = None
            for c in range(NCH):
                if c == 0:
                    ttok = p.op("vector", lambda e: e.memset(s_xiF[:, :, 0:1], 0.0),
                                inc=True)
                else:
                    ttok = p.op("vector", lambda e: e.tensor_copy(
                        s_xiF[:, :, 0:1], s_xiF[:, :, TC:TC + 1]), inc=True)
                p.wait("sync", ttok)
                if xist is not None:
                    p.wait("sync", xist)
                lv = p.dma("sync", s_xiF[:, :, 1:TC + 1],
                           xi[:, c * TC:(c + 1) * TC].rearrange("(j p) t -> p j t", p=128),
                           stream="xild")
                p.wait("vector", lv)
                cp = p.op("vector", lambda e: e.tensor_copy(s_xi16[:], s_xiF[:]), inc=True)
                ttoks = []
                for ti in range(4):
                    for hdup in range(2):
                        ttoks.append(p.dma(
                            "sync", s_tab[hdup * 64:(hdup + 1) * 64, ti, :],
                            h["tables"][ti][:, c * TC:(c + 1) * TC], stream="tab"))
                p.wait("tensor", cp)
                p.waits("vector", ttoks[-1:])
                for no in range(12):
                    alt = no % 2
                    p.wait("tensor", pa_g[alt])
                    mm = None
                    for kc in range(NJ):
                        rhs = (s_xi16[:, kc, 1:TC + 1] if kc < 4
                               else s_xi16[:, kc, 0:TC])
                        mm = p.op("tensor",
                                  lambda e, no=no, kc=kc, alt=alt, rhs=rhs: e.matmul(
                                      p_a[alt][:], s_w3[:, kc, no * 128:(no + 1) * 128],
                                      rhs, start=(kc == 0), stop=(kc == NJ - 1)),
                                  inc=(kc == NJ - 1))
                    p.wait("vector", mm)
                    if no < 8:
                        j = no % 4
                        pa_g[alt] = p.op("vector",
                                         lambda e, j=j, alt=alt: e.tensor_copy(
                                             s_qf[:, j, :], p_a[alt][:]), inc=True)
                        p.wait("scalar", pa_g[alt])
                        sq = p.op("scalar",
                                  lambda e, j=j: e.activation(
                                      s_q2[:, j, :], s_qf[:, j, :], AF.Square),
                                  inc=(j == 3))
                        if j == 3:
                            p.wait("tensor", sq)
                            p.wait("tensor", pc_g[0])
                            nm = None
                            for jj in range(4):
                                nm = p.op("tensor",
                                          lambda e, jj=jj: e.matmul(
                                              p_c[:], h_selh[jj], s_q2[:, jj, :],
                                              start=(jj == 0), stop=(jj == 3)),
                                          inc=(jj == 3))
                            qk = 0 if no < 4 else 1
                            p.wait("scalar", nm)
                            t1 = p.op("scalar", lambda e, qk=qk: e.activation(
                                s_n2[0:8, qk * TC:(qk + 1) * TC], p_c[0:8, :], AF.Ln),
                                inc=True)
                            p.wait("scalar", t1)
                            t2 = p.op("scalar", lambda e, qk=qk: e.activation(
                                s_n2[0:8, qk * TC:(qk + 1) * TC],
                                s_n2[0:8, qk * TC:(qk + 1) * TC], AF.Exp, scale=0.5),
                                inc=True)
                            p.wait("vector", t2)
                            t3 = p.op("vector", lambda e, qk=qk: e.tensor_scalar_max(
                                s_n2[0:8, qk * TC:(qk + 1) * TC],
                                s_n2[0:8, qk * TC:(qk + 1) * TC], 1e-12), inc=True)
                            p.wait("vector", t3)
                            t4 = p.op("vector", lambda e, qk=qk: e.reciprocal(
                                s_n2[0:8, qk * TC:(qk + 1) * TC],
                                s_n2[0:8, qk * TC:(qk + 1) * TC]), inc=True)
                            p.wait("vector", t4)
                            rn16 = p.op("vector", lambda e, qk=qk: e.tensor_copy(
                                s_n16[0:8, qk * TC:(qk + 1) * TC],
                                s_n2[0:8, qk * TC:(qk + 1) * TC]), inc=True)
                            pc_g[0] = rn16
                            dst16 = kbufs[cur] if qk == 1 else s_q
                            scol = ks_col if qk == 1 else qs_col
                            tci = 2 if qk == 1 else 0
                            p.wait("tensor", rn16)
                            for jj in range(4):
                                p.wait("tensor", pd_g[0])
                                bm = p.op("tensor", lambda e, jj=jj, qk=qk: e.matmul(
                                    p_d[:], h_b8[jj], s_n16[0:8, qk * TC:(qk + 1) * TC],
                                    start=True, stop=True), inc=True)
                                p.wait("vector", bm)
                                u1 = p.op("vector", lambda e, jj=jj: e.tensor_tensor(
                                    s_u1[:], s_qf[:, jj, :], p_d[:], ALU.mult), inc=True)
                                pd_g[0] = u1
                                u2 = p.op("vector", lambda e, jj=jj, scol=scol: e.tensor_scalar_mul(
                                    s_u2[:], s_u1[:], scol), inc=True)
                                u2b = p.op("vector", lambda e: e.tensor_copy(
                                    s_u2b[:], s_u2[:]), inc=True)
                                p.wait("tensor", u2b)
                                p.wait("tensor", pe_g[0])
                                rm = p.op("tensor", lambda e: e.matmul(
                                    p_e[:], s_rot[:], s_u2b[:], start=True, stop=True),
                                    inc=True)
                                p.wait("vector", rm)
                                w1_ = p.op("vector", lambda e, tci=tci: e.tensor_tensor(
                                    s_u1[:], s_u2[:], s_tab[:, tci, :], ALU.mult))
                                w2_ = p.op("vector", lambda e, tci=tci: e.tensor_tensor(
                                    s_u3[:], p_e[:], s_tab[:, tci + 1, :], ALU.mult),
                                    inc=True)
                                pe_g[0] = w2_
                                p.op("vector",
                                     lambda e, jj=jj, dst16=dst16: e.tensor_tensor(
                                         dst16[:, jj, :], s_u1[:], s_u3[:], ALU.add),
                                     inc=True)
                    else:
                        j = no - 8
                        vb = vbufs[cur]
                        pa_g[alt] = p.op("vector",
                                         lambda e, j=j, alt=alt, vb=vb: e.tensor_copy(
                                             vb[:, j, :], p_a[alt][:]), inc=True)
                vdone = p.mark("vector")
                p.wait("tensor", vdone)
                tr_fm2tm(p, vbufs[cur], vTbufs[cur], p_b, s_id16, 4, pb_g)
                first = (c == 0)
                for hh in range(LH):
                    hp = (hh % 2) * 64
                    jt = hh // 2
                    for it in range(NT):
                        halves = (("cur",),) if first else (("prv", "cur"),)
                        hv_toks = []
                        width = LW if first else 2 * LW
                        for hi, half in enumerate(halves[0]):
                            kb = kbufs[cur] if half == "cur" else kbufs[prv]
                            p.wait("tensor", pb_g[hi])
                            sm = p.op("tensor",
                                      lambda e, hh=hh, it=it, hi=hi, kb=kb, hp=hp, jt=jt: e.matmul(
                                          p_b[hi][:],
                                          s_q[hp:hp + 64, jt, it * 128:(it + 1) * 128],
                                          kb[hp:hp + 64, jt, :],
                                          start=True, stop=True),
                                      inc=True)
                            p.wait("vector", sm)
                            mi = (0 if half == "prv" else 4) + it
                            off = 0 if (half == "prv" and not first) else (0 if first else LW)
                            if first:
                                off = 0
                            pb_g[hi] = p.op(
                                "vector",
                                lambda e, hi=hi, mi=mi, off=off: e.tensor_tensor(
                                    s_sim[:, off:off + LW], p_b[hi][:],
                                    s_mask[:, mi, :], ALU.add),
                                inc=True)
                            hv_toks.append(pb_g[hi])
                        mx = p.op("vector", lambda e, width=width: e.reduce_max(
                            s_row[:, 0:1], s_sim[:, 0:width], AXL.X), inc=True)
                        p.wait("vector", mx)
                        ng = p.op("vector", lambda e: e.tensor_scalar_mul(
                            s_row[:, 1:2], s_row[:, 0:1], -1.0), inc=True)
                        p.wait("scalar", ng)
                        ex = p.op("scalar", lambda e, width=width: e.activation(
                            s_prob[:, 0:width], s_sim[:, 0:width], AF.Exp,
                            bias=s_row[:, 1:2], accum_out=s_row[:, 2:3]), inc=True)
                        p.wait("vector", ex)
                        rc = p.op("vector", lambda e: e.reciprocal(
                            s_row[:, 3:4], s_row[:, 2:3]), inc=True)
                        p.wait("tensor", ex)
                        p.wait("tensor", pe_g[0])
                        ntj = width // 128
                        trp = None
                        for jc in range(ntj):
                            trp = p.op("tensor",
                                       lambda e, jc=jc: e.transpose(
                                           p_e[:].bitcast(BF16)[:, jc * 128:(jc + 1) * 128],
                                           s_prob[:, jc * 128:(jc + 1) * 128], s_id16[:]),
                                       inc=(jc == ntj - 1))
                        p.wait("vector", trp)
                        pe_g[0] = p.op("vector", lambda e, ntj=ntj: e.tensor_copy(
                            s_pT[:, 0:ntj * 128], p_e[:].bitcast(BF16)[:, 0:ntj * 128]),
                            inc=True)
                        p.wait("tensor", pe_g[0])
                        p.wait("tensor", pd_g[0])
                        pv = None
                        for jc in range(ntj):
                            if first:
                                vT_src, vjc = vTbufs[cur], jc
                            else:
                                vT_src = vTbufs[prv] if jc < 4 else vTbufs[cur]
                                vjc = jc % 4
                            pv = p.op("tensor",
                                      lambda e, jc=jc, vT_src=vT_src, vjc=vjc, hh=hh,
                                      ntj=ntj: e.matmul(
                                          p_d[0:128, 0:64],
                                          s_pT[:, jc * 128:(jc + 1) * 128],
                                          vT_src[:, vjc, hh * 64:(hh + 1) * 64],
                                          start=(jc == 0), stop=(jc == ntj - 1)),
                                      inc=(jc == ntj - 1))
                        p.wait("vector", pv)
                        p.wait("vector", rc)
                        pd_g[0] = p.op("vector",
                                       lambda e, hh=hh, it=it: e.tensor_scalar_mul(
                                           s_oT[:, it, hh * 64:(hh + 1) * 64],
                                           p_d[0:128, 0:64], s_row[:, 3:4]),
                                       inc=True)
                odone = p.mark("vector")
                p.wait("tensor", odone)
                tr_tm2fm(p, s_oT, s_oF, p_a, s_id16, 4, pa_g)
                for no in range(NJ):
                    alt = no % 2
                    p.wait("tensor", pa_g[alt])
                    mm = None
                    for kc in range(4):
                        mm = p.op("tensor",
                                  lambda e, no=no, kc=kc, alt=alt: e.matmul(
                                      p_a[alt][:], s_wo[:, kc, no * 128:(no + 1) * 128],
                                      s_oF[:, kc, :], start=(kc == 0), stop=(kc == 3)),
                                  inc=(kc == 3))
                    p.wait("vector", mm)
                    pa_g[alt] = p.op("vector",
                                     lambda e, no=no, alt=alt: e.tensor_tensor(
                                         s_xiF[:, no, 1:TC + 1], p_a[alt][:],
                                         s_xiF[:, no, 1:TC + 1], ALU.add), inc=True)
                xadd = p.mark("vector")
                p.wait("gpsimd", xadd)
                xist = p.dma("gpsimd", xi[:, c * TC:(c + 1) * TC].rearrange(
                    "(j p) t -> p j t", p=128), s_xiF[:, :, 1:TC + 1], stream="xist")
                cur, prv = prv, cur
            p.barrier()

    def _passF(l):
        with contextlib.ExitStack() as pc:
            def fsb(name, shape, dt):
                return pc.enter_context(nc.sbuf_tensor(name + str(l), shape, dt))
            s_w1 = fsb("l_w1", [128, NJ, 5460], BF16)
            s_w2g = [fsb("l_w2g0", [128, 6, D], BF16)]
            s_lnrow = fsb("l_lnrow", [1, 4 * TC], F32)
            s_gs = fsb("l_gs2", [128, 2, TC], F32)
            s_t2m = fsb("l_t2m", [128, NT, D], F32)
            s_xiF = fsb("l_xiF2", [128, NJ, TC + 1], F32)
            s_hn = fsb("l_hn", [128, NJ, TC], BF16)
            s_sq16 = s_hn
            s_g = fsb("l_g", [128, 22, TC], BF16)
            s_u1 = fsb("l_u1f", [128, TC], F32)
            s_x = fsb("l_x2", [128, NT, D], F32)
            wld = p.dma("gpsimd", s_w1[:],
                        h["w1p"][l].rearrange("(kc p) n -> p kc n", p=128), stream="wld")
            p.wait("tensor", wld)
            W2G = [(0, 6, 768), (6, 6, 768), (12, 6, 768), (18, 4, 426)]
            w2ld = [None]
            w2use = [None]
            xist = None
            outst = None
            for c in range(NCH):
                if c == 0:
                    ttok = p.op("vector", lambda e: e.memset(s_xiF[:, :, 0:1], 0.0),
                                inc=True)
                else:
                    ttok = p.op("vector", lambda e: e.tensor_copy(
                        s_xiF[:, :, 0:1], s_xiF[:, :, TC:TC + 1]), inc=True)
                p.wait("sync", ttok)
                if xist is not None:
                    p.wait("sync", xist)
                lv = p.dma("sync", s_xiF[:, :, 1:TC + 1],
                           xi[:, c * TC:(c + 1) * TC].rearrange("(j p) t -> p j t", p=128),
                           stream="xild")
                h2f = [(s_xiF[:, kc, 1:TC + 1] if kc < 4 else s_xiF[:, kc, 0:TC])
                       for kc in range(NJ)]
                p.wait("tensor", lv)
                p.wait("tensor", pc_g[0])
                sm = None
                for kc in range(NJ):
                    sm = p.op("tensor",
                              lambda e, kc=kc: e.matmul(
                                  p_c[0:1, :], s_ones[:, 0:1], h2f[kc],
                                  start=(kc == 0), stop=(kc == NJ - 1)),
                              inc=(kc == NJ - 1))
                p.wait("scalar", lv)
                sq = None
                for kc in range(NJ):
                    p.wait("scalar", getattr(p, "_sq_tok", None))
                    p._sq_tok = sq = p.op("scalar",
                                          lambda e, kc=kc: e.activation(
                                              s_sq16[:, kc, :], h2f[kc], AF.Square),
                                          inc=True)
                p.wait("tensor", sq)
                p.wait("tensor", pd_g[0])
                sqm = None
                for kc in range(NJ):
                    sqm = p.op("tensor",
                               lambda e, kc=kc: e.matmul(
                                   p_d[0:1, :], s_ones16[:, 0:1], s_sq16[:, kc, :],
                                   start=(kc == 0), stop=(kc == NJ - 1)),
                               inc=(kc == NJ - 1))
                row_mu = s_lnrow[:, 0:TC]
                row_ms = s_lnrow[:, TC:2 * TC]
                row_r = s_lnrow[:, 2 * TC:3 * TC]
                row_mr = s_lnrow[:, 3 * TC:4 * TC]
                p.wait("vector", sm)
                p.wait("vector", sqm)
                p.op("vector", lambda e: e.tensor_scalar_mul(row_mu, p_c[0:1, :], 1.0 / D))
                p.op("vector", lambda e: e.tensor_scalar_mul(row_ms, p_d[0:1, :], 1.0 / D))
                p.op("vector", lambda e: e.tensor_tensor(row_r, row_mu, row_mu, ALU.mult))
                rv = p.op("vector", lambda e: e.tensor_scalar(
                    row_r, row_r, -1.0, 1e-5, ALU.mult, ALU.add), inc=True)
                p.wait("vector", rv)
                rv = p.op("vector", lambda e: e.tensor_tensor(row_r, row_ms, row_r, ALU.add),
                          inc=True)
                p.wait("scalar", rv)
                t1 = p.op("scalar", lambda e: e.activation(row_r, row_r, AF.Ln), inc=True)
                p.wait("scalar", t1)
                t2 = p.op("scalar", lambda e: e.activation(row_r, row_r, AF.Exp, scale=-0.5),
                          inc=True)
                p.wait("vector", t2)
                mr = p.op("vector", lambda e: e.tensor_tensor(row_mr, row_mu, row_r, ALU.mult),
                          inc=True)
                p.wait("tensor", mr)
                bc1 = p.op("tensor", lambda e: e.matmul(p_c[:], s_ones[0:1, :], row_r,
                                                        start=True, stop=True), inc=True)
                bc2 = p.op("tensor", lambda e: e.matmul(p_d[:], s_ones[0:1, :], row_mr,
                                                        start=True, stop=True), inc=True)
                p.wait("vector", bc2)
                ap_tok = None
                for kc in range(NJ):
                    p.op("vector", lambda e, kc=kc: e.tensor_tensor(
                        s_u1[:], h2f[kc], p_c[:], ALU.mult))
                    p.op("vector", lambda e: e.tensor_tensor(
                        s_u1[:], s_u1[:], p_d[:], ALU.subtract))
                    ap_tok = p.op("vector", lambda e, kc=kc: e.tensor_scalar(
                        s_hn[:, kc, :], s_u1[:],
                        s_gb[:, l * 2 * NJ + kc:l * 2 * NJ + kc + 1],
                        s_gb[:, l * 2 * NJ + NJ + kc:l * 2 * NJ + NJ + kc + 1],
                        ALU.mult, ALU.add), inc=True)
                pc_g[0] = pd_g[0] = ap_tok
                p.wait("tensor", ap_tok)
                gate_done = []
                for s in range(44):
                    ws = W1S[s]
                    alt = s % 2
                    p.wait("tensor", pa_g[alt])
                    mm = None
                    for kc in range(NJ):
                        mm = p.op("tensor",
                                  lambda e, s=s, kc=kc, alt=alt, ws=ws: e.matmul(
                                      p_a[alt][0:ws, :],
                                      s_w1[:, kc, W1OFF[s]:W1OFF[s] + ws],
                                      s_hn[:, kc, :], start=(kc == 0), stop=(kc == NJ - 1)),
                                  inc=(kc == NJ - 1))
                    bcol = s_b1[:, l * 44 + s:l * 44 + s + 1]
                    if s % 2 == 0:
                        p.wait("vector", mm)
                        pa_g[alt] = p.op("vector",
                                         lambda e, ws=ws, alt=alt, bcol=bcol: e.tensor_scalar_add(
                                             s_gs[0:ws, 0, :], p_a[alt][0:ws, :], bcol[0:ws, :]),
                                         inc=True)
                    else:
                        p.wait("scalar", mm)
                        gtok = p.op("scalar",
                                    lambda e, ws=ws, alt=alt, bcol=bcol: e.activation(
                                        s_gs[0:ws, 1, :], p_a[alt][0:ws, :], AF.Gelu,
                                        bias=bcol[0:ws, :]), inc=True)
                        pa_g[alt] = gtok
                        p.wait("vector", gtok)
                        gate_done.append(p.op("vector",
                                              lambda e, s=s, ws=ws: e.tensor_tensor(
                                                  s_g[0:ws, s // 2, :], s_gs[0:ws, 0, :],
                                                  s_gs[0:ws, 1, :], ALU.mult), inc=True))
                p.wait("tensor", gate_done[-1])
                banks = [p_a[0], p_a[1], p_b[0], p_b[1], p_c, p_d, p_e, p_f]
                bguards = [pa_g, pa_g, pb_g, pb_g, pc_g, pd_g, pe_g, pf_g]
                bgidx = [0, 1, 0, 1, 0, 0, 0, 0]
                mm_toks = [None] * NJ
                for gi, (kc0, nkc, nrows) in enumerate(W2G):
                    b = 0
                    p.wait("gpsimd", w2use[b])
                    if nrows % 128 == 0:
                        w2ld[b] = p.dma("gpsimd", s_w2g[b][:, 0:nkc, :],
                                        h["w2"][l][W2OFF[kc0]:W2OFF[kc0] + nrows, :].rearrange(
                                            "(kc p) n -> p kc n", p=128), stream="w20")
                    else:
                        p.dma("gpsimd", s_w2g[b][:, 0:3, :],
                              h["w2"][l][W2OFF[kc0]:W2OFF[kc0] + 384, :].rearrange(
                                  "(kc p) n -> p kc n", p=128), stream="w20")
                        w2ld[b] = p.dma("gpsimd", s_w2g[b][0:42, 3, :],
                                        h["w2"][l][W2OFF[kc0] + 384:W2OFF[kc0] + 426, :],
                                        stream="w20")
                    p.wait("tensor", w2ld[b])
                    mm = None
                    for no in range(NJ):
                        if gi == 0:
                            p.wait("tensor", bguards[no][bgidx[no]])
                        for kci in range(nkc):
                            ws = W2S[kc0 + kci]
                            mm = p.op("tensor",
                                      lambda e, kci=kci, no=no, ws=ws, b=b,
                                      kc0=kc0, nkc=nkc, gi=gi: e.matmul(
                                          banks[no][:],
                                          s_w2g[b][0:ws, kci, no * 128:(no + 1) * 128],
                                          s_g[0:ws, kc0 + kci, :],
                                          start=(gi == 0 and kci == 0),
                                          stop=(gi == 3 and kci == nkc - 1),
                                          skip_group_check=True),
                                      inc=(kci == nkc - 1))
                        if gi == 3:
                            mm_toks[no] = mm
                    w2use[b] = mm
                for no in range(NJ):
                    b2col = s_b2[:, l * NJ + no:l * NJ + no + 1]
                    p.wait("vector", mm_toks[no])
                    p.op("vector", lambda e, no=no, b2col=b2col: e.tensor_scalar_add(
                        s_u1[:], banks[no][:], b2col))
                    tok = p.op("vector",
                               lambda e, no=no: e.tensor_tensor(
                                   s_xiF[:, no, 1:TC + 1], s_u1[:],
                                   s_xiF[:, no, 1:TC + 1], ALU.add), inc=True)
                    bguards[no][bgidx[no]] = tok
                xdone = p.mark("vector")
                if l == 0:
                    p.wait("gpsimd", xdone)
                    xist = p.dma("gpsimd", xi[:, c * TC:(c + 1) * TC].rearrange(
                        "(j p) t -> p j t", p=128), s_xiF[:, :, 1:TC + 1], stream="xist")
                else:
                    if outst is not None:
                        p.wait("sync", outst)
                    xin_lv = p.dma("sync", s_x[:],
                                   h["xin"][c * TC:(c + 1) * TC, :].rearrange(
                                       "(i p) d -> p i d", p=128), stream="xld")
                    p.wait("tensor", xdone)
                    for j in range(NJ):
                        bt = j % 2
                        p.wait("tensor", pa_g[bt])
                        tr = None
                        for i in range(NT):
                            tr = p.op("tensor",
                                      lambda e, i=i, j=j, bt=bt: e.transpose(
                                          p_a[bt][:, i * 128:(i + 1) * 128],
                                          s_xiF[:, j, 1 + i * 128:1 + (i + 1) * 128],
                                          s_id32[:]),
                                      inc=(i == NT - 1))
                        p.wait("vector", tr)
                        pa_g[bt] = p.op("vector",
                                        lambda e, j=j, bt=bt: e.tensor_copy(
                                            s_t2m[:, :, j * 128:(j + 1) * 128],
                                            p_a[bt][:].rearrange("p (i w) -> p i w", i=NT)),
                                        inc=True)
                    pv = p.mark("vector")
                    rstd_tok, mu_tok = emit_ln_stats(p, s_t2m, NT, D, s_mu, s_rstd, s_sq,
                                                     1e-5, wait_vec=[], wait_sc=[pv])
                    emit_ln_apply(p, s_t2m, s_t2m, NT, s_mu, s_rstd, rstd_tok, mu_tok)
                    p.wait("vector", xin_lv)
                    ov = p.op("vector", lambda e: e.tensor_tensor(
                        s_t2m[:], s_t2m[:], s_x[:], ALU.add), inc=True)
                    p.wait("gpsimd", ov)
                    outst = p.dma("gpsimd",
                                  h["xout"][c * TC:(c + 1) * TC, :].rearrange(
                                      "(i p) d -> p i d", p=128), s_t2m[:], stream="xst")
                    xist = outst
            p.barrier()
            return outst

    outst = None
    for l in range(2):
        if max_pass < 2 * l + 1:
            return None
        _passA(l)
        if max_pass < 2 * l + 2:
            return None
        outst = _passF(l)
    p.wait("gpsimd", outst)
    return outst


def lffn_consts():
    T_, HD64 = T, 64
    R = np.zeros((128, 128), np.float32)
    for po in range(128):
        h0 = (po // 64) * 64
        i = po - h0
        if i < 32:
            R[h0 + i + 32, po] = -1.0
        else:
            R[h0 + i - 32, po] = 1.0
    M = np.zeros((8, 128, LW), np.float32)
    jj = np.arange(LW)
    for it in range(4):
        ii = it * 128 + np.arange(128)[:, None]
        M[it] = np.where(jj[None, :] >= ii, 0.0, -1e30)
        M[4 + it] = np.where(jj[None, :] <= ii, 0.0, -1e30)
    half = np.arange(0, HD64, 2, dtype=np.float64)
    inv_freq = 1.0 / 10000.0 ** (half / HD64)
    t = np.arange(T_, dtype=np.float64)
    fr = t[:, None] * inv_freq[None, :]
    freqs = np.concatenate([fr, fr], axis=-1)
    base = (half + 0.4 * HD64) / (1.4 * HD64)
    power = (t - T_ // 2) / float(LW // 2)
    sc = base[None, :] ** power[:, None]
    scale = np.concatenate([sc, sc], axis=-1)
    cos, sin = np.cos(freqs), np.sin(freqs)
    QKS = 8.0
    tabs = [
        (QKS * cos * scale).T.astype(np.float32),
        (QKS * sin * scale).T.astype(np.float32),
        (cos / scale).T.astype(np.float32),
        (sin / scale).T.astype(np.float32),
    ]
    selh = np.zeros((4, 128, 128), np.float32)
    b8 = np.zeros((8, 4, 128), np.float32)
    for j in range(4):
        for pp in range(128):
            selh[j, pp, 2 * j + pp // 64] = 1.0
            b8[2 * j + pp // 64, j, pp] = 1.0
    return dict(
        rot128=R.astype(BF16NP),
        lmask=np.ascontiguousarray(M.reshape(8 * 128, LW)),
        tables=[np.ascontiguousarray(x) for x in tabs],
        selh=np.ascontiguousarray(selh.reshape(4 * 128, 128).astype(BF16NP)),
        b8=np.ascontiguousarray(b8.reshape(8, 4 * 128).astype(BF16NP)),
        onesf=np.ones((128, 128), np.float32),
        ones16=np.ones((128, 128), BF16NP),
    )


def w1_perm():
    perm = []
    for s in range(21):
        perm += list(range(s * 128, (s + 1) * 128))
        perm += list(range(HID + s * 128, HID + (s + 1) * 128))
    perm += list(range(21 * 128, HID))
    perm += list(range(HID + 21 * 128, 2 * HID))
    return np.array(perm)


def build_lffn_nc(debug=False, max_pass=99):
    nc = bass.Bass("TRN2", target_bir_lowering=False, debug=False)
    dbg = None
    h = {}
    h["xin"] = nc.dram_tensor("xin", [T, D], F32, kind="ExternalInput").ap()
    h["xout"] = nc.dram_tensor("xout", [T, D], F32, kind="ExternalOutput").ap()
    h["wqkv"] = [nc.dram_tensor(f"wqkv{l}", [D, 3 * INNER], F32, kind="ExternalInput").ap()
                 for l in range(2)]
    h["wo"] = [nc.dram_tensor(f"wo{l}", [INNER, D], F32, kind="ExternalInput").ap()
               for l in range(2)]
    h["w1p"] = [nc.dram_tensor(f"w1p{l}", [D, 2 * HID], F32, kind="ExternalInput").ap()
                for l in range(2)]
    h["w2"] = [nc.dram_tensor(f"w2_{l}", [HID, D], F32, kind="ExternalInput").ap()
               for l in range(2)]
    h["qkscale"] = nc.dram_tensor("qkscale", [128, 4], F32, kind="ExternalInput").ap()
    h["lngb"] = nc.dram_tensor("lngb", [128, 32], F32, kind="ExternalInput").ap()
    h["lb1"] = nc.dram_tensor("lb1", [128, 88], F32, kind="ExternalInput").ap()
    h["lb2"] = nc.dram_tensor("lb2", [128, 16], F32, kind="ExternalInput").ap()
    h["rot128"] = nc.dram_tensor("rot128", [128, 128], BF16, kind="ExternalInput").ap()
    h["lmask"] = nc.dram_tensor("lmask", [8 * 128, LW], F32, kind="ExternalInput").ap()
    tabs = [nc.dram_tensor(f"tab{i}", [64, T], F32, kind="ExternalInput").ap()
            for i in range(4)]
    h["tables"] = tabs
    h["selh"] = nc.dram_tensor("selh", [4 * 128, 128], BF16, kind="ExternalInput").ap()
    h["b8"] = nc.dram_tensor("b8", [8, 4 * 128], BF16, kind="ExternalInput").ap()
    h["onesf"] = nc.dram_tensor("onesf", [128, 128], F32, kind="ExternalInput").ap()
    h["ones16"] = nc.dram_tensor("ones16", [128, 128], BF16, kind="ExternalInput").ap()
    h["ident16"] = nc.dram_tensor("ident16", [128, 128], BF16, kind="ExternalInput").ap()
    h["ident32"] = nc.dram_tensor("ident32", [128, 128], F32, kind="ExternalInput").ap()
    if debug:
        dbg = dict(xi0=nc.dram_tensor("dbg_xi0", [D, T], F32, kind="ExternalOutput").ap())
    with contextlib.ExitStack() as ctx:
        p = Prog(nc, ctx)
        def sb(name, shape, dt):
            return ctx.enter_context(nc.sbuf_tensor(name, shape, dt))
        def ps(name, shape, dt):
            return ctx.enter_context(nc.psum_tensor(name, shape, dt))
        build_lffn_block(p, sb, ps, h, dbg, max_pass=max_pass)
        p.emit()
    return nc


def lffn_in_map(x, wqkv, q_scale, k_scale, wo, ln_g, ln_b, w1, b1, w2, b2):
    id16, id32 = _ident_np()
    consts = lffn_consts()
    perm = w1_perm()
    im = dict(
        xin=np.ascontiguousarray(x, np.float32),
        ident16=id16, ident32=id32,
        rot128=consts["rot128"], lmask=consts["lmask"],
        selh=consts["selh"], b8=consts["b8"],
        onesf=consts["onesf"], ones16=consts["ones16"],
    )
    for i in range(4):
        im[f"tab{i}"] = consts["tables"][i]
    qk = np.zeros((128, 4), np.float32)
    lngb = np.zeros((128, 32), np.float32)
    lb1 = np.zeros((128, 88), np.float32)
    lb2 = np.zeros((128, 16), np.float32)
    for l in range(2):
        qk[:, 2 * l] = np.tile(q_scale[l], 2)
        qk[:, 2 * l + 1] = np.tile(k_scale[l], 2)
        for kc in range(8):
            lngb[:, l * 16 + kc] = ln_g[l][kc * 128:(kc + 1) * 128]
            lngb[:, l * 16 + 8 + kc] = ln_b[l][kc * 128:(kc + 1) * 128]
            lb2[:, l * 8 + kc] = b2[l][kc * 128:(kc + 1) * 128]
        b1p = b1[l][perm]
        for s in range(44):
            ws = W1S[s]
            lb1[0:ws, l * 44 + s] = b1p[W1OFF[s]:W1OFF[s] + ws]
        im[f"wqkv{l}"] = np.ascontiguousarray(wqkv[l], np.float32)
        im[f"wo{l}"] = np.ascontiguousarray(wo[l], np.float32)
        im[f"w1p{l}"] = np.ascontiguousarray(w1[l][:, perm], np.float32)
        im[f"w2_{l}"] = np.ascontiguousarray(w2[l], np.float32)
    im["qkscale"] = qk
    im["lngb"] = lngb
    im["lb1"] = lb1
    im["lb2"] = lb2
    return im



DI = 2048
NDB = DI // 128
NG = 8
NGRP = 64 // NG


def build_mamba_block(p, sb, ps, h, dbg=None):
    nc = p.nc
    NT = TC // 128
    NJ = D // 128
    xc_d = nc.dram_tensor("mb_xc", [DI, T], BF16).ap()
    g_d = nc.dram_tensor("mb_g", [DI, T], BF16).ap()
    ym_d = nc.dram_tensor("mb_ym", [DI, T], BF16).ap()
    B_d = nc.dram_tensor("mb_B", [64, T], BF16).ap()

    s_id16 = sb("m_id16", [128, 128], BF16)
    s_id32 = sb("m_id32", [128, 128], F32)
    s_sq = sb("m_sq", [128, D], F32)
    s_mu = sb("m_mu", [128, 4 * NT], F32)
    s_rstd = sb("m_rstd", [128, NT], F32)
    s_nwbc = sb("m_nwbc", [128, D], F32)
    s_cols = sb("m_cols", [128, 4 * NDB], F32)
    s_cw = sb("m_cw", [128, NDB, 4], F32)
    p.dma("sync", s_id16[:], h["ident16"], stream="cst")
    p.dma("sync", s_id32[:], h["ident32"], stream="cst")
    p.dma("sync", s_nwbc[:], h["norm_w"].partition_broadcast(128).squeeze(1), stream="cst")
    p.dma("sync", s_cols[:, 0:NDB], h["dt_b"], stream="cst")
    p.dma("sync", s_cols[:, NDB:2 * NDB], h["conv_b"], stream="cst")
    p.dma("sync", s_cols[:, 2 * NDB:3 * NDB], h["D_param"], stream="cst")
    cst = p.dma("sync", s_cw[:], h["conv_w"], stream="cst")
    for e in ("tensor", "scalar", "vector"):
        p.wait(e, cst)
    p_k = [ps(f"m_p{i}", [128, TC], F32) for i in range(8)]
    pg = [[None] for _ in range(8)]

    def ln_z(c, war, s_x, s_z):
        p.waits("sync", war)
        lv = p.dma("sync", s_x[:],
                   h["xin"][c * TC:(c + 1) * TC, :].rearrange("(i p) d -> p i d", p=128),
                   stream="xld")
        rstd_tok, mu_tok = emit_ln_stats(p, s_x, NT, D, s_mu, s_rstd, s_sq, 1e-5,
                                         wait_vec=[lv], wait_sc=[lv])
        ztok = emit_ln_apply(p, s_x, s_z, NT, s_mu, s_rstd, rstd_tok, mu_tok)
        return lv, ztok

    def _passM1(pc):
        def msb(name, shape, dt):
            return pc.enter_context(nc.sbuf_tensor(name, shape, dt))
        s_x = msb("m1_x", [128, NT, D], F32)
        s_z = msb("m1_z", [128, NT, D], F32)
        s_t = msb("m1_t", [128, NT, D], F32)
        s_xn16 = msb("m1_xn16", [128, NT, D], BF16)
        s_xnF = msb("m1_xnF", [128, NJ, TC], BF16)
        s_w = msb("m1_w", [128, NJ, DI], BF16)
        s_xi = msb("m1_xi", [128, NDB, TC + 3], BF16)
        s_g16 = msb("m1_g16", [128, NDB, TC], BF16)
        s_xc16 = msb("m1_xc16", [128, NDB, TC], BF16)
        s_cacc = msb("m1_cacc", [128, TC], F32)
        war = []
        st_xc = st_g = None
        for c in range(NCH):
            lv, ztok = ln_z(c, war, s_x, s_z)
            mcol = s_cols[:, 3 * NDB + NT:3 * NDB + 2 * NT]
            p.wait("scalar", ztok)
            sqv = None
            for i in range(NT):
                p.wait("scalar", getattr(p, "_sq_tok", None))
                p._sq_tok = sqv = p.op("scalar",
                                       lambda e, i=i, mcol=mcol: e.activation(
                                           s_sq[:], s_z[:, i, :], AF.Square,
                                           accum_out=mcol[:, i:i + 1]),
                                       inc=True)
            rcol = s_cols[:, 3 * NDB:3 * NDB + NT]
            p.wait("vector", sqv)
            t1 = p.op("vector", lambda e, rcol=rcol, mcol=mcol: e.tensor_scalar(
                rcol, mcol, 1.0 / D, 1e-6, ALU.mult, ALU.add), inc=True)
            p.wait("scalar", t1)
            t2 = p.op("scalar", lambda e, rcol=rcol: e.activation(rcol, rcol, AF.Ln),
                      inc=True)
            p.wait("scalar", t2)
            t3 = p.op("scalar", lambda e, rcol=rcol: e.activation(rcol, rcol, AF.Exp,
                                                                  scale=-0.5), inc=True)
            p.wait("vector", t3)
            xnv = None
            for i in range(NT):
                p.op("vector", lambda e, i=i, rcol=rcol: e.tensor_scalar_mul(
                    s_t[:, i, :], s_z[:, i, :], rcol[:, i:i + 1]))
                xnv = p.op("vector", lambda e, i=i: e.tensor_tensor(
                    s_xn16[:, i, :], s_t[:, i, :], s_nwbc[:], ALU.mult),
                    inc=(i == NT - 1))
            p.wait("tensor", xnv)
            tr_tm2fm(p, s_xn16, s_xnF, [p_k[6], p_k[7]], s_id16, NJ, pg, gidx=(6, 7))
            xnF_ok = p.mark("vector")
            p.wait("tensor", xnF_ok)
            if c == 0:
                ttok = p.op("vector", lambda e: e.memset(s_xi[:, :, 0:3], 0.0), inc=True)
            else:
                ttok = p.op("vector", lambda e: e.tensor_copy(
                    s_xi[:, :, 0:3], s_xi[:, :, TC:TC + 3]), inc=True)
            for half in range(2):
                p.wait("gpsimd", p.mark("tensor"))
                wld = p.dma("gpsimd", s_w[:],
                            h["wipT"][:, half * DI:(half + 1) * DI].rearrange(
                                "(kc p) n -> p kc n", p=128), stream="wld")
                p.wait("tensor", wld)
                if half == 0:
                    p.wait("tensor", ttok)
                for no in range(NDB):
                    alt = no % 2
                    p.wait("tensor", pg[alt][0])
                    mm = None
                    for kc in range(NJ):
                        mm = p.op("tensor",
                                  lambda e, no=no, kc=kc, alt=alt: e.matmul(
                                      p_k[alt][:], s_w[:, kc, no * 128:(no + 1) * 128],
                                      s_xnF[:, kc, :], start=(kc == 0),
                                      stop=(kc == NJ - 1)),
                                  inc=(kc == NJ - 1))
                    if half == 0:
                        p.wait("vector", mm)
                        pg[alt][0] = p.op("vector",
                                          lambda e, no=no, alt=alt: e.tensor_copy(
                                              s_xi[:, no, 3:TC + 3], p_k[alt][:]),
                                          inc=True)
                    else:
                        p.wait("scalar", mm)
                        pg[alt][0] = p.op("scalar",
                                          lambda e, no=no, alt=alt: e.activation(
                                              s_g16[:, no, :], p_k[alt][:], AF.Silu),
                                          inc=True)
            xiv = p.mark("vector")
            p.wait("vector", xiv)
            prev_silu = None
            for db in range(NDB):
                p.wait("vector", prev_silu)
                t0 = p.op("vector", lambda e, db=db: e.tensor_scalar_mul(
                    s_cacc[:], s_xi[:, db, 0:TC], s_cw[:, db, 0:1]), inc=True)
                for k in range(1, 4):
                    p.wait("vector", t0)
                    t0 = p.op("vector", lambda e, db=db, k=k: e.scalar_tensor_tensor(
                        s_cacc[:], s_xi[:, db, k:k + TC], s_cw[:, db, k:k + 1],
                        s_cacc[:], ALU.mult, ALU.add), inc=True)
                p.wait("scalar", t0)
                prev_silu = p.op("scalar", lambda e, db=db: e.activation(
                    s_xc16[:, db, :], s_cacc[:], AF.Silu,
                    bias=s_cols[:, NDB + db:NDB + db + 1]), inc=True)
            cdone = p.mark("scalar")
            gdone = p.mark("scalar")
            p.wait("gpsimd", cdone)
            p.wait("gpsimd", gdone)
            st_xc = p.dma("gpsimd", xc_d[:, c * TC:(c + 1) * TC].rearrange(
                "(db pp) t -> pp db t", pp=128), s_xc16[:], stream="xcst")
            st_g = p.dma("gpsimd", g_d[:, c * TC:(c + 1) * TC].rearrange(
                "(db pp) t -> pp db t", pp=128), s_g16[:], stream="gst")
            war = [st_xc, st_g, p.mark("tensor")]
        return None

    with contextlib.ExitStack() as pc:
        _passM1(pc)
    p.barrier()

    def _passM2(pc):
        def msb(name, shape, dt):
            return pc.enter_context(nc.sbuf_tensor(name, shape, dt))
        s_wdt = msb("m2_wdt", [128, NDB, DI], BF16)
        s_xpB = msb("m2_xpB", [128, NDB, 64], BF16)
        s_A = msb("m2_A", [128, NDB, 64], F32)
        s_xc16 = msb("m2_xc", [128, NDB, TC], BF16)
        s_g16 = msb("m2_g", [128, NDB, TC], BF16)
        s_dt16 = msb("m2_dt", [128, NDB, TC], BF16)
        s_B = msb("m2_B", [64, TC], BF16)
        s_Bbc = msb("m2_Bbc", [128, NG, TC], BF16)
        s_a = msb("m2_a", [128, NG, TC], F32)
        s_b = msb("m2_b", [128, NG, TC], BF16)
        s_hh = msb("m2_h", [128, NG, TC], BF16)
        s_fix = msb("m2_fix", [128, NG], F32)
        s_state = msb("m2_state", [128, NDB * 64], F32)
        s_ym = msb("m2_ym", [128, NDB, TC], BF16)
        s_t1 = msb("m2_t1", [128, TC], F32)
        s_t2 = msb("m2_t2", [128, TC], F32)
        wld = p.dma("gpsimd", s_wdt[:], h["wdtT"].rearrange("(kc p) n -> p kc n", p=128),
                    stream="wld")
        wld = p.dma("gpsimd", s_xpB[:], h["xpBT"].rearrange("(kc p) n -> p kc n", p=128),
                    stream="wld")
        al = p.dma("sync", s_A[:], h["A_log"].rearrange("(db pp) n -> pp db n", pp=128),
                   stream="cst")
        p.wait("scalar", al)
        av = p.op("scalar", lambda e: e.activation(
            s_A[:].rearrange("p a b -> p (a b)"), s_A[:].rearrange("p a b -> p (a b)"),
            AF.Exp), inc=True)
        p.wait("vector", av)
        av = p.op("vector", lambda e: e.tensor_scalar_mul(
            s_A[:].rearrange("p a b -> p (a b)"),
            s_A[:].rearrange("p a b -> p (a b)"), -1.0), inc=True)
        stz = p.op("vector", lambda e: e.memset(s_state[:], 0.0), inc=True)
        war = []
        st_ym = None
        bst = None
        for c in range(NCH):
            p.waits("sync", war)
            xcl = p.dma("sync", s_xc16[:], xc_d[:, c * TC:(c + 1) * TC].rearrange(
                "(db pp) t -> pp db t", pp=128), stream="xcld")
            gl = p.dma("sync", s_g16[:], g_d[:, c * TC:(c + 1) * TC].rearrange(
                "(db pp) t -> pp db t", pp=128), stream="gld")
            p.wait("tensor", xcl)
            if c == 0:
                p.wait("tensor", wld)
            for no in range(NDB):
                alt = no % 2
                p.wait("tensor", pg[alt][0])
                mm = None
                for kc in range(NDB):
                    mm = p.op("tensor",
                              lambda e, no=no, kc=kc, alt=alt: e.matmul(
                                  p_k[alt][:], s_wdt[:, kc, no * 128:(no + 1) * 128],
                                  s_xc16[:, kc, :], start=(kc == 0),
                                  stop=(kc == NDB - 1)),
                              inc=(kc == NDB - 1))
                dbcol = s_cols[:, no:no + 1]
                p.wait("scalar", mm)
                p.wait("scalar", getattr(p, "_m2_sp_v", None))
                u1 = p.op("scalar", lambda e, alt=alt, dbcol=dbcol: e.activation(
                    s_t1[:], p_k[alt][:], AF.Abs, bias=dbcol), inc=True)
                p.wait("scalar", u1)
                u2 = p.op("scalar", lambda e: e.activation(
                    s_t1[:], s_t1[:], AF.Exp, scale=-1.0), inc=True)
                p.wait("scalar", u2)
                u3 = p.op("scalar", lambda e: e.activation(
                    s_t1[:], s_t1[:], AF.Ln, bias=1.0), inc=True)
                u4 = p.op("scalar", lambda e, alt=alt, dbcol=dbcol: e.activation(
                    s_t2[:], p_k[alt][:], AF.Relu, bias=dbcol), inc=True)
                p.wait("vector", u3)
                p.wait("vector", u4)
                pg[alt][0] = p.op("vector", lambda e, no=no: e.tensor_tensor(
                    s_dt16[:, no, :], s_t1[:], s_t2[:], ALU.add), inc=True)
                p._m2_sp_v = pg[alt][0]
            p.wait("tensor", pg[2][0])
            mm = None
            for kc in range(NDB):
                mm = p.op("tensor",
                          lambda e, kc=kc: e.matmul(
                              p_k[2][0:64, :], s_xpB[:, kc, :], s_xc16[:, kc, :],
                              start=(kc == 0), stop=(kc == NDB - 1)),
                          inc=(kc == NDB - 1))
            p.wait("vector", mm)
            if c > 0:
                p.wait("vector", bst)
            bev = p.op("vector", lambda e: e.tensor_copy(s_B[:], p_k[2][0:64, :]),
                       inc=True)
            pg[2][0] = bev
            p.wait("gpsimd", bev)
            bst = p.dma("gpsimd", B_d[:, c * TC:(c + 1) * TC], s_B[:], stream="Bst")
            dtdone = p.mark("vector")
            last_ym = None
            for halfdb in range(2):
                ymt = [None] * 8
                for ng in range(NGRP):
                    p.wait("sync", bst)
                    p.wait("sync", last_ym)
                    bbc = p.dma("sync", s_Bbc[:],
                                B_d[ng * NG:(ng + 1) * NG,
                                    c * TC:(c + 1) * TC].partition_broadcast(128),
                                stream="Bbc")
                    for dbi in range(8):
                        db = halfdb * 8 + dbi
                        at = None
                        p.wait("scalar", dtdone if ng == 0 else None)
                        p.wait("scalar", getattr(p, "_m2_scan_v", None))
                        for n in range(NG):
                            at = p.op("scalar",
                                      lambda e, db=db, ng=ng, n=n: e.activation(
                                          s_a[:, n, :], s_dt16[:, db, :], AF.Exp,
                                          scale=s_A[:, db, ng * NG + n:ng * NG + n + 1]),
                                      inc=(n == NG - 1))
                        p.wait("vector", bbc)
                        xrep = s_xc16[:, db, :].unsqueeze(1).broadcast_to((128, NG, TC))
                        bt = p.op("vector", lambda e, db=db, xrep=xrep: e.tensor_tensor(
                            s_b[:], xrep, s_Bbc[:], ALU.mult), inc=True)
                        a0 = s_a[:, :, 0:1].squeeze(2)
                        b0 = s_b[:, :, 0:1].squeeze(2)
                        scols = s_state[:, db * 64 + ng * NG: db * 64 + (ng + 1) * NG]
                        p.wait("vector", at)
                        f1 = p.op("vector", lambda e, a0=a0, scols=scols: e.tensor_tensor(
                            s_fix[:], a0, scols, ALU.mult), inc=True)
                        p.wait("vector", f1)
                        p.wait("vector", bt)
                        f2 = p.op("vector", lambda e, b0=b0: e.tensor_tensor(
                            b0, s_fix[:], b0, ALU.add), inc=True)
                        f3 = p.op("vector", lambda e, a0=a0: e.memset(a0, 0.0), inc=True)
                        p.wait("vector", f2)
                        p.wait("vector", f3)
                        p.wait("vector", last_ym)
                        if dbg is not None and dbg.get("_const_h"):
                            sc = p.op("vector", lambda e: e.memset(
                                s_hh[:], 1.0), inc=True)
                        else:
                            sc = p.op("vector", lambda e: e.tensor_tensor_scan(
                            s_hh[:].rearrange("p a b -> p (a b)"),
                            s_a[:].rearrange("p a b -> p (a b)"),
                            s_b[:].rearrange("p a b -> p (a b)"),
                                0.0, ALU.mult, ALU.add), inc=True)
                        p.wait("vector", sc)
                        sv = p.op("vector", lambda e, scols=scols: e.tensor_copy(
                            scols, s_hh[:, :, TC - 1:TC].squeeze(2)), inc=True)
                        p._m2_scan_v = sv
                        if dbg is not None and dbg.get("_h_sel") == (c, db, ng):
                            p.wait("gpsimd", sv)
                            p.dma("gpsimd", dbg["hdump"], s_hh[:], stream="dbg")
                            p.dma("gpsimd", dbg["adump"], s_a[:], stream="dbg")
                            dtk = p.dma("gpsimd", dbg["bdump"], s_b[:], stream="dbg")
                            p.wait("vector", dtk)
                            p.wait("tensor", dtk)
                            p.wait("scalar", dtk)
                        p.wait("tensor", sc)
                        if ng == 0:
                            p.wait("tensor", pg[dbi][0])
                        for n in range(NG):
                            p.op("tensor", lambda e, dbi=dbi, n=n, ng=ng: e.matmul(
                                p_k[dbi][:], s_id16[:], s_hh[:, n, :],
                                start=(ng == 0 and n == 0),
                                stop=(ng == NGRP - 1 and n == NG - 1),
                                skip_group_check=True),
                                 inc=(n == NG - 1))
                        last_ym = ymt[dbi] = p.mark("tensor")
                p.wait("vector", gl)
                for dbi in range(8):
                    db = halfdb * 8 + dbi
                    p.wait("vector", ymt[dbi])
                    dcol = s_cols[:, 2 * NDB + db:2 * NDB + db + 1]
                    p.op("vector", lambda e, db=db, dcol=dcol: e.tensor_scalar_mul(
                        s_t1[:], s_xc16[:, db, :], dcol))
                    p.op("vector", lambda e, dbi=dbi: e.tensor_tensor(
                        s_t1[:], s_t1[:], p_k[dbi][:], ALU.add))
                    pg[dbi][0] = p.op("vector",
                                      lambda e, db=db: e.tensor_tensor(
                                          s_ym[:, db, :], s_t1[:], s_g16[:, db, :],
                                          ALU.mult), inc=True)
            ymdone = p.mark("vector")
            p._m2_sp_v = ymdone
            p.wait("gpsimd", ymdone)
            st_ym = p.dma("gpsimd", ym_d[:, c * TC:(c + 1) * TC].rearrange(
                "(db pp) t -> pp db t", pp=128), s_ym[:], stream="ymst")
            war = [st_ym, p.mark("tensor"), xcl, gl]
        return None

    with contextlib.ExitStack() as pc:
        _passM2(pc)
    p.barrier()

    def _passM3(pc):
        def msb(name, shape, dt):
            return pc.enter_context(nc.sbuf_tensor(name, shape, dt))
        s_x = msb("m3_x", [128, NT, D], F32)
        s_z = msb("m3_z", [128, NT, D], F32)
        s_t = msb("m3_t", [128, NT, D], F32)
        s_ym = msb("m3_ym", [128, NDB, TC], BF16)
        s_wop = msb("m3_wop", [128, NDB, D], BF16)
        s_o2 = msb("m3_o2", [128, 2, TC], F32)
        wld = p.dma("gpsimd", s_wop[:], h["wopT"].rearrange("(kc p) n -> p kc n", p=128),
                    stream="wld")
        war = []
        out_tok = None
        for c in range(NCH):
            p.waits("sync", war)
            yl = p.dma("sync", s_ym[:], ym_d[:, c * TC:(c + 1) * TC].rearrange(
                "(db pp) t -> pp db t", pp=128), stream="ymld")
            lv, ztok = ln_z(c, [], s_x, s_z)
            p.wait("tensor", yl)
            if c == 0:
                p.wait("tensor", wld)
            trv = [None, None]
            for no in range(NJ):
                alt = no % 2
                p.wait("tensor", pg[alt][0])
                mm = None
                for kc in range(NDB):
                    mm = p.op("tensor",
                              lambda e, no=no, kc=kc, alt=alt: e.matmul(
                                  p_k[alt][:], s_wop[:, kc, no * 128:(no + 1) * 128],
                                  s_ym[:, kc, :], start=(kc == 0), stop=(kc == NDB - 1)),
                              inc=(kc == NDB - 1))
                p.wait("vector", mm)
                p.wait("vector", trv[alt])
                ev = p.op("vector", lambda e, alt=alt: e.tensor_copy(
                    s_o2[:, alt, :], p_k[alt][:]), inc=True)
                pg[alt][0] = ev
                p.wait("tensor", ev)
                if no == 0:
                    p.wait("tensor", ztok)
                p.wait("tensor", pg[4 + alt][0])
                tr = None
                for i in range(NT):
                    tr = p.op("tensor",
                              lambda e, i=i, alt=alt: e.transpose(
                                  p_k[4 + alt][:, i * 128:(i + 1) * 128],
                                  s_o2[:, alt, i * 128:(i + 1) * 128], s_id32[:]),
                              inc=(i == NT - 1))
                trv[alt] = tr
                p.wait("vector", tr)
                pg[4 + alt][0] = p.op(
                    "vector",
                    lambda e, no=no, alt=alt: e.tensor_tensor(
                        s_t[:, :, no * 128:(no + 1) * 128],
                        p_k[4 + alt][:].rearrange("p (i w) -> p i w", i=NT),
                        s_z[:, :, no * 128:(no + 1) * 128], ALU.add),
                    inc=True)
            pv = p.mark("vector")
            rstd_tok, mu_tok = emit_ln_stats(p, s_t, NT, D, s_mu, s_rstd, s_sq, 1e-5,
                                             wait_vec=[], wait_sc=[pv])
            emit_ln_apply(p, s_t, s_t, NT, s_mu, s_rstd, rstd_tok, mu_tok)
            ov = p.op("vector", lambda e: e.tensor_tensor(s_t[:], s_t[:], s_x[:],
                                                          ALU.add), inc=True)
            p.wait("gpsimd", ov)
            out_tok = p.dma("gpsimd",
                            h["xout"][c * TC:(c + 1) * TC, :].rearrange(
                                "(i p) d -> p i d", p=128), s_t[:], stream="xst")
            war = [out_tok, p.mark("tensor")]
        return out_tok

    with contextlib.ExitStack() as pc:
        out_tok = _passM3(pc)
    p.wait("gpsimd", out_tok)
    return out_tok


def AP_diag(s_diag, db):
    t = s_diag[:, db, :]
    lst = [list(x) for x in t.ap]
    lst[0][0] = lst[0][0] + 1
    lst[-1] = [1, 1]
    return bass.AP(t.tensor, t.offset, lst)


def build_mamba_nc(debug=False, h_sel=None):
    nc = bass.Bass("TRN2", target_bir_lowering=False, debug=False)
    dbg = None
    h = {}
    h["xin"] = nc.dram_tensor("xin", [T, D], F32, kind="ExternalInput").ap()
    h["xout"] = nc.dram_tensor("xout", [T, D], F32, kind="ExternalOutput").ap()
    h["norm_w"] = nc.dram_tensor("norm_w", [1, D], F32, kind="ExternalInput").ap()
    h["wipT"] = nc.dram_tensor("wipT", [D, 2 * DI], F32, kind="ExternalInput").ap()
    h["wdtT"] = nc.dram_tensor("wdtT", [DI, DI], F32, kind="ExternalInput").ap()
    h["wopT"] = nc.dram_tensor("wopT", [DI, D], F32, kind="ExternalInput").ap()
    h["xpBT"] = nc.dram_tensor("xpBT", [DI, 64], F32, kind="ExternalInput").ap()
    h["A_log"] = nc.dram_tensor("A_log", [DI, 64], F32, kind="ExternalInput").ap()
    h["dt_b"] = nc.dram_tensor("dt_b", [128, NDB], F32, kind="ExternalInput").ap()
    h["conv_b"] = nc.dram_tensor("conv_b", [128, NDB], F32, kind="ExternalInput").ap()
    h["D_param"] = nc.dram_tensor("D_param", [128, NDB], F32, kind="ExternalInput").ap()
    h["conv_w"] = nc.dram_tensor("conv_w", [128, NDB, 4], F32, kind="ExternalInput").ap()
    h["ident16"] = nc.dram_tensor("ident16", [128, 128], BF16, kind="ExternalInput").ap()
    h["ident32"] = nc.dram_tensor("ident32", [128, 128], F32, kind="ExternalInput").ap()
    if debug or h_sel is not None:
        dbg = dict(hdump=nc.dram_tensor("dbg_h", [128, NG, TC], F32,
                                        kind="ExternalOutput").ap(),
                   adump=nc.dram_tensor("dbg_a", [128, NG, TC], F32,
                                        kind="ExternalOutput").ap(),
                   bdump=nc.dram_tensor("dbg_b", [128, NG, TC], F32,
                                        kind="ExternalOutput").ap())
        if h_sel is not None:
            dbg["_h_sel"] = h_sel
    with contextlib.ExitStack() as ctx:
        p = Prog(nc, ctx)
        def sb(name, shape, dt):
            return ctx.enter_context(nc.sbuf_tensor(name, shape, dt))
        def ps(name, shape, dt):
            return ctx.enter_context(nc.psum_tensor(name, shape, dt))
        build_mamba_block(p, sb, ps, h, dbg)
        p.emit()
    return nc


def col16(v):
    return np.ascontiguousarray(v.reshape(NDB, 128).T, np.float32)


def mamba_in_map(x, norm_w, in_proj_w, conv_w, conv_b, x_proj_w, dt_w, dt_b,
                 A_log, D_param, out_proj_w):
    id16, id32 = _ident_np()
    return dict(
        xin=np.ascontiguousarray(x, np.float32),
        norm_w=np.ascontiguousarray(norm_w.reshape(1, D), np.float32),
        wipT=np.ascontiguousarray(in_proj_w.T, np.float32),
        wdtT=np.ascontiguousarray(dt_w.T, np.float32),
        wopT=np.ascontiguousarray(out_proj_w.T, np.float32),
        xpBT=np.ascontiguousarray(x_proj_w[64:128, :].T, np.float32),
        A_log=np.ascontiguousarray(A_log, np.float32),
        dt_b=col16(dt_b), conv_b=col16(conv_b), D_param=col16(D_param),
        conv_w=np.ascontiguousarray(
            conv_w[:, 0, :].reshape(NDB, 128, 4).transpose(1, 0, 2), np.float32),
        ident16=id16, ident32=id32,
    )



def kernel_unfused(**inp):
    x = np.ascontiguousarray(np.asarray(inp["L0_attn_x"], np.float32))
    ncores = x.shape[0]
    core_ids = list(range(ncores))

    nc1 = build_mamba_nc()
    base = mamba_in_map(
        x[0], np.asarray(inp["L0_attn_norm_w"]), np.asarray(inp["L0_attn_in_proj_w"]),
        np.asarray(inp["L0_attn_conv_w"]), np.asarray(inp["L0_attn_conv_b"]),
        np.asarray(inp["L0_attn_x_proj_w"]), np.asarray(inp["L0_attn_dt_w"]),
        np.asarray(inp["L0_attn_dt_b"]), np.asarray(inp["L0_attn_A_log"]),
        np.asarray(inp["L0_attn_D_param"]), np.asarray(inp["L0_attn_out_proj_w"]))
    in_maps = [dict(base, xin=np.ascontiguousarray(x[i])) for i in range(ncores)]
    res = run_bass_kernel_spmd(nc1, in_maps, core_ids=core_ids)
    x = [np.asarray(res.results[i]["xout"]) for i in range(ncores)]

    nc2 = build_ffn_nc()
    base = ffn_in_map(
        x[0], np.asarray(inp["L0_ffn_ln_gamma"]), np.asarray(inp["L0_ffn_ln_beta"]),
        np.asarray(inp["L0_ffn_fc1_q"]), np.asarray(inp["L0_ffn_fc1_scales"]),
        np.asarray(inp["L0_ffn_fc1_bias"]), np.asarray(inp["L0_ffn_fc2_q"]),
        np.asarray(inp["L0_ffn_fc2_scales"]), np.asarray(inp["L0_ffn_fc2_bias"]))
    in_maps = [dict(base, xin=x[i]) for i in range(ncores)]
    res = run_bass_kernel_spmd(nc2, in_maps, core_ids=core_ids)
    x = [np.asarray(res.results[i]["xout"]) for i in range(ncores)]

    nc3 = build_latt_nc()
    base = latt_in_map(
        x[0], np.asarray(inp["L1_attn_Wq"]), np.asarray(inp["L1_attn_bq"]),
        np.asarray(inp["L1_attn_Wk"]), np.asarray(inp["L1_attn_bk"]),
        np.asarray(inp["L1_attn_Wv"]), np.asarray(inp["L1_attn_bv"]),
        np.asarray(inp["L1_attn_Wp"]), np.asarray(inp["L1_attn_bp"]))
    in_maps = [dict(base, xin=x[i]) for i in range(ncores)]
    res = run_bass_kernel_spmd(nc3, in_maps, core_ids=core_ids)
    x = [np.asarray(res.results[i]["xout"]) for i in range(ncores)]

    nc4 = build_lffn_nc()
    base = lffn_in_map(
        x[0], np.asarray(inp["L1_ffn_wqkv"]), np.asarray(inp["L1_ffn_q_scale"]),
        np.asarray(inp["L1_ffn_k_scale"]), np.asarray(inp["L1_ffn_wo"]),
        np.asarray(inp["L1_ffn_ln_g"]), np.asarray(inp["L1_ffn_ln_b"]),
        np.asarray(inp["L1_ffn_w1"]), np.asarray(inp["L1_ffn_b1"]),
        np.asarray(inp["L1_ffn_w2"]), np.asarray(inp["L1_ffn_b2"]))
    in_maps = [dict(base, xin=x[i]) for i in range(ncores)]
    res = run_bass_kernel_spmd(nc4, in_maps, core_ids=core_ids)
    out = np.stack([np.asarray(res.results[i]["xout"]) for i in range(ncores)], 0)
    return out.astype(np.float32)



def build_fused_nc(nblocks=4):
    nc = bass.Bass("TRN2", target_bir_lowering=False, debug=False)
    FF = 4096
    x0 = nc.dram_tensor("x0", [T, D], F32, kind="ExternalInput").ap()
    xout = nc.dram_tensor("xout", [T, D], F32, kind="ExternalOutput").ap()
    x1 = nc.dram_tensor("x1", [T, D], F32).ap()
    x2 = nc.dram_tensor("x2", [T, D], F32).ap()
    x3 = nc.dram_tensor("x3", [T, D], F32).ap()
    id16 = nc.dram_tensor("ident16", [128, 128], BF16, kind="ExternalInput").ap()
    id32 = nc.dram_tensor("ident32", [128, 128], F32, kind="ExternalInput").ap()

    outs = [x1, x2, x3, xout]
    outs[nblocks - 1] = xout
    h1 = dict(xin=x0, xout=outs[0], ident16=id16, ident32=id32)
    h1["norm_w"] = nc.dram_tensor("norm_w", [1, D], F32, kind="ExternalInput").ap()
    h1["wipT"] = nc.dram_tensor("wipT", [D, 2 * DI], F32, kind="ExternalInput").ap()
    h1["wdtT"] = nc.dram_tensor("wdtT", [DI, DI], F32, kind="ExternalInput").ap()
    h1["wopT"] = nc.dram_tensor("wopT", [DI, D], F32, kind="ExternalInput").ap()
    h1["xpBT"] = nc.dram_tensor("xpBT", [DI, 64], F32, kind="ExternalInput").ap()
    h1["A_log"] = nc.dram_tensor("A_log", [DI, 64], F32, kind="ExternalInput").ap()
    for nm in ("dt_b", "conv_b", "D_param"):
        h1[nm] = nc.dram_tensor(nm, [128, NDB], F32, kind="ExternalInput").ap()
    h1["conv_w"] = nc.dram_tensor("conv_w", [128, NDB, 4], F32, kind="ExternalInput").ap()

    h2 = dict(xin=outs[0], xout=outs[1], ident16=id16, ident32=id32)
    h2["ln_gamma"] = nc.dram_tensor("ln_gamma", [1, D], F32, kind="ExternalInput").ap()
    h2["ln_beta"] = nc.dram_tensor("ln_beta", [1, D], F32, kind="ExternalInput").ap()
    h2["fc1_q"] = nc.dram_tensor("fc1_q", [D, FF], I32, kind="ExternalInput").ap()
    h2["fc1_scales"] = nc.dram_tensor("fc1_scales", [D // 128, FF], F32, kind="ExternalInput").ap()
    h2["fc1_bias"] = nc.dram_tensor("fc1_bias", [128, FF // 128], F32, kind="ExternalInput").ap()
    h2["fc2_q"] = nc.dram_tensor("fc2_q", [FF, D], I32, kind="ExternalInput").ap()
    h2["fc2_scales"] = nc.dram_tensor("fc2_scales", [FF // 128, D], F32, kind="ExternalInput").ap()
    h2["fc2_bias"] = nc.dram_tensor("fc2_bias", [128, D // 128], F32, kind="ExternalInput").ap()

    h3 = dict(xin=outs[1], xout=outs[2], ident16=id16, ident32=id32)
    for nm in ("Wq", "Wk", "Wv", "Wp"):
        h3[nm] = nc.dram_tensor(nm, [D, D], F32, kind="ExternalInput").ap()
    for nm in ("bq", "bk", "bv", "bp"):
        h3[nm] = nc.dram_tensor(nm, [128, D // 128], F32, kind="ExternalInput").ap()

    h4 = dict(xin=outs[2], xout=outs[3], ident16=id16, ident32=id32)
    h4["wqkv"] = [nc.dram_tensor(f"wqkv{l}", [D, 3 * INNER], F32, kind="ExternalInput").ap()
                  for l in range(2)]
    h4["wo"] = [nc.dram_tensor(f"wo{l}", [INNER, D], F32, kind="ExternalInput").ap()
                for l in range(2)]
    h4["w1p"] = [nc.dram_tensor(f"w1p{l}", [D, 2 * HID], F32, kind="ExternalInput").ap()
                 for l in range(2)]
    h4["w2"] = [nc.dram_tensor(f"w2_{l}", [HID, D], F32, kind="ExternalInput").ap()
                for l in range(2)]
    for nm, shp, dt in (("qkscale", [128, 4], F32), ("lngb", [128, 32], F32),
                        ("lb1", [128, 88], F32), ("lb2", [128, 16], F32),
                        ("rot128", [128, 128], BF16), ("lmask", [8 * 128, LW], F32),
                        ("selh", [4 * 128, 128], BF16), ("b8", [8, 4 * 128], BF16),
                        ("onesf", [128, 128], F32), ("ones16", [128, 128], BF16)):
        h4[nm] = nc.dram_tensor(nm, shp, dt, kind="ExternalInput").ap()
    h4["tables"] = [nc.dram_tensor(f"tab{i}", [64, T], F32, kind="ExternalInput").ap()
                    for i in range(4)]

    with contextlib.ExitStack() as ctx:
        p = Prog(nc, ctx)
        pool = [ctx.enter_context(nc.psum_tensor(f"pool{i}", [128, 512], F32))
                for i in range(8)]

        def run_block(builder, h):
            with contextlib.ExitStack() as bctx:
                def sb(name, shape, dt):
                    return bctx.enter_context(nc.sbuf_tensor(name, shape, dt))
                nxt = {"i": 0}
                def ps(name, shape, dt):
                    t = pool[nxt["i"]]
                    nxt["i"] += 1
                    return t
                builder(p, sb, ps, h)
            p.barrier()

        blocks = [(build_mamba_block, h1), (build_ffn_block, h2),
                  (build_latt_block, h3), (build_lffn_block, h4)]
        for bi in range(nblocks):
            run_block(*blocks[bi])
        p.emit()
    return nc


def kernel(**inp):
    x = np.ascontiguousarray(np.asarray(inp["L0_attn_x"], np.float32))
    ncores = x.shape[0]
    import os as _os
    nc = build_fused_nc(int(_os.environ.get("MK_NBLOCKS", "4")))
    base = {}
    base.update(mamba_in_map(
        x[0], np.asarray(inp["L0_attn_norm_w"]), np.asarray(inp["L0_attn_in_proj_w"]),
        np.asarray(inp["L0_attn_conv_w"]), np.asarray(inp["L0_attn_conv_b"]),
        np.asarray(inp["L0_attn_x_proj_w"]), np.asarray(inp["L0_attn_dt_w"]),
        np.asarray(inp["L0_attn_dt_b"]), np.asarray(inp["L0_attn_A_log"]),
        np.asarray(inp["L0_attn_D_param"]), np.asarray(inp["L0_attn_out_proj_w"])))
    base.update(ffn_in_map(
        x[0], np.asarray(inp["L0_ffn_ln_gamma"]), np.asarray(inp["L0_ffn_ln_beta"]),
        np.asarray(inp["L0_ffn_fc1_q"]), np.asarray(inp["L0_ffn_fc1_scales"]),
        np.asarray(inp["L0_ffn_fc1_bias"]), np.asarray(inp["L0_ffn_fc2_q"]),
        np.asarray(inp["L0_ffn_fc2_scales"]), np.asarray(inp["L0_ffn_fc2_bias"])))
    base.update(latt_in_map(
        x[0], np.asarray(inp["L1_attn_Wq"]), np.asarray(inp["L1_attn_bq"]),
        np.asarray(inp["L1_attn_Wk"]), np.asarray(inp["L1_attn_bk"]),
        np.asarray(inp["L1_attn_Wv"]), np.asarray(inp["L1_attn_bv"]),
        np.asarray(inp["L1_attn_Wp"]), np.asarray(inp["L1_attn_bp"])))
    base.update(lffn_in_map(
        x[0], np.asarray(inp["L1_ffn_wqkv"]), np.asarray(inp["L1_ffn_q_scale"]),
        np.asarray(inp["L1_ffn_k_scale"]), np.asarray(inp["L1_ffn_wo"]),
        np.asarray(inp["L1_ffn_ln_g"]), np.asarray(inp["L1_ffn_ln_b"]),
        np.asarray(inp["L1_ffn_w1"]), np.asarray(inp["L1_ffn_b1"]),
        np.asarray(inp["L1_ffn_w2"]), np.asarray(inp["L1_ffn_b2"])))
    del base["xin"]
    in_maps = [dict(base, x0=np.ascontiguousarray(x[i])) for i in range(ncores)]
    res = run_bass_kernel_spmd(nc, in_maps, core_ids=list(range(ncores)))
    out = np.stack([np.asarray(res.results[i]["xout"]) for i in range(ncores)], 0)
    return out.astype(np.float32)


# version 04 (final)
# speedup vs baseline: 1.1032x; 1.1032x over previous
import contextlib
import math
import numpy as np
import ml_dtypes
import concourse.bass as bass
import concourse.mybir as mybir
from concourse.bass_utils import run_bass_kernel_spmd

F32 = mybir.dt.float32
BF16 = mybir.dt.bfloat16
I32 = mybir.dt.int32
AF = mybir.ActivationFunctionType
ALU = mybir.AluOpType
AXL = mybir.AxisListType
BF16NP = ml_dtypes.bfloat16

T = 4096
D = 1024
TC = 512
NCH = T // TC
ENGINES = ("sync", "tensor", "vector", "scalar", "gpsimd")
QSEM = {"sync": "Qld", "tensor": "Qt", "vector": "Qv", "scalar": "Qs", "gpsimd": "Qg"}


class Prog:

    def __init__(self, nc, ctx):
        self.nc = nc
        self.ctx = ctx
        self.items = {e: [] for e in ENGINES}
        self.sems = {}
        self.tot = {}
        for s in QSEM.values():
            self.sems[s] = ctx.enter_context(nc.semaphore(s))
            self.tot[s] = 0

    def wait(self, eng, tok):
        if tok is None:
            return
        name, val = tok
        if name.startswith("D_") and val != self.tot[name]:
            raise AssertionError(
                f"unsound DMA wait: stream {name} token {val} < issued {self.tot[name]}")
        if val > 0:
            self.items[eng].append(("wait", name, val))

    def waits(self, eng, toks):
        for t in toks:
            self.wait(eng, t)

    def op(self, eng, fn, inc=False):
        amt = 0
        name = QSEM[eng]
        if inc:
            amt = 16 if eng in ("sync", "gpsimd") and getattr(fn, "_is_dma", False) else 1
            self.tot[name] += amt
        self.items[eng].append(("op", fn, name if inc else None, amt))
        return (name, self.tot[name]) if inc else None

    def dma(self, eng, out, in_, stream=None):
        name = "D_" + (stream if stream is not None else eng)
        if name not in self.sems:
            self.sems[name] = self.ctx.enter_context(self.nc.semaphore(name))
            self.tot[name] = 0
        self.tot[name] += 16
        def fn(e):
            return e.dma_start(out=out, in_=in_)
        self.items[eng].append(("op", fn, name, 16))
        return (name, self.tot[name])

    def mark(self, eng):
        name = QSEM[eng]
        return (name, self.tot[name])

    def barrier(self, producer_engs=None, consumer_engs=None):
        producer_engs = producer_engs or ENGINES
        consumer_engs = consumer_engs or ENGINES
        toks = [self.mark(e) for e in producer_engs]
        toks += [(n, self.tot[n]) for n in self.sems if n.startswith("D_")]
        for ce in consumer_engs:
            for t in toks:
                self.wait(ce, t)

    def emit(self):
        block = self.ctx.enter_context(self.nc.Block())
        for eng_name in ENGINES:
            items = self.items[eng_name]
            if not items:
                continue
            def mk(items=items):
                def body(eng):
                    for it in items:
                        if it[0] == "wait":
                            eng.wait_ge(self.sems[it[1]], it[2])
                        else:
                            _, fn, sem, amt = it
                            inst = fn(eng)
                            if sem is not None:
                                assert inst is not None, "op with inc returned None"
                                inst.then_inc(self.sems[sem], amt)
                return body
            getattr(block, eng_name)(mk())


def ceil_div(a, b):
    return -(-a // b)


def _gget(guards, i):
    g = guards[i]
    return g[0] if isinstance(g, list) else g


def _gset(guards, i, tok):
    if isinstance(guards[i], list):
        guards[i][0] = tok
    else:
        guards[i] = tok



def emit_ln_stats(p, s_in, ntile, dfree, s_mu, s_rstd, s_sq, eps,
                  wait_vec=(), wait_sc=()):
    p.waits("scalar", wait_sc)
    for i in range(ntile):
        p.wait("scalar", getattr(p, "_sq_tok", None))
        p._sq_tok = p.op("scalar",
                         lambda e, i=i: e.activation(s_sq[:], s_in[:, i, :], AF.Square,
                                                     accum_out=s_mu[:, ntile + i:ntile + i + 1]),
                         inc=True)
    sq_tok = p.mark("scalar")
    p.waits("vector", wait_vec)
    red_tok = None
    for i in range(ntile):
        red_tok = p.op("vector",
                       lambda e, i=i: e.reduce_sum(s_mu[:, i:i + 1], s_in[:, i, :], AXL.X),
                       inc=(i == ntile - 1))
    p.wait("vector", red_tok)
    p.wait("vector", sq_tok)
    summ = s_mu[:, 0:ntile]
    msq = s_mu[:, ntile:2 * ntile]
    var = s_mu[:, 2 * ntile:3 * ntile]
    mu = s_mu[:, 3 * ntile:4 * ntile]
    t = p.op("vector", lambda e: e.tensor_tensor(var, summ, summ, ALU.mult), inc=True)
    p.wait("vector", t)
    t = p.op("vector", lambda e: e.scalar_tensor_tensor(
        var, msq, float(dfree), var, ALU.mult, ALU.subtract), inc=True)
    p.wait("vector", t)
    var_tok = p.op("vector", lambda e: e.tensor_scalar(
        var, var, 1.0 / (dfree * dfree), eps, ALU.mult, ALU.add), inc=True)
    mu_tok = p.op("vector", lambda e: e.tensor_scalar_mul(mu, summ, 1.0 / dfree),
                  inc=True)
    p.wait("scalar", var_tok)
    t = p.op("scalar", lambda e: e.activation(var, var, AF.Ln), inc=True)
    p.wait("scalar", t)
    rstd_tok = p.op("scalar",
                    lambda e: e.activation(s_rstd[:, 0:ntile], var, AF.Exp, scale=-0.5),
                    inc=True)
    return rstd_tok, mu_tok


def emit_ln_apply(p, s_in, s_out, ntile, s_mu, s_rstd, rstd_tok, mu_tok=None):
    p.wait("vector", rstd_tok)
    p.wait("vector", mu_tok)
    tok = None
    for i in range(ntile):
        tok = p.op("vector",
                   lambda e, i=i: e.tensor_scalar(s_out[:, i, :], s_in[:, i, :],
                                                  s_mu[:, 3 * ntile + i:3 * ntile + i + 1],
                                                  s_rstd[:, i:i + 1],
                                                  ALU.subtract, ALU.mult),
                   inc=(i == ntile - 1))
    return tok



def emit_dequant(p, s_q, s_sc, s_qf, q_dram, scales_dram, K, N, dest_sbuf=None,
                 dest_dram=None, s_w=None, wait_first=()):
    nk = K // 128
    last = None
    for kc in range(nk):
        if kc == 0:
            p.waits("sync", wait_first)
        else:
            p.wait("sync", p.mark("vector"))
        p.dma("sync", s_q[:, 0:N], q_dram[kc * 128:(kc + 1) * 128, :], stream="dq")
        lv = p.dma("sync", s_sc[:, 0:N],
                   scales_dram[kc:kc + 1, :].partition_broadcast(128).squeeze(1),
                   stream="dq")
        p.wait("vector", lv)
        p.op("vector", lambda e: e.tensor_copy(s_qf[:, 0:N], s_q[:, 0:N]))
        if dest_sbuf is not None:
            last = p.op("vector",
                        lambda e, kc=kc: e.scalar_tensor_tensor(
                            dest_sbuf[:, kc, :], s_qf[:, 0:N], -128.0, s_sc[:, 0:N],
                            ALU.add, ALU.mult), inc=True)
        else:
            if kc >= 1:
                p.wait("vector", last)
            vtok = p.op("vector",
                        lambda e: e.scalar_tensor_tensor(
                            s_w[:, 0:N], s_qf[:, 0:N], -128.0, s_sc[:, 0:N],
                            ALU.add, ALU.mult), inc=True)
            p.wait("gpsimd", vtok)
            last = p.dma("gpsimd", dest_dram[kc * 128:(kc + 1) * 128, :], s_w[:, 0:N], stream="dqst")
    return last


def build_ffn_block(p, sb, ps, h, dbg=None):
    nc = p.nc
    FF = 4096
    NT = TC // 128
    NKC1, NNO1 = D // 128, FF // 128
    NKC2, NNO2 = FF // 128, D // 128
    w2bf = nc.dram_tensor("ffn_w2bf", [FF, D], BF16).ap()
    ybf = nc.dram_tensor("ffn_ybf", [FF, T], BF16).ap()

    s_id16 = sb("f_id16", [128, 128], BF16)
    s_id32 = sb("f_id32", [128, 128], F32)
    s_gb = sb("f_gbeta", [128, 2 * D], F32)
    s_b1 = sb("f_fc1b", [128, NNO1], F32)
    s_b2 = sb("f_fc2b", [128, NNO2], F32)
    p.dma("sync", s_id16[:], h["ident16"], stream="cst")
    p.dma("sync", s_id32[:], h["ident32"], stream="cst")
    p.dma("sync", s_gb[:, 0:D], h["ln_gamma"].partition_broadcast(128).squeeze(1), stream="cst")
    p.dma("sync", s_gb[:, D:2 * D], h["ln_beta"].partition_broadcast(128).squeeze(1), stream="cst")
    p.dma("sync", s_b1[:], h["fc1_bias"], stream="cst")
    cst = p.dma("sync", s_b2[:, 0:NNO2], h["fc2_bias"], stream="cst")

    s_w = sb("f_w", [128, 32, 1024], BF16)
    w1v = s_w[:].rearrange("p a b -> p (a b)").rearrange("p (kc n) -> p kc n", kc=NKC1)
    w2v = s_w[:]

    with contextlib.ExitStack() as dqctx:
        s_q = dqctx.enter_context(nc.sbuf_tensor("dq_q", [128, FF], I32))
        s_sc = dqctx.enter_context(nc.sbuf_tensor("dq_sc", [128, FF], F32))
        s_qf = dqctx.enter_context(nc.sbuf_tensor("dq_qf", [128, FF], F32))
        s_wst = dqctx.enter_context(nc.sbuf_tensor("dq_w", [128, FF], BF16))
        dq1 = emit_dequant(p, s_q, s_sc, s_qf, h["fc1_q"], h["fc1_scales"], D, FF,
                           dest_sbuf=w1v)
        dq2 = emit_dequant(p, s_q, s_sc, s_qf, h["fc2_q"], h["fc2_scales"], FF, D,
                           dest_dram=w2bf, s_w=s_wst, wait_first=[dq1])
    p.barrier()

    s_x = sb("f_x", [128, NT, D], F32)
    s_z1 = sb("f_z1", [128, NT, D], F32)
    s_t = sb("f_t", [128, NT, D], F32)
    s_sq = sb("f_sq", [128, D], F32)
    s_mu = sb("f_mu", [128, 4 * NT], F32)
    s_rstd = sb("f_rstd", [128, NT], F32)
    s_mu2 = sb("f_mu2", [128, 4 * NT], F32)
    s_rstd2 = sb("f_rstd2", [128, NT], F32)
    s_h16 = sb("f_h16", [128, NT, D], BF16)
    s_hF = sb("f_hF", [128, D // 128, TC], BF16)
    s_y16 = sb("f_y16", [128, FF // 128, TC], BF16)
    s_t2 = sb("f_t2", [128, 2, TC], F32)
    p_tr = [ps(f"f_ptr{i}", [128, TC], F32) for i in range(2)]
    p_mm = [ps(f"f_pmm{i}", [128, TC], F32) for i in range(2)]
    pmm_g = [None, None]
    ptr_g = [None, None]

    def ln1(c, war_tok):
        p.wait("sync", war_tok)
        lv = p.dma("sync", s_x[:],
                   h["xin"][c * TC:(c + 1) * TC, :].rearrange("(i p) d -> p i d", p=128),
                   stream="xld")
        rstd_tok, mu_tok = emit_ln_stats(p, s_x, NT, D, s_mu, s_rstd, s_sq, 1e-5,
                                    wait_vec=[lv], wait_sc=[lv])
        z_tok = emit_ln_apply(p, s_x, s_z1, NT, s_mu, s_rstd, rstd_tok, mu_tok)
        return lv, z_tok

    def ln2gb(z_tok):
        rstd_tok, mu_tok = emit_ln_stats(p, s_z1, NT, D, s_mu2, s_rstd2, s_sq, 1e-5,
                                    wait_vec=[], wait_sc=[z_tok])
        emit_ln_apply(p, s_z1, s_t, NT, s_mu2, s_rstd2, rstd_tok, mu_tok)
        gb = s_gb[:, 0:D].unsqueeze(1).broadcast_to((128, NT, D))
        bb = s_gb[:, D:2 * D].unsqueeze(1).broadcast_to((128, NT, D))
        p.op("vector", lambda e: e.tensor_tensor(s_t[:], s_t[:], gb, ALU.mult))
        return p.op("vector", lambda e: e.tensor_tensor(s_h16[:], s_t[:], bb, ALU.add),
                    inc=True)

    prev_consumed = None
    prev_store = None
    for c in range(NCH):
        lv, z_tok = ln1(c, prev_store)
        if prev_consumed is not None:
            p.wait("vector", prev_consumed)
        hv = ln2gb(z_tok)
        p.wait("tensor", hv)
        ev_tok = ptr_g
        hf_toks = []
        for j in range(D // 128):
            bt = j % 2
            p.wait("tensor", ev_tok[bt])
            tr = None
            for i in range(NT):
                tr = p.op("tensor",
                          lambda e, i=i, j=j, bt=bt: e.transpose(
                              p_tr[bt][:].bitcast(BF16)[:, i * 128:(i + 1) * 128],
                              s_h16[:, i, j * 128:(j + 1) * 128], s_id16[:]),
                          inc=(i == NT - 1))
            p.wait("vector", tr)
            ev_tok[bt] = p.op("vector",
                              lambda e, j=j, bt=bt: e.tensor_copy(
                                  s_hF[:, j, :], p_tr[bt][:].bitcast(BF16)[:, 0:TC]),
                              inc=True)
            hf_toks.append(ev_tok[bt])
        if dbg is not None and c == 0:
            p.wait("gpsimd", p.mark("vector"))
            p.dma("gpsimd", dbg["z1"], s_z1[:], stream="dbg")
            p.dma("gpsimd", dbg["h16"], s_h16[:], stream="dbg")
            p.dma("gpsimd", dbg["hF"], s_hF[:], stream="dbg")
        p.waits("tensor", hf_toks[-2:])
        mv_tok = pmm_g
        for no in range(NNO1):
            alt = no % 2
            p.wait("tensor", mv_tok[alt])
            mm = None
            for kc in range(NKC1):
                mm = p.op("tensor",
                          lambda e, no=no, kc=kc, alt=alt: e.matmul(
                              p_mm[alt][:], w1v[:, kc, no * 128:(no + 1) * 128],
                              s_hF[:, kc, :], start=(kc == 0), stop=(kc == NKC1 - 1)),
                          inc=(kc == NKC1 - 1))
            p.wait("scalar", mm)
            mv_tok[alt] = p.op("scalar",
                               lambda e, no=no, alt=alt: e.activation(
                                   s_y16[:, no, :], p_mm[alt][:], AF.Silu,
                                   bias=s_b1[:, no:no + 1]), inc=True)
        mm_all = p.mark("tensor")
        p.wait("gpsimd", p.mark("scalar"))
        if dbg is not None and c == 0:
            p.dma("gpsimd", dbg["y16"], s_y16[:], stream="dbg")
        for q in range(4):
            prev_store = p.dma(
                "gpsimd",
                ybf[q * FF // 4:(q + 1) * FF // 4, c * TC:(c + 1) * TC].rearrange(
                    "(kc p) t -> p kc t", p=128),
                s_y16[:, q * 8:(q + 1) * 8, :], stream="yst")
        prev_consumed = mm_all
    passA_mm = p.mark("tensor")
    passA_st = prev_store

    p.wait("sync", passA_mm)
    wld2 = p.dma("sync", s_w[:], w2bf.rearrange("(kc p) n -> p kc n", p=128), stream="wld")
    s_yF = s_y16

    prev_done = passA_st
    out_tok = None
    for c in range(NCH):
        p.wait("sync", prev_done)
        yld = None
        for q in range(4):
            yld = p.dma("sync",
                        s_yF[:, q * 8:(q + 1) * 8, :],
                        ybf[q * FF // 4:(q + 1) * FF // 4, c * TC:(c + 1) * TC].rearrange(
                            "(kc p) t -> p kc t", p=128), stream="yld")
        lv, z_tok = ln1(c, out_tok)
        p.wait("tensor", yld)
        if c == 0:
            p.wait("tensor", wld2)
        mv_tok = pmm_g
        tr_tok = [None, None]
        pv_tok = ptr_g
        for no in range(NNO2):
            alt = no % 2
            p.wait("tensor", mv_tok[alt])
            mm = None
            for kc in range(NKC2):
                mm = p.op("tensor",
                          lambda e, no=no, kc=kc, alt=alt: e.matmul(
                              p_mm[alt][:], w2v[:, kc, no * 128:(no + 1) * 128],
                              s_yF[:, kc, :], start=(kc == 0), stop=(kc == NKC2 - 1)),
                          inc=(kc == NKC2 - 1))
            p.wait("vector", mm)
            p.wait("vector", tr_tok[alt])
            mv_tok[alt] = p.op("vector",
                               lambda e, no=no, alt=alt: e.tensor_scalar(
                                   s_t2[:, no % 2, :], p_mm[alt][:],
                                   s_b2[:, no:no + 1], 0.5, ALU.add, ALU.mult),
                               inc=True)
            p.wait("tensor", mv_tok[alt])
            if no == 0:
                p.wait("tensor", z_tok)
            p.wait("tensor", pv_tok[alt])
            ptrf = p_tr[alt][:]
            tr = None
            for i in range(NT):
                tr = p.op("tensor",
                          lambda e, i=i, alt=alt, ptrf=ptrf: e.transpose(
                              ptrf[:, i * 128:(i + 1) * 128],
                              s_t2[:, alt, i * 128:(i + 1) * 128], s_id32[:]),
                          inc=(i == NT - 1))
            tr_tok[alt] = tr
            p.wait("vector", tr)
            pv_tok[alt] = p.op(
                "vector",
                lambda e, no=no, ptrf=ptrf: e.tensor_tensor(
                    s_t[:, :, no * 128:(no + 1) * 128],
                    ptrf.rearrange("p (i w) -> p i w", i=NT),
                    s_z1[:, :, no * 128:(no + 1) * 128], ALU.add),
                inc=True)
        pv = p.mark("vector")
        if dbg is not None and c == 0:
            p.wait("gpsimd", pv)
            p.dma("gpsimd", dbg["inner"], s_t[:], stream="dbg")
        rstd_tok, mu_tok = emit_ln_stats(p, s_t, NT, D, s_mu2, s_rstd2, s_sq, 1e-5,
                                    wait_vec=[], wait_sc=[pv])
        emit_ln_apply(p, s_t, s_t, NT, s_mu2, s_rstd2, rstd_tok, mu_tok)
        ov = p.op("vector", lambda e: e.tensor_tensor(s_t[:], s_t[:], s_x[:], ALU.add),
                  inc=True)
        p.wait("gpsimd", ov)
        out_tok = p.dma("gpsimd",
                        h["xout"][c * TC:(c + 1) * TC, :].rearrange("(i p) d -> p i d", p=128),
                        s_t[:], stream="xst")
        prev_done = p.mark("tensor")
    p.wait("gpsimd", out_tok)
    return out_tok



def _ident_np():
    return np.eye(128, dtype=BF16NP), np.eye(128, dtype=np.float32)


def build_ffn_nc(debug=False):
    nc = bass.Bass("TRN2", target_bir_lowering=False, debug=False)
    FF = 4096
    h = {}
    h["xin"] = nc.dram_tensor("xin", [T, D], F32, kind="ExternalInput").ap()
    h["xout"] = nc.dram_tensor("xout", [T, D], F32, kind="ExternalOutput").ap()
    h["ln_gamma"] = nc.dram_tensor("ln_gamma", [1, D], F32, kind="ExternalInput").ap()
    h["ln_beta"] = nc.dram_tensor("ln_beta", [1, D], F32, kind="ExternalInput").ap()
    h["fc1_q"] = nc.dram_tensor("fc1_q", [D, FF], I32, kind="ExternalInput").ap()
    h["fc1_scales"] = nc.dram_tensor("fc1_scales", [D // 128, FF], F32, kind="ExternalInput").ap()
    h["fc1_bias"] = nc.dram_tensor("fc1_bias", [128, FF // 128], F32, kind="ExternalInput").ap()
    h["fc2_q"] = nc.dram_tensor("fc2_q", [FF, D], I32, kind="ExternalInput").ap()
    h["fc2_scales"] = nc.dram_tensor("fc2_scales", [FF // 128, D], F32, kind="ExternalInput").ap()
    h["fc2_bias"] = nc.dram_tensor("fc2_bias", [128, D // 128], F32, kind="ExternalInput").ap()
    h["ident16"] = nc.dram_tensor("ident16", [128, 128], BF16, kind="ExternalInput").ap()
    h["ident32"] = nc.dram_tensor("ident32", [128, 128], F32, kind="ExternalInput").ap()
    dbg = None
    if debug:
        dbg = dict(
            z1=nc.dram_tensor("dbg_z1", [128, 4, D], F32, kind="ExternalOutput").ap(),
            h16=nc.dram_tensor("dbg_h16", [128, 4, D], F32, kind="ExternalOutput").ap(),
            hF=nc.dram_tensor("dbg_hF", [128, 8, TC], F32, kind="ExternalOutput").ap(),
            y16=nc.dram_tensor("dbg_y16", [128, 32, TC], F32, kind="ExternalOutput").ap(),
            inner=nc.dram_tensor("dbg_inner", [128, 4, D], F32, kind="ExternalOutput").ap(),
        )
    with contextlib.ExitStack() as ctx:
        p = Prog(nc, ctx)
        def sb(name, shape, dt):
            return ctx.enter_context(nc.sbuf_tensor(name, shape, dt))
        def ps(name, shape, dt):
            return ctx.enter_context(nc.psum_tensor(name, shape, dt))
        build_ffn_block(p, sb, ps, h, dbg)
        p.emit()
    return nc


def ffn_in_map(x, gamma, beta, fc1_q, fc1_scales, fc1_bias, fc2_q, fc2_scales, fc2_bias):
    id16, id32 = _ident_np()
    return dict(
        xin=np.ascontiguousarray(x, np.float32),
        ln_gamma=np.ascontiguousarray(gamma.reshape(1, D), np.float32),
        ln_beta=np.ascontiguousarray(beta.reshape(1, D), np.float32),
        fc1_q=np.ascontiguousarray(fc1_q, np.int32),
        fc1_scales=np.ascontiguousarray(fc1_scales, np.float32),
        fc1_bias=np.ascontiguousarray(fc1_bias.reshape(32, 128).T, np.float32),
        fc2_q=np.ascontiguousarray(fc2_q, np.int32),
        fc2_scales=np.ascontiguousarray(fc2_scales, np.float32),
        fc2_bias=np.ascontiguousarray(fc2_bias.reshape(8, 128).T, np.float32),
        ident16=id16, ident32=id32,
    )



def tr_tm2fm(p, src, dst, p_tr, s_id16, nj, guards, gidx=(0, 1), nt=None):
    nt = nt if nt is not None else TC // 128
    for j in range(nj):
        bt = j % 2
        p.wait("tensor", _gget(guards, gidx[bt]))
        tr = None
        for i in range(nt):
            tr = p.op("tensor",
                      lambda e, i=i, j=j, bt=bt: e.transpose(
                          p_tr[bt][:].bitcast(BF16)[:, i * 128:(i + 1) * 128],
                          src[:, i, j * 128:(j + 1) * 128], s_id16[:]),
                      inc=(i == nt - 1))
        p.wait("vector", tr)
        _gset(guards, gidx[bt],
              p.op("vector",
                   lambda e, j=j, bt=bt, nt=nt: e.tensor_copy(
                       dst[:, j, :], p_tr[bt][:].bitcast(BF16)[:, 0:nt * 128]),
                   inc=True))
    return [_gget(guards, gidx[0]), _gget(guards, gidx[1])]


def tr_fm2tm(p, src, dst, p_tr, s_id16, nj, guards, gidx=(0, 1), nt=None):
    nt = nt if nt is not None else TC // 128
    for i in range(nt):
        bt = i % 2
        p.wait("tensor", _gget(guards, gidx[bt]))
        tr = None
        for j in range(nj):
            tr = p.op("tensor",
                      lambda e, i=i, j=j, bt=bt: e.transpose(
                          p_tr[bt][:].bitcast(BF16)[:, j * 128:(j + 1) * 128],
                          src[:, j, i * 128:(i + 1) * 128], s_id16[:]),
                      inc=(j == nj - 1))
        p.wait("vector", tr)
        _gset(guards, gidx[bt],
              p.op("vector",
                   lambda e, i=i, bt=bt, nj=nj: e.tensor_copy(
                       dst[:, i, :], p_tr[bt][:].bitcast(BF16)[:, 0:nj * 128]),
                   inc=True))
    return [_gget(guards, gidx[0]), _gget(guards, gidx[1])]


def build_latt_block(p, sb, ps, h, dbg=None):
    nc = p.nc
    NT = TC // 128
    NH, HD = 16, 64
    NJ = D // 128
    qbf = nc.dram_tensor("latt_qbf", [D, T], BF16).ap()

    s_wq = sb("a_wq", [128, NJ, D], BF16)
    s_wk = sb("a_wk", [128, NJ, D], BF16)
    s_wv = sb("a_wv", [128, NJ, D], BF16)
    s_wp = sb("a_wp", [128, NJ, D], BF16)
    s_b = sb("a_b", [128, 4 * NJ], F32)
    s_id16 = sb("a_id16", [128, 128], BF16)
    s_id32 = sb("a_id32", [128, 128], F32)
    wtoks = []
    for nm, s in (("Wq", s_wq), ("Wk", s_wk), ("Wv", s_wv), ("Wp", s_wp)):
        wtoks.append(p.dma("gpsimd", s[:], h[nm].rearrange("(kc p) n -> p kc n", p=128), stream="wld"))
    p.dma("sync", s_b[:, 0:NJ], h["bq"], stream="cst")
    p.dma("sync", s_b[:, NJ:2 * NJ], h["bk"], stream="cst")
    p.dma("sync", s_b[:, 2 * NJ:3 * NJ], h["bv"], stream="cst")
    p.dma("sync", s_b[:, 3 * NJ:4 * NJ], h["bp"], stream="cst")
    p.dma("sync", s_id16[:], h["ident16"], stream="cst")
    cst = p.dma("sync", s_id32[:], h["ident32"], stream="cst")
    p.wait("tensor", wtoks[-1])
    for e in ("tensor", "scalar", "vector"):
        p.wait(e, cst)

    s_x = sb("a_x", [128, NT, D], F32)
    s_z16 = sb("a_z16", [128, NT, D], BF16)
    s_zF = sb("a_zF", [128, NJ, TC], BF16)
    s_sq = sb("a_sq", [128, D], F32)
    s_mu = sb("a_mu", [128, 4 * NT], F32)
    s_rstd = sb("a_rstd", [128, NT], F32)
    s_q16 = sb("a_q16", [128, NJ, TC], BF16)
    s_k16 = sb("a_k16", [128, NJ, TC], BF16)
    s_v16 = sb("a_v16", [128, NJ, TC], BF16)
    s_kT = sb("a_kT", [128, NT, D], BF16)
    s_vT = sb("a_vT", [128, NT, D], BF16)
    s_u = sb("a_u", [128, TC], F32)
    s_tm = sb("a_tm", [128, TC], F32)
    s_kv = sb("a_kv", [128, NJ * HD], BF16)
    s_kv32 = sb("a_kv32", [128, NJ * HD], F32)
    s_kvbd = sb("a_kvbd", [128, NJ * 128], BF16)
    s_zero = sb("a_zero", [128, TC], BF16)
    s_o = sb("a_o", [128, NJ, TC], BF16)
    s_oT = sb("a_oT", [128, 2, TC], F32)
    s_t = sb("a_t", [128, NT, D], F32)
    p_mm = [ps(f"a_pmm{i}", [128, TC], F32) for i in range(2)]
    p_tr = [ps(f"a_ptr{i}", [128, TC], F32) for i in range(2)]
    p_kv = ps("a_pkv", [128, NJ * HD], F32)

    def ln_z16(c, war_tok):
        p.wait("sync", war_tok)
        lv = p.dma("sync", s_x[:],
                   h["xin"][c * TC:(c + 1) * TC, :].rearrange("(i p) d -> p i d", p=128),
                   stream="xld")
        rstd_tok, mu_tok = emit_ln_stats(p, s_x, NT, D, s_mu, s_rstd, s_sq, 1e-5,
                                         wait_vec=[lv], wait_sc=[lv])
        return emit_ln_apply(p, s_x, s_z16, NT, s_mu, s_rstd, rstd_tok, mu_tok)

    pmm_g = [None, None]
    ptr_g = [None, None]
    state = {"prev_exp": None}

    def proj(wv, bcols, dst16, elu):
        evac = pmm_g
        prev_exp = state["prev_exp"]
        tok = None
        for no in range(NJ):
            alt = no % 2
            p.wait("tensor", evac[alt])
            mm = None
            for kc in range(NJ):
                mm = p.op("tensor",
                          lambda e, no=no, kc=kc, alt=alt: e.matmul(
                              p_mm[alt][:], wv[:, kc, no * 128:(no + 1) * 128],
                              s_zF[:, kc, :], start=(kc == 0), stop=(kc == NJ - 1)),
                          inc=(kc == NJ - 1))
            p.wait("vector", mm)
            if elu:
                evac[alt] = p.op("vector",
                                 lambda e, no=no, alt=alt: e.tensor_scalar_add(
                                     s_u[:], p_mm[alt][:], bcols[:, no:no + 1]),
                                 inc=True)
                p.wait("vector", prev_exp)
                t1 = p.op("vector", lambda e: e.tensor_scalar_min(s_tm[:], s_u[:], 0.0),
                          inc=True)
                p.wait("scalar", t1)
                prev_exp = p.op("scalar",
                                lambda e: e.activation(s_tm[:], s_tm[:], AF.Exp),
                                inc=True)
                p.wait("vector", prev_exp)
                tok = p.op("vector",
                           lambda e, no=no: e.scalar_tensor_tensor(
                               dst16[:, no, :], s_u[:], 0.0, s_tm[:],
                               ALU.max, ALU.add), inc=True)
            else:
                evac[alt] = tok = p.op(
                    "vector",
                    lambda e, no=no, alt=alt: e.tensor_scalar_add(
                        dst16[:, no, :], p_mm[alt][:], bcols[:, no:no + 1]),
                    inc=True)
        state["prev_exp"] = prev_exp
        return tok

    zz = p.op("vector", lambda e: e.memset(s_zero[:], 0.0), inc=True)
    zb = p.op("vector", lambda e: e.memset(s_kvbd[:], 0.0), inc=True)
    p.wait("tensor", zz)

    war = None
    kv_started = False
    kv_consumed = None
    store_q = None
    for c in range(NCH):
        ztok = ln_z16(c, war)
        p.wait("tensor", ztok)
        ev = tr_tm2fm(p, s_z16, s_zF, p_tr, s_id16, NJ, ptr_g)
        p.waits("tensor", ev)
        p.wait("tensor", store_q)
        qtok = proj(s_wq, s_b[:, 0:NJ], s_q16, elu=True)
        p.wait("gpsimd", qtok)
        if dbg is not None and c == dbg.get("_kv_tap_chunk", 0):
            p.dma("gpsimd", dbg["q"], s_q16[:], stream="dbg")
            p.dma("gpsimd", dbg["zF"], s_zF[:], stream="dbg")
        store_q = p.dma("gpsimd", qbf[:, c * TC:(c + 1) * TC].rearrange(
            "(kc p) t -> p kc t", p=128), s_q16[:], stream="qst")
        ktok = proj(s_wk, s_b[:, NJ:2 * NJ], s_k16, elu=True)
        vtok = proj(s_wv, s_b[:, 2 * NJ:3 * NJ], s_v16, elu=False)
        p.wait("tensor", ktok)
        p.wait("tensor", vtok)
        tr_fm2tm(p, s_k16, s_kT, p_tr, s_id16, NJ, ptr_g)
        evv = tr_fm2tm(p, s_v16, s_vT, p_tr, s_id16, NJ, ptr_g)
        p.waits("tensor", evv)
        p.wait("tensor", kv_consumed)
        p.op("tensor", lambda e: e.matmul(p_kv[:], s_id16[:], s_zero[:],
                                          start=True, stop=False,
                                          skip_group_check=True))
        for i in range(NT):
            for hh in range(NH):
                po = (hh % 2) * 64
                fo = (hh // 2) * HD
                p.op("tensor",
                     lambda e, i=i, hh=hh, po=po, fo=fo: e.matmul(
                         p_kv[po:po + 64, fo:fo + HD],
                         s_kT[:, i, hh * HD:(hh + 1) * HD],
                         s_vT[:, i, hh * HD:(hh + 1) * HD],
                         start=False, stop=False, skip_group_check=True))
        kvm = p.op("tensor", lambda e: e.matmul(p_kv[:], s_id16[:], s_zero[:],
                                                start=False, stop=True,
                                                skip_group_check=True), inc=True)
        p.wait("vector", kvm)
        if not kv_started:
            kv_consumed = p.op("vector", lambda e: e.tensor_copy(s_kv32[:], p_kv[:]),
                               inc=True)
        else:
            kv_consumed = p.op("vector", lambda e: e.tensor_tensor(
                s_kv32[:], s_kv32[:], p_kv[:], ALU.add), inc=True)
        kv_started = True
        war = p.mark("tensor")
    p.wait("vector", kv_consumed)
    kv16 = p.op("vector", lambda e: e.tensor_copy(s_kv[:], s_kv32[:]), inc=True)
    p.wait("vector", zb)
    kvbd_src_e = s_kv[0:64, :].rearrange("p (j w) -> p j w", j=NJ)
    kvbd_dst_e = s_kvbd[0:64, :].rearrange("p (j w) -> p j w", w=128)[:, :, 0:64]
    kvbd_src_o = s_kv[64:128, :].rearrange("p (j w) -> p j w", j=NJ)
    kvbd_dst_o = s_kvbd[64:128, :].rearrange("p (j w) -> p j w", w=128)[:, :, 64:128]
    p.op("vector", lambda e: e.tensor_copy(kvbd_dst_e, kvbd_src_e))
    kv16 = p.op("vector", lambda e: e.tensor_copy(kvbd_dst_o, kvbd_src_o), inc=True)
    if dbg is not None:
        p.wait("gpsimd", kv16)
        p.dma("gpsimd", dbg["kv"], s_kv[:], stream="dbg")

    war = [store_q]
    out_tok = None
    for c in range(NCH):
        p.waits("sync", war)
        qld = p.dma("sync", s_q16[:], qbf[:, c * TC:(c + 1) * TC].rearrange(
            "(kc p) t -> p kc t", p=128), stream="qld")
        lv = p.dma("sync", s_x[:],
                   h["xin"][c * TC:(c + 1) * TC, :].rearrange("(i p) d -> p i d", p=128),
                   stream="xld")
        p.wait("tensor", qld)
        if c == 0:
            p.wait("tensor", kv16)
        o_evac = pmm_g
        for j in range(NJ):
            alt = j % 2
            p.wait("tensor", o_evac[alt])
            mm = p.op("tensor",
                      lambda e, j=j, alt=alt: e.matmul(
                          p_mm[alt][:], s_kvbd[:, j * 128:(j + 1) * 128],
                          s_q16[:, j, :], start=True, stop=True),
                      inc=True)
            p.wait("vector", mm)
            o_evac[alt] = p.op("vector",
                               lambda e, j=j, alt=alt: e.tensor_copy(
                                   s_o[:, j, :], p_mm[alt][:]), inc=True)
        o_done = p.mark("vector")
        if dbg is not None and c == 0:
            p.wait("gpsimd", o_done)
            p.dma("gpsimd", dbg["o"], s_o[:], stream="dbg")
        p.wait("tensor", o_done)
        mv = pmm_g
        pv = ptr_g
        trv = [None, None]
        for no in range(NJ):
            alt = no % 2
            p.wait("tensor", mv[alt])
            mm = None
            for kc in range(NJ):
                mm = p.op("tensor",
                          lambda e, no=no, kc=kc, alt=alt: e.matmul(
                              p_mm[alt][:], s_wp[:, kc, no * 128:(no + 1) * 128],
                              s_o[:, kc, :], start=(kc == 0), stop=(kc == NJ - 1)),
                          inc=(kc == NJ - 1))
            p.wait("vector", mm)
            p.wait("vector", trv[alt])
            mv[alt] = p.op("vector",
                           lambda e, no=no, alt=alt: e.tensor_scalar_add(
                               s_oT[:, alt, :], p_mm[alt][:],
                               s_b[:, 3 * NJ + no:3 * NJ + no + 1]), inc=True)
            p.wait("tensor", mv[alt])
            p.wait("tensor", pv[alt])
            tr = None
            for i in range(NT):
                tr = p.op("tensor",
                          lambda e, i=i, alt=alt: e.transpose(
                              p_tr[alt][:, i * 128:(i + 1) * 128],
                              s_oT[:, alt, i * 128:(i + 1) * 128], s_id32[:]),
                          inc=(i == NT - 1))
            trv[alt] = tr
            p.wait("vector", tr)
            pv[alt] = p.op("vector",
                           lambda e, no=no, alt=alt: e.tensor_copy(
                               s_t[:, :, no * 128:(no + 1) * 128],
                               p_tr[alt][:].rearrange("p (i w) -> p i w", i=NT)),
                           inc=True)
        pvm = p.mark("vector")
        rstd_tok, mu_tok = emit_ln_stats(p, s_t, NT, D, s_mu, s_rstd, s_sq, 1e-5,
                                         wait_vec=[], wait_sc=[pvm])
        emit_ln_apply(p, s_t, s_t, NT, s_mu, s_rstd, rstd_tok, mu_tok)
        ov = p.op("vector", lambda e: e.tensor_tensor(s_t[:], s_t[:], s_x[:], ALU.add),
                  inc=True)
        p.wait("gpsimd", ov)
        out_tok = p.dma("gpsimd",
                        h["xout"][c * TC:(c + 1) * TC, :].rearrange("(i p) d -> p i d", p=128),
                        s_t[:], stream="xst")
        war = [p.mark("tensor"), ov]
    p.wait("gpsimd", out_tok)
    return out_tok


def build_latt_nc(debug=False):
    nc = bass.Bass("TRN2", target_bir_lowering=False, debug=False)
    h = {}
    dbg = None
    h["xin"] = nc.dram_tensor("xin", [T, D], F32, kind="ExternalInput").ap()
    h["xout"] = nc.dram_tensor("xout", [T, D], F32, kind="ExternalOutput").ap()
    for nm in ("Wq", "Wk", "Wv", "Wp"):
        h[nm] = nc.dram_tensor(nm, [D, D], F32, kind="ExternalInput").ap()
    for nm in ("bq", "bk", "bv", "bp"):
        h[nm] = nc.dram_tensor(nm, [128, D // 128], F32, kind="ExternalInput").ap()
    h["ident16"] = nc.dram_tensor("ident16", [128, 128], BF16, kind="ExternalInput").ap()
    h["ident32"] = nc.dram_tensor("ident32", [128, 128], F32, kind="ExternalInput").ap()
    if debug:
        dbg = dict(
            q=nc.dram_tensor("dbg_q", [128, 8, TC], F32, kind="ExternalOutput").ap(),
            zF=nc.dram_tensor("dbg_zF", [128, 8, TC], F32, kind="ExternalOutput").ap(),
            kT=nc.dram_tensor("dbg_kT", [128, 4, D], F32, kind="ExternalOutput").ap(),
            vT=nc.dram_tensor("dbg_vT", [128, 4, D], F32, kind="ExternalOutput").ap(),
            kv=nc.dram_tensor("dbg_kv", [128, 8 * 64], F32, kind="ExternalOutput").ap(),
            o=nc.dram_tensor("dbg_o", [128, 8, TC], F32, kind="ExternalOutput").ap(),
            k16=nc.dram_tensor("dbg_k16", [128, 8, TC], F32, kind="ExternalOutput").ap(),
            v16=nc.dram_tensor("dbg_v16", [128, 8, TC], F32, kind="ExternalOutput").ap(),
        )
    with contextlib.ExitStack() as ctx:
        p = Prog(nc, ctx)
        def sb(name, shape, dt):
            return ctx.enter_context(nc.sbuf_tensor(name, shape, dt))
        def ps(name, shape, dt):
            return ctx.enter_context(nc.psum_tensor(name, shape, dt))
        build_latt_block(p, sb, ps, h, dbg)
        p.emit()
    return nc


def latt_in_map(x, Wq, bq, Wk, bk, Wv, bv, Wp, bp):
    id16, id32 = _ident_np()
    def bcol(b):
        return np.ascontiguousarray(b.reshape(8, 128).T, np.float32)
    return dict(
        xin=np.ascontiguousarray(x, np.float32),
        Wq=np.ascontiguousarray(Wq, np.float32), bq=bcol(bq),
        Wk=np.ascontiguousarray(Wk, np.float32), bk=bcol(bk),
        Wv=np.ascontiguousarray(Wv, np.float32), bv=bcol(bv),
        Wp=np.ascontiguousarray(Wp, np.float32), bp=bcol(bp),
        ident16=id16, ident32=id32,
    )



LW = 512
LH, LHD = 8, 64
INNER = 512
HID = 2730
W1S = [128] * 42 + [42, 42]
W1OFF = np.cumsum([0] + W1S).tolist()
W2S = [128] * 21 + [42]
W2OFF = np.cumsum([0] + W2S).tolist()


def build_lffn_block(p, sb, ps, h, dbg=None, max_pass=99):
    nc = p.nc
    NT = TC // 128
    NJ = D // 128
    xi = nc.dram_tensor("lf_xi", [D, T], F32).ap()

    s_id16 = sb("l_id16", [128, 128], BF16)
    s_id32 = sb("l_id32", [128, 128], F32)
    s_rot = sb("l_rot", [128, 128], BF16)
    s_qk = sb("l_qk", [128, 4], F32)
    s_gb = sb("l_gb", [128, 4 * NJ], F32)
    s_b2 = sb("l_b2", [128, 2 * NJ], F32)
    s_b1 = sb("l_b1", [128, 2 * 44], F32)
    p.dma("sync", s_id16[:], h["ident16"], stream="cst")
    p.dma("sync", s_id32[:], h["ident32"], stream="cst")
    p.dma("sync", s_rot[:], h["rot128"], stream="cst")
    p.dma("sync", s_qk[:], h["qkscale"], stream="cst")
    p.dma("sync", s_gb[:], h["lngb"], stream="cst")
    p.dma("sync", s_b2[:], h["lb2"], stream="cst")
    cst = p.dma("sync", s_b1[:], h["lb1"], stream="cst")
    for e in ("tensor", "scalar", "vector"):
        p.wait(e, cst)

    s_sq = sb("l_sq", [128, D], F32)
    s_mu = sb("l_mu", [128, 4 * NT], F32)
    s_rstd = sb("l_rstd", [128, NT], F32)
    s_selh = sb("l_selh", [128, 4 * 128], BF16)
    s_b8 = sb("l_b8", [128, 4 * 128], BF16)
    s_ones = sb("l_ones", [128, 128], F32)
    s_ones16 = sb("l_ones16", [128, 128], BF16)
    p_a = [ps(f"l_pa{i}", [128, TC], F32) for i in range(2)]
    p_b = [ps(f"l_pb{i}", [128, TC], F32) for i in range(2)]
    p_c = ps("l_pc", [128, TC], F32)
    p_d = ps("l_pd", [128, TC], F32)
    p_e = ps("l_pe", [128, TC], F32)
    p_f = ps("l_pf", [128, TC], F32)
    p.dma("sync", s_selh[:].rearrange("p (a n) -> p a n", a=4),
          h["selh"].rearrange("(a p) n -> p a n", p=128), stream="cst")
    p.dma("sync", s_b8[0:8, :], h["b8"], stream="cst")
    p.dma("sync", s_ones[:], h["onesf"], stream="cst")
    cst2 = p.dma("sync", s_ones16[:], h["ones16"], stream="cst")
    for e in ("tensor", "scalar", "vector"):
        p.wait(e, cst2)
    h_selh = [s_selh[:, jj * 128:(jj + 1) * 128] for jj in range(4)]
    h_b8 = [s_b8[0:8, jj * 128:(jj + 1) * 128] for jj in range(4)]

    pa_g = [None, None]
    pb_g = [None, None]
    pc_g = [None]
    pd_g = [None]
    pe_g = [None]
    pf_g = [None]

    def ln_stats_apply_bf16(lv, dst16):
        rstd_tok, mu_tok = emit_ln_stats(p, s_x, NT, D, s_mu, s_rstd, s_sq, 1e-5,
                                         wait_vec=[lv], wait_sc=[lv])
        return emit_ln_apply(p, s_x, dst16, NT, s_mu, s_rstd, rstd_tok, mu_tok)

    def _pass0(p0ctx):
      s_x = p0ctx.enter_context(nc.sbuf_tensor("p0_x", [128, NT, D], F32))
      s_xiF = p0ctx.enter_context(nc.sbuf_tensor("p0_xiF", [128, NJ, TC + 1], F32))
      war = []
      if True:
        for c in range(NCH):
            p.waits("sync", war)
            lv = p.dma("sync", s_x[:],
                       h["xin"][c * TC:(c + 1) * TC, :].rearrange("(i p) d -> p i d", p=128),
                       stream="xld")
            rstd_tok, mu_tok = emit_ln_stats(p, s_x, NT, D, s_mu, s_rstd, s_sq, 1e-5,
                                             wait_vec=[lv], wait_sc=[lv])
            ztok = emit_ln_apply(p, s_x, s_x, NT, s_mu, s_rstd, rstd_tok, mu_tok)
            p.wait("tensor", ztok)
            for j in range(NJ):
                bt = j % 2
                p.wait("tensor", pa_g[bt])
                tr = None
                for i in range(NT):
                    tr = p.op("tensor",
                              lambda e, i=i, j=j, bt=bt: e.transpose(
                                  p_a[bt][:, i * 128:(i + 1) * 128],
                                  s_x[:, i, j * 128:(j + 1) * 128], s_id32[:]),
                              inc=(i == NT - 1))
                p.wait("vector", tr)
                pa_g[bt] = p.op("vector",
                                lambda e, j=j, bt=bt: e.tensor_copy(s_xiF[:, j, 1:TC + 1],
                                                                    p_a[bt][:]),
                                inc=True)
            ev = p.mark("vector")
            p.wait("gpsimd", ev)
            st = p.dma("gpsimd", xi[:, c * TC:(c + 1) * TC].rearrange(
                "(j p) t -> p j t", p=128), s_xiF[:, :, 1:TC + 1], stream="xist")
            if dbg is not None and "xi0" in dbg:
                st = p.dma("gpsimd", dbg["xi0"][:, c * TC:(c + 1) * TC].rearrange(
                    "(j p) t -> p j t", p=128), s_xiF[:, :, 1:TC + 1], stream="xist")
            war = [st, p.mark("tensor")]
    with contextlib.ExitStack() as _p0ctx:
        _pass0(_p0ctx)
    p.barrier()

    if max_pass < 1:
        return None

    def _passA(l):
        with contextlib.ExitStack() as pc:
            def qsb(name, shape, dt):
                return pc.enter_context(nc.sbuf_tensor(name + str(l), shape, dt))
            s_q2 = qsb("l_q2", [128, 4, TC], BF16)
            s_xiF = qsb("l_xiF", [128, NJ, TC + 1], F32)
            s_xi16 = qsb("l_xi16", [128, NJ, TC + 1], BF16)
            s_q = qsb("l_q", [128, 4, TC], BF16)
            s_k = qsb("l_k", [128, 4, TC], BF16)
            s_v = qsb("l_v", [128, 4, TC], BF16)
            s_kp = qsb("l_kp", [128, 4, TC], BF16)
            s_vp = qsb("l_vp", [128, 4, TC], BF16)
            s_vT = qsb("l_vT", [128, NT, INNER], BF16)
            s_vTp = qsb("l_vTp", [128, NT, INNER], BF16)
            s_qf = qsb("l_qf", [128, 4, TC], F32)
            s_n2 = qsb("l_n2", [128, 2 * TC], F32)
            s_n16 = qsb("l_n16", [128, 2 * TC], BF16)
            s_tab = qsb("l_tab", [128, 4, TC], F32)
            s_sim = qsb("l_sim", [128, 2 * LW], F32)
            s_prob = qsb("l_prob", [128, 2 * LW], BF16)
            s_pT = qsb("l_pT", [128, 2 * LW], BF16)
            s_row = qsb("l_row", [128, 8], F32)
            s_oT = qsb("l_oT", [128, NT, INNER], BF16)
            s_oF = qsb("l_oF", [128, 4, TC], BF16)
            s_w3 = qsb("l_w3", [128, NJ, 1536], BF16)
            s_wo = qsb("l_wo", [128, 4, D], BF16)
            s_u1 = qsb("l_u1", [128, TC], F32)
            s_u2 = qsb("l_u2", [128, TC], F32)
            s_u2b = qsb("l_u2b", [128, TC], BF16)
            s_u3 = qsb("l_u3", [128, TC], F32)
            s_mask = qsb("l_mask", [128, 8, LW], F32)
            p.dma("sync", s_mask[:], h["lmask"].rearrange("(a p) w -> p a w", p=128),
                  stream="cst")
            wld = p.dma("gpsimd", s_w3[:],
                        h["wqkv"][l].rearrange("(kc p) n -> p kc n", p=128), stream="wld")
            wld = p.dma("gpsimd", s_wo[:],
                        h["wo"][l].rearrange("(kc p) n -> p kc n", p=128), stream="wld")
            p.wait("tensor", wld)
            qs_col = s_qk[:, 2 * l:2 * l + 1]
            ks_col = s_qk[:, 2 * l + 1:2 * l + 2]
            cur, prv = 0, 1
            kbufs = [s_k, s_kp]
            vbufs = [s_v, s_vp]
            vTbufs = [s_vT, s_vTp]
            xist = None
            for c in range(NCH):
                if c == 0:
                    ttok = p.op("vector", lambda e: e.memset(s_xiF[:, :, 0:1], 0.0),
                                inc=True)
                else:
                    ttok = p.op("vector", lambda e: e.tensor_copy(
                        s_xiF[:, :, 0:1], s_xiF[:, :, TC:TC + 1]), inc=True)
                p.wait("sync", ttok)
                if xist is not None:
                    p.wait("sync", xist)
                lv = p.dma("sync", s_xiF[:, :, 1:TC + 1],
                           xi[:, c * TC:(c + 1) * TC].rearrange("(j p) t -> p j t", p=128),
                           stream="xild")
                p.wait("vector", lv)
                cp = p.op("vector", lambda e: e.tensor_copy(s_xi16[:], s_xiF[:]), inc=True)
                ttoks = []
                for ti in range(4):
                    for hdup in range(2):
                        ttoks.append(p.dma(
                            "sync", s_tab[hdup * 64:(hdup + 1) * 64, ti, :],
                            h["tables"][ti][:, c * TC:(c + 1) * TC], stream="tab"))
                p.wait("tensor", cp)
                p.waits("vector", ttoks[-1:])
                for no in range(12):
                    alt = no % 2
                    p.wait("tensor", pa_g[alt])
                    mm = None
                    for kc in range(NJ):
                        rhs = (s_xi16[:, kc, 1:TC + 1] if kc < 4
                               else s_xi16[:, kc, 0:TC])
                        mm = p.op("tensor",
                                  lambda e, no=no, kc=kc, alt=alt, rhs=rhs: e.matmul(
                                      p_a[alt][:], s_w3[:, kc, no * 128:(no + 1) * 128],
                                      rhs, start=(kc == 0), stop=(kc == NJ - 1)),
                                  inc=(kc == NJ - 1))
                    p.wait("vector", mm)
                    if no < 8:
                        j = no % 4
                        pa_g[alt] = p.op("vector",
                                         lambda e, j=j, alt=alt: e.tensor_copy(
                                             s_qf[:, j, :], p_a[alt][:]), inc=True)
                        p.wait("scalar", pa_g[alt])
                        sq = p.op("scalar",
                                  lambda e, j=j: e.activation(
                                      s_q2[:, j, :], s_qf[:, j, :], AF.Square),
                                  inc=(j == 3))
                        if j == 3:
                            p.wait("tensor", sq)
                            p.wait("tensor", pc_g[0])
                            nm = None
                            for jj in range(4):
                                nm = p.op("tensor",
                                          lambda e, jj=jj: e.matmul(
                                              p_c[:], h_selh[jj], s_q2[:, jj, :],
                                              start=(jj == 0), stop=(jj == 3)),
                                          inc=(jj == 3))
                            qk = 0 if no < 4 else 1
                            p.wait("scalar", nm)
                            t1 = p.op("scalar", lambda e, qk=qk: e.activation(
                                s_n2[0:8, qk * TC:(qk + 1) * TC], p_c[0:8, :], AF.Ln),
                                inc=True)
                            p.wait("scalar", t1)
                            t2 = p.op("scalar", lambda e, qk=qk: e.activation(
                                s_n2[0:8, qk * TC:(qk + 1) * TC],
                                s_n2[0:8, qk * TC:(qk + 1) * TC], AF.Exp, scale=0.5),
                                inc=True)
                            p.wait("vector", t2)
                            t3 = p.op("vector", lambda e, qk=qk: e.tensor_scalar_max(
                                s_n2[0:8, qk * TC:(qk + 1) * TC],
                                s_n2[0:8, qk * TC:(qk + 1) * TC], 1e-12), inc=True)
                            p.wait("vector", t3)
                            t4 = p.op("vector", lambda e, qk=qk: e.reciprocal(
                                s_n2[0:8, qk * TC:(qk + 1) * TC],
                                s_n2[0:8, qk * TC:(qk + 1) * TC]), inc=True)
                            p.wait("vector", t4)
                            rn16 = p.op("vector", lambda e, qk=qk: e.tensor_copy(
                                s_n16[0:8, qk * TC:(qk + 1) * TC],
                                s_n2[0:8, qk * TC:(qk + 1) * TC]), inc=True)
                            pc_g[0] = rn16
                            dst16 = kbufs[cur] if qk == 1 else s_q
                            scol = ks_col if qk == 1 else qs_col
                            tci = 2 if qk == 1 else 0
                            p.wait("tensor", rn16)
                            for jj in range(4):
                                p.wait("tensor", pd_g[0])
                                bm = p.op("tensor", lambda e, jj=jj, qk=qk: e.matmul(
                                    p_d[:], h_b8[jj], s_n16[0:8, qk * TC:(qk + 1) * TC],
                                    start=True, stop=True), inc=True)
                                p.wait("vector", bm)
                                u1 = p.op("vector", lambda e, jj=jj: e.tensor_tensor(
                                    s_u1[:], s_qf[:, jj, :], p_d[:], ALU.mult), inc=True)
                                pd_g[0] = u1
                                u2 = p.op("vector", lambda e, jj=jj, scol=scol: e.tensor_scalar_mul(
                                    s_u2[:], s_u1[:], scol), inc=True)
                                u2b = p.op("vector", lambda e: e.tensor_copy(
                                    s_u2b[:], s_u2[:]), inc=True)
                                p.wait("tensor", u2b)
                                p.wait("tensor", pe_g[0])
                                rm = p.op("tensor", lambda e: e.matmul(
                                    p_e[:], s_rot[:], s_u2b[:], start=True, stop=True),
                                    inc=True)
                                p.wait("vector", rm)
                                w1_ = p.op("vector", lambda e, tci=tci: e.tensor_tensor(
                                    s_u1[:], s_u2[:], s_tab[:, tci, :], ALU.mult))
                                w2_ = p.op("vector", lambda e, tci=tci: e.tensor_tensor(
                                    s_u3[:], p_e[:], s_tab[:, tci + 1, :], ALU.mult),
                                    inc=True)
                                pe_g[0] = w2_
                                p.op("vector",
                                     lambda e, jj=jj, dst16=dst16: e.tensor_tensor(
                                         dst16[:, jj, :], s_u1[:], s_u3[:], ALU.add),
                                     inc=True)
                    else:
                        j = no - 8
                        vb = vbufs[cur]
                        pa_g[alt] = p.op("vector",
                                         lambda e, j=j, alt=alt, vb=vb: e.tensor_copy(
                                             vb[:, j, :], p_a[alt][:]), inc=True)
                vdone = p.mark("vector")
                p.wait("tensor", vdone)
                tr_fm2tm(p, vbufs[cur], vTbufs[cur], p_b, s_id16, 4, pb_g)
                first = (c == 0)
                for hh in range(LH):
                    hp = (hh % 2) * 64
                    jt = hh // 2
                    for it in range(NT):
                        halves = (("cur",),) if first else (("prv", "cur"),)
                        hv_toks = []
                        width = LW if first else 2 * LW
                        for hi, half in enumerate(halves[0]):
                            kb = kbufs[cur] if half == "cur" else kbufs[prv]
                            p.wait("tensor", pb_g[hi])
                            sm = p.op("tensor",
                                      lambda e, hh=hh, it=it, hi=hi, kb=kb, hp=hp, jt=jt: e.matmul(
                                          p_b[hi][:],
                                          s_q[hp:hp + 64, jt, it * 128:(it + 1) * 128],
                                          kb[hp:hp + 64, jt, :],
                                          start=True, stop=True),
                                      inc=True)
                            p.wait("vector", sm)
                            mi = (0 if half == "prv" else 4) + it
                            off = 0 if (half == "prv" and not first) else (0 if first else LW)
                            if first:
                                off = 0
                            pb_g[hi] = p.op(
                                "vector",
                                lambda e, hi=hi, mi=mi, off=off: e.tensor_tensor(
                                    s_sim[:, off:off + LW], p_b[hi][:],
                                    s_mask[:, mi, :], ALU.add),
                                inc=True)
                            hv_toks.append(pb_g[hi])
                        mx = p.op("vector", lambda e, width=width: e.reduce_max(
                            s_row[:, 0:1], s_sim[:, 0:width], AXL.X), inc=True)
                        p.wait("vector", mx)
                        ng = p.op("vector", lambda e: e.tensor_scalar_mul(
                            s_row[:, 1:2], s_row[:, 0:1], -1.0), inc=True)
                        p.wait("scalar", ng)
                        ex = p.op("scalar", lambda e, width=width: e.activation(
                            s_prob[:, 0:width], s_sim[:, 0:width], AF.Exp,
                            bias=s_row[:, 1:2], accum_out=s_row[:, 2:3]), inc=True)
                        p.wait("vector", ex)
                        rc = p.op("vector", lambda e: e.reciprocal(
                            s_row[:, 3:4], s_row[:, 2:3]), inc=True)
                        p.wait("tensor", ex)
                        p.wait("tensor", pe_g[0])
                        ntj = width // 128
                        trp = None
                        for jc in range(ntj):
                            trp = p.op("tensor",
                                       lambda e, jc=jc: e.transpose(
                                           p_e[:].bitcast(BF16)[:, jc * 128:(jc + 1) * 128],
                                           s_prob[:, jc * 128:(jc + 1) * 128], s_id16[:]),
                                       inc=(jc == ntj - 1))
                        p.wait("vector", trp)
                        pe_g[0] = p.op("vector", lambda e, ntj=ntj: e.tensor_copy(
                            s_pT[:, 0:ntj * 128], p_e[:].bitcast(BF16)[:, 0:ntj * 128]),
                            inc=True)
                        p.wait("tensor", pe_g[0])
                        p.wait("tensor", pd_g[0])
                        pv = None
                        for jc in range(ntj):
                            if first:
                                vT_src, vjc = vTbufs[cur], jc
                            else:
                                vT_src = vTbufs[prv] if jc < 4 else vTbufs[cur]
                                vjc = jc % 4
                            pv = p.op("tensor",
                                      lambda e, jc=jc, vT_src=vT_src, vjc=vjc, hh=hh,
                                      ntj=ntj: e.matmul(
                                          p_d[0:128, 0:64],
                                          s_pT[:, jc * 128:(jc + 1) * 128],
                                          vT_src[:, vjc, hh * 64:(hh + 1) * 64],
                                          start=(jc == 0), stop=(jc == ntj - 1)),
                                      inc=(jc == ntj - 1))
                        p.wait("vector", pv)
                        p.wait("vector", rc)
                        pd_g[0] = p.op("vector",
                                       lambda e, hh=hh, it=it: e.tensor_scalar_mul(
                                           s_oT[:, it, hh * 64:(hh + 1) * 64],
                                           p_d[0:128, 0:64], s_row[:, 3:4]),
                                       inc=True)
                odone = p.mark("vector")
                p.wait("tensor", odone)
                tr_tm2fm(p, s_oT, s_oF, p_a, s_id16, 4, pa_g)
                for no in range(NJ):
                    alt = no % 2
                    p.wait("tensor", pa_g[alt])
                    mm = None
                    for kc in range(4):
                        mm = p.op("tensor",
                                  lambda e, no=no, kc=kc, alt=alt: e.matmul(
                                      p_a[alt][:], s_wo[:, kc, no * 128:(no + 1) * 128],
                                      s_oF[:, kc, :], start=(kc == 0), stop=(kc == 3)),
                                  inc=(kc == 3))
                    p.wait("vector", mm)
                    pa_g[alt] = p.op("vector",
                                     lambda e, no=no, alt=alt: e.tensor_tensor(
                                         s_xiF[:, no, 1:TC + 1], p_a[alt][:],
                                         s_xiF[:, no, 1:TC + 1], ALU.add), inc=True)
                xadd = p.mark("vector")
                p.wait("gpsimd", xadd)
                xist = p.dma("gpsimd", xi[:, c * TC:(c + 1) * TC].rearrange(
                    "(j p) t -> p j t", p=128), s_xiF[:, :, 1:TC + 1], stream="xist")
                cur, prv = prv, cur
            p.barrier()

    def _passF(l):
        with contextlib.ExitStack() as pc:
            def fsb(name, shape, dt):
                return pc.enter_context(nc.sbuf_tensor(name + str(l), shape, dt))
            s_w1 = fsb("l_w1", [128, NJ, 5460], BF16)
            s_w2g = [fsb("l_w2g0", [128, 6, D], BF16)]
            s_lnrow = fsb("l_lnrow", [1, 4 * TC], F32)
            s_gs = fsb("l_gs2", [128, 2, TC], F32)
            s_t2m = fsb("l_t2m", [128, NT, D], F32)
            s_xiF = fsb("l_xiF2", [128, NJ, TC + 1], F32)
            s_hn = fsb("l_hn", [128, NJ, TC], BF16)
            s_sq16 = s_hn
            s_g = fsb("l_g", [128, 22, TC], BF16)
            s_u1 = fsb("l_u1f", [128, TC], F32)
            s_x = fsb("l_x2", [128, NT, D], F32)
            wld = p.dma("gpsimd", s_w1[:],
                        h["w1p"][l].rearrange("(kc p) n -> p kc n", p=128), stream="wld")
            p.wait("tensor", wld)
            W2G = [(0, 6, 768), (6, 6, 768), (12, 6, 768), (18, 4, 426)]
            w2ld = [None]
            w2use = [None]
            xist = None
            outst = None
            for c in range(NCH):
                if c == 0:
                    ttok = p.op("vector", lambda e: e.memset(s_xiF[:, :, 0:1], 0.0),
                                inc=True)
                else:
                    ttok = p.op("vector", lambda e: e.tensor_copy(
                        s_xiF[:, :, 0:1], s_xiF[:, :, TC:TC + 1]), inc=True)
                p.wait("sync", ttok)
                if xist is not None:
                    p.wait("sync", xist)
                lv = p.dma("sync", s_xiF[:, :, 1:TC + 1],
                           xi[:, c * TC:(c + 1) * TC].rearrange("(j p) t -> p j t", p=128),
                           stream="xild")
                h2f = [(s_xiF[:, kc, 1:TC + 1] if kc < 4 else s_xiF[:, kc, 0:TC])
                       for kc in range(NJ)]
                p.wait("tensor", lv)
                p.wait("tensor", pc_g[0])
                sm = None
                for kc in range(NJ):
                    sm = p.op("tensor",
                              lambda e, kc=kc: e.matmul(
                                  p_c[0:1, :], s_ones[:, 0:1], h2f[kc],
                                  start=(kc == 0), stop=(kc == NJ - 1)),
                              inc=(kc == NJ - 1))
                p.wait("scalar", lv)
                sq = None
                for kc in range(NJ):
                    p.wait("scalar", getattr(p, "_sq_tok", None))
                    p._sq_tok = sq = p.op("scalar",
                                          lambda e, kc=kc: e.activation(
                                              s_sq16[:, kc, :], h2f[kc], AF.Square),
                                          inc=True)
                p.wait("tensor", sq)
                p.wait("tensor", pd_g[0])
                sqm = None
                for kc in range(NJ):
                    sqm = p.op("tensor",
                               lambda e, kc=kc: e.matmul(
                                   p_d[0:1, :], s_ones16[:, 0:1], s_sq16[:, kc, :],
                                   start=(kc == 0), stop=(kc == NJ - 1)),
                               inc=(kc == NJ - 1))
                row_mu = s_lnrow[:, 0:TC]
                row_ms = s_lnrow[:, TC:2 * TC]
                row_r = s_lnrow[:, 2 * TC:3 * TC]
                row_mr = s_lnrow[:, 3 * TC:4 * TC]
                p.wait("vector", sm)
                p.wait("vector", sqm)
                p.op("vector", lambda e: e.tensor_scalar_mul(row_mu, p_c[0:1, :], 1.0 / D))
                p.op("vector", lambda e: e.tensor_scalar_mul(row_ms, p_d[0:1, :], 1.0 / D))
                p.op("vector", lambda e: e.tensor_tensor(row_r, row_mu, row_mu, ALU.mult))
                rv = p.op("vector", lambda e: e.tensor_scalar(
                    row_r, row_r, -1.0, 1e-5, ALU.mult, ALU.add), inc=True)
                p.wait("vector", rv)
                rv = p.op("vector", lambda e: e.tensor_tensor(row_r, row_ms, row_r, ALU.add),
                          inc=True)
                p.wait("scalar", rv)
                t1 = p.op("scalar", lambda e: e.activation(row_r, row_r, AF.Ln), inc=True)
                p.wait("scalar", t1)
                t2 = p.op("scalar", lambda e: e.activation(row_r, row_r, AF.Exp, scale=-0.5),
                          inc=True)
                p.wait("vector", t2)
                mr = p.op("vector", lambda e: e.tensor_tensor(row_mr, row_mu, row_r, ALU.mult),
                          inc=True)
                p.wait("tensor", mr)
                bc1 = p.op("tensor", lambda e: e.matmul(p_c[:], s_ones[0:1, :], row_r,
                                                        start=True, stop=True), inc=True)
                bc2 = p.op("tensor", lambda e: e.matmul(p_d[:], s_ones[0:1, :], row_mr,
                                                        start=True, stop=True), inc=True)
                p.wait("vector", bc2)
                ap_tok = None
                for kc in range(NJ):
                    p.op("vector", lambda e, kc=kc: e.tensor_tensor(
                        s_u1[:], h2f[kc], p_c[:], ALU.mult))
                    p.op("vector", lambda e: e.tensor_tensor(
                        s_u1[:], s_u1[:], p_d[:], ALU.subtract))
                    ap_tok = p.op("vector", lambda e, kc=kc: e.tensor_scalar(
                        s_hn[:, kc, :], s_u1[:],
                        s_gb[:, l * 2 * NJ + kc:l * 2 * NJ + kc + 1],
                        s_gb[:, l * 2 * NJ + NJ + kc:l * 2 * NJ + NJ + kc + 1],
                        ALU.mult, ALU.add), inc=True)
                pc_g[0] = pd_g[0] = ap_tok
                p.wait("tensor", ap_tok)
                gate_done = []
                for s in range(44):
                    ws = W1S[s]
                    alt = s % 2
                    p.wait("tensor", pa_g[alt])
                    mm = None
                    for kc in range(NJ):
                        mm = p.op("tensor",
                                  lambda e, s=s, kc=kc, alt=alt, ws=ws: e.matmul(
                                      p_a[alt][0:ws, :],
                                      s_w1[:, kc, W1OFF[s]:W1OFF[s] + ws],
                                      s_hn[:, kc, :], start=(kc == 0), stop=(kc == NJ - 1)),
                                  inc=(kc == NJ - 1))
                    bcol = s_b1[:, l * 44 + s:l * 44 + s + 1]
                    if s % 2 == 0:
                        p.wait("vector", mm)
                        pa_g[alt] = p.op("vector",
                                         lambda e, ws=ws, alt=alt, bcol=bcol: e.tensor_scalar_add(
                                             s_gs[0:ws, 0, :], p_a[alt][0:ws, :], bcol[0:ws, :]),
                                         inc=True)
                    else:
                        p.wait("scalar", mm)
                        gtok = p.op("scalar",
                                    lambda e, ws=ws, alt=alt, bcol=bcol: e.activation(
                                        s_gs[0:ws, 1, :], p_a[alt][0:ws, :], AF.Gelu,
                                        bias=bcol[0:ws, :]), inc=True)
                        pa_g[alt] = gtok
                        p.wait("vector", gtok)
                        gate_done.append(p.op("vector",
                                              lambda e, s=s, ws=ws: e.tensor_tensor(
                                                  s_g[0:ws, s // 2, :], s_gs[0:ws, 0, :],
                                                  s_gs[0:ws, 1, :], ALU.mult), inc=True))
                p.wait("tensor", gate_done[-1])
                banks = [p_a[0], p_a[1], p_b[0], p_b[1], p_c, p_d, p_e, p_f]
                bguards = [pa_g, pa_g, pb_g, pb_g, pc_g, pd_g, pe_g, pf_g]
                bgidx = [0, 1, 0, 1, 0, 0, 0, 0]
                mm_toks = [None] * NJ
                for gi, (kc0, nkc, nrows) in enumerate(W2G):
                    b = 0
                    p.wait("gpsimd", w2use[b])
                    if nrows % 128 == 0:
                        w2ld[b] = p.dma("gpsimd", s_w2g[b][:, 0:nkc, :],
                                        h["w2"][l][W2OFF[kc0]:W2OFF[kc0] + nrows, :].rearrange(
                                            "(kc p) n -> p kc n", p=128), stream="w20")
                    else:
                        p.dma("gpsimd", s_w2g[b][:, 0:3, :],
                              h["w2"][l][W2OFF[kc0]:W2OFF[kc0] + 384, :].rearrange(
                                  "(kc p) n -> p kc n", p=128), stream="w20")
                        w2ld[b] = p.dma("gpsimd", s_w2g[b][0:42, 3, :],
                                        h["w2"][l][W2OFF[kc0] + 384:W2OFF[kc0] + 426, :],
                                        stream="w20")
                    p.wait("tensor", w2ld[b])
                    mm = None
                    for no in range(NJ):
                        if gi == 0:
                            p.wait("tensor", bguards[no][bgidx[no]])
                        for kci in range(nkc):
                            ws = W2S[kc0 + kci]
                            mm = p.op("tensor",
                                      lambda e, kci=kci, no=no, ws=ws, b=b,
                                      kc0=kc0, nkc=nkc, gi=gi: e.matmul(
                                          banks[no][:],
                                          s_w2g[b][0:ws, kci, no * 128:(no + 1) * 128],
                                          s_g[0:ws, kc0 + kci, :],
                                          start=(gi == 0 and kci == 0),
                                          stop=(gi == 3 and kci == nkc - 1),
                                          skip_group_check=True),
                                      inc=(kci == nkc - 1))
                        if gi == 3:
                            mm_toks[no] = mm
                    w2use[b] = mm
                for no in range(NJ):
                    b2col = s_b2[:, l * NJ + no:l * NJ + no + 1]
                    p.wait("vector", mm_toks[no])
                    p.op("vector", lambda e, no=no, b2col=b2col: e.tensor_scalar_add(
                        s_u1[:], banks[no][:], b2col))
                    tok = p.op("vector",
                               lambda e, no=no: e.tensor_tensor(
                                   s_xiF[:, no, 1:TC + 1], s_u1[:],
                                   s_xiF[:, no, 1:TC + 1], ALU.add), inc=True)
                    bguards[no][bgidx[no]] = tok
                xdone = p.mark("vector")
                if l == 0:
                    p.wait("gpsimd", xdone)
                    xist = p.dma("gpsimd", xi[:, c * TC:(c + 1) * TC].rearrange(
                        "(j p) t -> p j t", p=128), s_xiF[:, :, 1:TC + 1], stream="xist")
                else:
                    if outst is not None:
                        p.wait("sync", outst)
                    xin_lv = p.dma("sync", s_x[:],
                                   h["xin"][c * TC:(c + 1) * TC, :].rearrange(
                                       "(i p) d -> p i d", p=128), stream="xld")
                    p.wait("tensor", xdone)
                    for j in range(NJ):
                        bt = j % 2
                        p.wait("tensor", pa_g[bt])
                        tr = None
                        for i in range(NT):
                            tr = p.op("tensor",
                                      lambda e, i=i, j=j, bt=bt: e.transpose(
                                          p_a[bt][:, i * 128:(i + 1) * 128],
                                          s_xiF[:, j, 1 + i * 128:1 + (i + 1) * 128],
                                          s_id32[:]),
                                      inc=(i == NT - 1))
                        p.wait("vector", tr)
                        pa_g[bt] = p.op("vector",
                                        lambda e, j=j, bt=bt: e.tensor_copy(
                                            s_t2m[:, :, j * 128:(j + 1) * 128],
                                            p_a[bt][:].rearrange("p (i w) -> p i w", i=NT)),
                                        inc=True)
                    pv = p.mark("vector")
                    rstd_tok, mu_tok = emit_ln_stats(p, s_t2m, NT, D, s_mu, s_rstd, s_sq,
                                                     1e-5, wait_vec=[], wait_sc=[pv])
                    emit_ln_apply(p, s_t2m, s_t2m, NT, s_mu, s_rstd, rstd_tok, mu_tok)
                    p.wait("vector", xin_lv)
                    ov = p.op("vector", lambda e: e.tensor_tensor(
                        s_t2m[:], s_t2m[:], s_x[:], ALU.add), inc=True)
                    p.wait("gpsimd", ov)
                    outst = p.dma("gpsimd",
                                  h["xout"][c * TC:(c + 1) * TC, :].rearrange(
                                      "(i p) d -> p i d", p=128), s_t2m[:], stream="xst")
                    xist = outst
            p.barrier()
            return outst

    outst = None
    for l in range(2):
        if max_pass < 2 * l + 1:
            return None
        _passA(l)
        if max_pass < 2 * l + 2:
            return None
        outst = _passF(l)
    p.wait("gpsimd", outst)
    return outst


def lffn_consts():
    T_, HD64 = T, 64
    R = np.zeros((128, 128), np.float32)
    for po in range(128):
        h0 = (po // 64) * 64
        i = po - h0
        if i < 32:
            R[h0 + i + 32, po] = -1.0
        else:
            R[h0 + i - 32, po] = 1.0
    M = np.zeros((8, 128, LW), np.float32)
    jj = np.arange(LW)
    for it in range(4):
        ii = it * 128 + np.arange(128)[:, None]
        M[it] = np.where(jj[None, :] >= ii, 0.0, -1e30)
        M[4 + it] = np.where(jj[None, :] <= ii, 0.0, -1e30)
    half = np.arange(0, HD64, 2, dtype=np.float64)
    inv_freq = 1.0 / 10000.0 ** (half / HD64)
    t = np.arange(T_, dtype=np.float64)
    fr = t[:, None] * inv_freq[None, :]
    freqs = np.concatenate([fr, fr], axis=-1)
    base = (half + 0.4 * HD64) / (1.4 * HD64)
    power = (t - T_ // 2) / float(LW // 2)
    sc = base[None, :] ** power[:, None]
    scale = np.concatenate([sc, sc], axis=-1)
    cos, sin = np.cos(freqs), np.sin(freqs)
    QKS = 8.0
    tabs = [
        (QKS * cos * scale).T.astype(np.float32),
        (QKS * sin * scale).T.astype(np.float32),
        (cos / scale).T.astype(np.float32),
        (sin / scale).T.astype(np.float32),
    ]
    selh = np.zeros((4, 128, 128), np.float32)
    b8 = np.zeros((8, 4, 128), np.float32)
    for j in range(4):
        for pp in range(128):
            selh[j, pp, 2 * j + pp // 64] = 1.0
            b8[2 * j + pp // 64, j, pp] = 1.0
    return dict(
        rot128=R.astype(BF16NP),
        lmask=np.ascontiguousarray(M.reshape(8 * 128, LW)),
        tables=[np.ascontiguousarray(x) for x in tabs],
        selh=np.ascontiguousarray(selh.reshape(4 * 128, 128).astype(BF16NP)),
        b8=np.ascontiguousarray(b8.reshape(8, 4 * 128).astype(BF16NP)),
        onesf=np.ones((128, 128), np.float32),
        ones16=np.ones((128, 128), BF16NP),
    )


def w1_perm():
    perm = []
    for s in range(21):
        perm += list(range(s * 128, (s + 1) * 128))
        perm += list(range(HID + s * 128, HID + (s + 1) * 128))
    perm += list(range(21 * 128, HID))
    perm += list(range(HID + 21 * 128, 2 * HID))
    return np.array(perm)


def build_lffn_nc(debug=False, max_pass=99):
    nc = bass.Bass("TRN2", target_bir_lowering=False, debug=False)
    dbg = None
    h = {}
    h["xin"] = nc.dram_tensor("xin", [T, D], F32, kind="ExternalInput").ap()
    h["xout"] = nc.dram_tensor("xout", [T, D], F32, kind="ExternalOutput").ap()
    h["wqkv"] = [nc.dram_tensor(f"wqkv{l}", [D, 3 * INNER], F32, kind="ExternalInput").ap()
                 for l in range(2)]
    h["wo"] = [nc.dram_tensor(f"wo{l}", [INNER, D], F32, kind="ExternalInput").ap()
               for l in range(2)]
    h["w1p"] = [nc.dram_tensor(f"w1p{l}", [D, 2 * HID], F32, kind="ExternalInput").ap()
                for l in range(2)]
    h["w2"] = [nc.dram_tensor(f"w2_{l}", [HID, D], F32, kind="ExternalInput").ap()
               for l in range(2)]
    h["qkscale"] = nc.dram_tensor("qkscale", [128, 4], F32, kind="ExternalInput").ap()
    h["lngb"] = nc.dram_tensor("lngb", [128, 32], F32, kind="ExternalInput").ap()
    h["lb1"] = nc.dram_tensor("lb1", [128, 88], F32, kind="ExternalInput").ap()
    h["lb2"] = nc.dram_tensor("lb2", [128, 16], F32, kind="ExternalInput").ap()
    h["rot128"] = nc.dram_tensor("rot128", [128, 128], BF16, kind="ExternalInput").ap()
    h["lmask"] = nc.dram_tensor("lmask", [8 * 128, LW], F32, kind="ExternalInput").ap()
    tabs = [nc.dram_tensor(f"tab{i}", [64, T], F32, kind="ExternalInput").ap()
            for i in range(4)]
    h["tables"] = tabs
    h["selh"] = nc.dram_tensor("selh", [4 * 128, 128], BF16, kind="ExternalInput").ap()
    h["b8"] = nc.dram_tensor("b8", [8, 4 * 128], BF16, kind="ExternalInput").ap()
    h["onesf"] = nc.dram_tensor("onesf", [128, 128], F32, kind="ExternalInput").ap()
    h["ones16"] = nc.dram_tensor("ones16", [128, 128], BF16, kind="ExternalInput").ap()
    h["ident16"] = nc.dram_tensor("ident16", [128, 128], BF16, kind="ExternalInput").ap()
    h["ident32"] = nc.dram_tensor("ident32", [128, 128], F32, kind="ExternalInput").ap()
    if debug:
        dbg = dict(xi0=nc.dram_tensor("dbg_xi0", [D, T], F32, kind="ExternalOutput").ap())
    with contextlib.ExitStack() as ctx:
        p = Prog(nc, ctx)
        def sb(name, shape, dt):
            return ctx.enter_context(nc.sbuf_tensor(name, shape, dt))
        def ps(name, shape, dt):
            return ctx.enter_context(nc.psum_tensor(name, shape, dt))
        build_lffn_block(p, sb, ps, h, dbg, max_pass=max_pass)
        p.emit()
    return nc


def lffn_in_map(x, wqkv, q_scale, k_scale, wo, ln_g, ln_b, w1, b1, w2, b2):
    id16, id32 = _ident_np()
    consts = lffn_consts()
    perm = w1_perm()
    im = dict(
        xin=np.ascontiguousarray(x, np.float32),
        ident16=id16, ident32=id32,
        rot128=consts["rot128"], lmask=consts["lmask"],
        selh=consts["selh"], b8=consts["b8"],
        onesf=consts["onesf"], ones16=consts["ones16"],
    )
    for i in range(4):
        im[f"tab{i}"] = consts["tables"][i]
    qk = np.zeros((128, 4), np.float32)
    lngb = np.zeros((128, 32), np.float32)
    lb1 = np.zeros((128, 88), np.float32)
    lb2 = np.zeros((128, 16), np.float32)
    for l in range(2):
        qk[:, 2 * l] = np.tile(q_scale[l], 2)
        qk[:, 2 * l + 1] = np.tile(k_scale[l], 2)
        for kc in range(8):
            lngb[:, l * 16 + kc] = ln_g[l][kc * 128:(kc + 1) * 128]
            lngb[:, l * 16 + 8 + kc] = ln_b[l][kc * 128:(kc + 1) * 128]
            lb2[:, l * 8 + kc] = b2[l][kc * 128:(kc + 1) * 128]
        b1p = b1[l][perm]
        for s in range(44):
            ws = W1S[s]
            lb1[0:ws, l * 44 + s] = b1p[W1OFF[s]:W1OFF[s] + ws]
        im[f"wqkv{l}"] = np.ascontiguousarray(wqkv[l], np.float32)
        im[f"wo{l}"] = np.ascontiguousarray(wo[l], np.float32)
        im[f"w1p{l}"] = np.ascontiguousarray(w1[l][:, perm], np.float32)
        im[f"w2_{l}"] = np.ascontiguousarray(w2[l], np.float32)
    im["qkscale"] = qk
    im["lngb"] = lngb
    im["lb1"] = lb1
    im["lb2"] = lb2
    return im



DI = 2048
NDB = DI // 128
NG = 8
NGRP = 64 // NG


def build_mamba_block(p, sb, ps, h, dbg=None):
    nc = p.nc
    NT = TC // 128
    NJ = D // 128
    xc_d = nc.dram_tensor("mb_xc", [DI, T], BF16).ap()
    g_d = nc.dram_tensor("mb_g", [DI, T], BF16).ap()
    ym_d = nc.dram_tensor("mb_ym", [DI, T], BF16).ap()
    B_d = nc.dram_tensor("mb_B", [64, T], BF16).ap()

    s_id16 = sb("m_id16", [128, 128], BF16)
    s_id32 = sb("m_id32", [128, 128], F32)
    s_mu = sb("m_mu", [128, 4 * NT], F32)
    s_rstd = sb("m_rstd", [128, NT], F32)
    s_nwbc = sb("m_nwbc", [128, D], F32)
    s_cols = sb("m_cols", [128, 4 * NDB], F32)
    s_cw = sb("m_cw", [128, NDB, 4], F32)
    p.dma("sync", s_id16[:], h["ident16"], stream="cst")
    p.dma("sync", s_id32[:], h["ident32"], stream="cst")
    p.dma("sync", s_nwbc[:], h["norm_w"].partition_broadcast(128).squeeze(1), stream="cst")
    p.dma("sync", s_cols[:, 0:NDB], h["dt_b"], stream="cst")
    p.dma("sync", s_cols[:, NDB:2 * NDB], h["conv_b"], stream="cst")
    p.dma("sync", s_cols[:, 2 * NDB:3 * NDB], h["D_param"], stream="cst")
    cst = p.dma("sync", s_cw[:], h["conv_w"], stream="cst")
    for e in ("tensor", "scalar", "vector"):
        p.wait(e, cst)
    p_k = [ps(f"m_p{i}", [128, TC], F32) for i in range(8)]
    pg = [[None] for _ in range(8)]

    def ln_z(c, war, s_x, s_z, s_sqv):
        p.waits("sync", war)
        lv = p.dma("sync", s_x[:],
                   h["xin"][c * TC:(c + 1) * TC, :].rearrange("(i p) d -> p i d", p=128),
                   stream="xld")
        rstd_tok, mu_tok = emit_ln_stats(p, s_x, NT, D, s_mu, s_rstd, s_sqv, 1e-5,
                                         wait_vec=[lv], wait_sc=[lv])
        ztok = emit_ln_apply(p, s_x, s_z, NT, s_mu, s_rstd, rstd_tok, mu_tok)
        return lv, ztok

    def _passM1(pc):
        def msb(name, shape, dt):
            return pc.enter_context(nc.sbuf_tensor(name, shape, dt))
        s_x = msb("m1_x", [128, NT, D], F32)
        s_z = msb("m1_z", [128, NT, D], F32)
        s_t = msb("m1_t", [128, NT, D], F32)
        s_xn16 = msb("m1_xn16", [128, NT, D], BF16)
        s_xnF = msb("m1_xnF", [128, NJ, TC], BF16)
        s_w = msb("m1_w", [128, NJ, DI], BF16)
        s_xi = msb("m1_xi", [128, NDB, TC + 3], BF16)
        s_g16 = msb("m1_g16", [128, NDB, TC], BF16)
        s_xc16 = msb("m1_xc16", [128, NDB, TC], BF16)
        s_cacc = msb("m1_cacc", [128, TC], F32)
        war = []
        st_xc = st_g = None
        for c in range(NCH):
            lv, ztok = ln_z(c, war, s_x, s_z, s_t[:, 0, :])
            mcol = s_cols[:, 3 * NDB + NT:3 * NDB + 2 * NT]
            p.wait("scalar", ztok)
            sqv = None
            for i in range(NT):
                p.wait("scalar", getattr(p, "_sq_tok", None))
                p._sq_tok = sqv = p.op("scalar",
                                       lambda e, i=i, mcol=mcol: e.activation(
                                           s_t[:, 0, :], s_z[:, i, :], AF.Square,
                                           accum_out=mcol[:, i:i + 1]),
                                       inc=True)
            rcol = s_cols[:, 3 * NDB:3 * NDB + NT]
            p.wait("vector", sqv)
            t1 = p.op("vector", lambda e, rcol=rcol, mcol=mcol: e.tensor_scalar(
                rcol, mcol, 1.0 / D, 1e-6, ALU.mult, ALU.add), inc=True)
            p.wait("scalar", t1)
            t2 = p.op("scalar", lambda e, rcol=rcol: e.activation(rcol, rcol, AF.Ln),
                      inc=True)
            p.wait("scalar", t2)
            t3 = p.op("scalar", lambda e, rcol=rcol: e.activation(rcol, rcol, AF.Exp,
                                                                  scale=-0.5), inc=True)
            p.wait("vector", t3)
            xnv = None
            for i in range(NT):
                p.op("vector", lambda e, i=i, rcol=rcol: e.tensor_scalar_mul(
                    s_t[:, i, :], s_z[:, i, :], rcol[:, i:i + 1]))
                xnv = p.op("vector", lambda e, i=i: e.tensor_tensor(
                    s_xn16[:, i, :], s_t[:, i, :], s_nwbc[:], ALU.mult),
                    inc=(i == NT - 1))
            p.wait("tensor", xnv)
            tr_tm2fm(p, s_xn16, s_xnF, [p_k[6], p_k[7]], s_id16, NJ, pg, gidx=(6, 7))
            xnF_ok = p.mark("vector")
            p.wait("tensor", xnF_ok)
            if c == 0:
                ttok = p.op("vector", lambda e: e.memset(s_xi[:, :, 0:3], 0.0), inc=True)
            else:
                ttok = p.op("vector", lambda e: e.tensor_copy(
                    s_xi[:, :, 0:3], s_xi[:, :, TC:TC + 3]), inc=True)
            for half in range(2):
                p.wait("gpsimd", p.mark("tensor"))
                wld = p.dma("gpsimd", s_w[:],
                            h["wipT"][:, half * DI:(half + 1) * DI].rearrange(
                                "(kc p) n -> p kc n", p=128), stream="wld")
                p.wait("tensor", wld)
                if half == 0:
                    p.wait("tensor", ttok)
                for no in range(NDB):
                    alt = no % 2
                    p.wait("tensor", pg[alt][0])
                    mm = None
                    for kc in range(NJ):
                        mm = p.op("tensor",
                                  lambda e, no=no, kc=kc, alt=alt: e.matmul(
                                      p_k[alt][:], s_w[:, kc, no * 128:(no + 1) * 128],
                                      s_xnF[:, kc, :], start=(kc == 0),
                                      stop=(kc == NJ - 1)),
                                  inc=(kc == NJ - 1))
                    if half == 0:
                        p.wait("vector", mm)
                        pg[alt][0] = p.op("vector",
                                          lambda e, no=no, alt=alt: e.tensor_copy(
                                              s_xi[:, no, 3:TC + 3], p_k[alt][:]),
                                          inc=True)
                    else:
                        p.wait("scalar", mm)
                        pg[alt][0] = p.op("scalar",
                                          lambda e, no=no, alt=alt: e.activation(
                                              s_g16[:, no, :], p_k[alt][:], AF.Silu),
                                          inc=True)
            xiv = p.mark("vector")
            p.wait("vector", xiv)
            prev_silu = None
            for db in range(NDB):
                p.wait("vector", prev_silu)
                t0 = p.op("vector", lambda e, db=db: e.tensor_scalar_mul(
                    s_cacc[:], s_xi[:, db, 0:TC], s_cw[:, db, 0:1]), inc=True)
                for k in range(1, 4):
                    p.wait("vector", t0)
                    t0 = p.op("vector", lambda e, db=db, k=k: e.scalar_tensor_tensor(
                        s_cacc[:], s_xi[:, db, k:k + TC], s_cw[:, db, k:k + 1],
                        s_cacc[:], ALU.mult, ALU.add), inc=True)
                p.wait("scalar", t0)
                prev_silu = p.op("scalar", lambda e, db=db: e.activation(
                    s_xc16[:, db, :], s_cacc[:], AF.Silu,
                    bias=s_cols[:, NDB + db:NDB + db + 1]), inc=True)
            cdone = p.mark("scalar")
            gdone = p.mark("scalar")
            p.wait("gpsimd", cdone)
            p.wait("gpsimd", gdone)
            st_xc = p.dma("gpsimd", xc_d[:, c * TC:(c + 1) * TC].rearrange(
                "(db pp) t -> pp db t", pp=128), s_xc16[:], stream="xcst")
            st_g = p.dma("gpsimd", g_d[:, c * TC:(c + 1) * TC].rearrange(
                "(db pp) t -> pp db t", pp=128), s_g16[:], stream="gst")
            war = [st_xc, st_g, p.mark("tensor")]
        return None

    with contextlib.ExitStack() as pc:
        _passM1(pc)
    p.barrier()

    def _passM2(pc):
        def msb(name, shape, dt):
            return pc.enter_context(nc.sbuf_tensor(name, shape, dt))
        s_wdt = msb("m2_wdt", [128, NDB, DI], BF16)
        s_xpB = msb("m2_xpB", [128, NDB, 64], BF16)
        s_A = msb("m2_A", [128, NDB, 64], F32)
        s_xc16 = msb("m2_xc", [128, NDB, TC], BF16)
        s_g16 = msb("m2_g", [128, NDB, TC], BF16)
        s_dt16 = msb("m2_dt", [128, NDB, TC], BF16)
        s_B = msb("m2_B", [64, TC], BF16)
        s_Bbc = msb("m2_Bbc", [128, NG, TC], BF16)
        s_a = msb("m2_a", [128, 2, NG, TC], F32)
        s_b = msb("m2_b", [128, NG, TC], BF16)
        s_hh = msb("m2_h", [128, NG, TC], BF16)
        s_fix = msb("m2_fix", [128, NG], F32)
        s_state = msb("m2_state", [128, NDB * 64], F32)
        s_ym = msb("m2_ym", [128, NDB, TC], BF16)
        s_t1 = msb("m2_t1", [128, TC], F32)
        s_t2 = msb("m2_t2", [128, TC], F32)
        wld = p.dma("gpsimd", s_wdt[:], h["wdtT"].rearrange("(kc p) n -> p kc n", p=128),
                    stream="wld")
        wld = p.dma("gpsimd", s_xpB[:], h["xpBT"].rearrange("(kc p) n -> p kc n", p=128),
                    stream="wld")
        al = p.dma("sync", s_A[:], h["A_log"].rearrange("(db pp) n -> pp db n", pp=128),
                   stream="cst")
        p.wait("scalar", al)
        av = p.op("scalar", lambda e: e.activation(
            s_A[:].rearrange("p a b -> p (a b)"), s_A[:].rearrange("p a b -> p (a b)"),
            AF.Exp), inc=True)
        p.wait("vector", av)
        av = p.op("vector", lambda e: e.tensor_scalar_mul(
            s_A[:].rearrange("p a b -> p (a b)"),
            s_A[:].rearrange("p a b -> p (a b)"), -1.0), inc=True)
        stz = p.op("vector", lambda e: e.memset(s_state[:], 0.0), inc=True)
        war = []
        st_ym = None
        bst = None
        p._m2_it = 0
        p._m2_scan_v2 = {0: None, 1: None}
        for c in range(NCH):
            p.waits("sync", war)
            xcl = p.dma("sync", s_xc16[:], xc_d[:, c * TC:(c + 1) * TC].rearrange(
                "(db pp) t -> pp db t", pp=128), stream="xcld")
            gl = p.dma("sync", s_g16[:], g_d[:, c * TC:(c + 1) * TC].rearrange(
                "(db pp) t -> pp db t", pp=128), stream="gld")
            p.wait("tensor", xcl)
            if c == 0:
                p.wait("tensor", wld)
            for no in range(NDB):
                alt = no % 2
                p.wait("tensor", pg[alt][0])
                mm = None
                for kc in range(NDB):
                    mm = p.op("tensor",
                              lambda e, no=no, kc=kc, alt=alt: e.matmul(
                                  p_k[alt][:], s_wdt[:, kc, no * 128:(no + 1) * 128],
                                  s_xc16[:, kc, :], start=(kc == 0),
                                  stop=(kc == NDB - 1)),
                              inc=(kc == NDB - 1))
                dbcol = s_cols[:, no:no + 1]
                p.wait("scalar", mm)
                p.wait("scalar", getattr(p, "_m2_sp_v", None))
                u1 = p.op("scalar", lambda e, alt=alt, dbcol=dbcol: e.activation(
                    s_t1[:], p_k[alt][:], AF.Abs, bias=dbcol), inc=True)
                p.wait("scalar", u1)
                u2 = p.op("scalar", lambda e: e.activation(
                    s_t1[:], s_t1[:], AF.Exp, scale=-1.0), inc=True)
                p.wait("scalar", u2)
                u3 = p.op("scalar", lambda e: e.activation(
                    s_t1[:], s_t1[:], AF.Ln, bias=1.0), inc=True)
                u4 = p.op("scalar", lambda e, alt=alt, dbcol=dbcol: e.activation(
                    s_t2[:], p_k[alt][:], AF.Relu, bias=dbcol), inc=True)
                p.wait("vector", u3)
                p.wait("vector", u4)
                pg[alt][0] = p.op("vector", lambda e, no=no: e.tensor_tensor(
                    s_dt16[:, no, :], s_t1[:], s_t2[:], ALU.add), inc=True)
                p._m2_sp_v = pg[alt][0]
            p.wait("tensor", pg[2][0])
            mm = None
            for kc in range(NDB):
                mm = p.op("tensor",
                          lambda e, kc=kc: e.matmul(
                              p_k[2][0:64, :], s_xpB[:, kc, :], s_xc16[:, kc, :],
                              start=(kc == 0), stop=(kc == NDB - 1)),
                          inc=(kc == NDB - 1))
            p.wait("vector", mm)
            if c > 0:
                p.wait("vector", bst)
            bev = p.op("vector", lambda e: e.tensor_copy(s_B[:], p_k[2][0:64, :]),
                       inc=True)
            pg[2][0] = bev
            p.wait("gpsimd", bev)
            bst = p.dma("gpsimd", B_d[:, c * TC:(c + 1) * TC], s_B[:], stream="Bst")
            dtdone = p.mark("vector")
            last_ym = None
            for halfdb in range(2):
                ymt = [None] * 8
                for ng in range(NGRP):
                    p.wait("sync", bst)
                    p.wait("sync", last_ym)
                    bbc = p.dma("sync", s_Bbc[:],
                                B_d[ng * NG:(ng + 1) * NG,
                                    c * TC:(c + 1) * TC].partition_broadcast(128),
                                stream="Bbc")
                    for dbi in range(8):
                        db = halfdb * 8 + dbi
                        par = p._m2_it % 2
                        p._m2_it += 1
                        s_av = s_a[:, par]
                        at = None
                        p.wait("scalar", dtdone if ng == 0 else None)
                        p.wait("scalar", p._m2_scan_v2[par])
                        for n in range(NG):
                            at = p.op("scalar",
                                      lambda e, db=db, ng=ng, n=n, s_av=s_av:
                                      e.activation(
                                          s_av[:, n, :], s_dt16[:, db, :], AF.Exp,
                                          scale=s_A[:, db, ng * NG + n:ng * NG + n + 1]),
                                      inc=(n == NG - 1))
                        p.wait("vector", bbc)
                        xrep = s_xc16[:, db, :].unsqueeze(1).broadcast_to((128, NG, TC))
                        bt = p.op("vector", lambda e, db=db, xrep=xrep: e.tensor_tensor(
                            s_b[:], xrep, s_Bbc[:], ALU.mult), inc=True)
                        a0 = s_av[:, :, 0:1].squeeze(2)
                        b0 = s_b[:, :, 0:1].squeeze(2)
                        scols = s_state[:, db * 64 + ng * NG: db * 64 + (ng + 1) * NG]
                        p.wait("vector", at)
                        f1 = p.op("vector", lambda e, a0=a0, scols=scols: e.tensor_tensor(
                            s_fix[:], a0, scols, ALU.mult), inc=True)
                        p.wait("vector", f1)
                        p.wait("vector", bt)
                        f2 = p.op("vector", lambda e, b0=b0: e.tensor_tensor(
                            b0, s_fix[:], b0, ALU.add), inc=True)
                        f3 = p.op("vector", lambda e, a0=a0: e.memset(a0, 0.0), inc=True)
                        p.wait("vector", f2)
                        p.wait("vector", f3)
                        p.wait("vector", last_ym)
                        if dbg is not None and dbg.get("_const_h"):
                            sc = p.op("vector", lambda e: e.memset(
                                s_hh[:], 1.0), inc=True)
                        else:
                            sc = p.op("vector", lambda e, s_av=s_av:
                                      e.tensor_tensor_scan(
                                          s_hh[:].rearrange("p a b -> p (a b)"),
                                          s_av[:].rearrange("p a b -> p (a b)"),
                                          s_b[:].rearrange("p a b -> p (a b)"),
                                          0.0, ALU.mult, ALU.add), inc=True)
                        p.wait("vector", sc)
                        sv = p.op("vector", lambda e, scols=scols: e.tensor_copy(
                            scols, s_hh[:, :, TC - 1:TC].squeeze(2)), inc=True)
                        p._m2_scan_v = sv
                        p._m2_scan_v2[par] = sv
                        if dbg is not None and dbg.get("_h_sel") == (c, db, ng):
                            p.wait("gpsimd", sv)
                            p.dma("gpsimd", dbg["hdump"], s_hh[:], stream="dbg")
                            p.dma("gpsimd", dbg["adump"], s_av[:], stream="dbg")
                            dtk = p.dma("gpsimd", dbg["bdump"], s_b[:], stream="dbg")
                            p.wait("vector", dtk)
                            p.wait("tensor", dtk)
                            p.wait("scalar", dtk)
                        p.wait("tensor", sc)
                        if ng == 0:
                            p.wait("tensor", pg[dbi][0])
                        for n in range(NG):
                            p.op("tensor", lambda e, dbi=dbi, n=n, ng=ng: e.matmul(
                                p_k[dbi][:], s_id16[:], s_hh[:, n, :],
                                start=(ng == 0 and n == 0),
                                stop=(ng == NGRP - 1 and n == NG - 1),
                                skip_group_check=True),
                                 inc=(n == NG - 1))
                        last_ym = ymt[dbi] = p.mark("tensor")
                p.wait("vector", gl)
                for dbi in range(8):
                    db = halfdb * 8 + dbi
                    p.wait("vector", ymt[dbi])
                    dcol = s_cols[:, 2 * NDB + db:2 * NDB + db + 1]
                    p.op("vector", lambda e, db=db, dcol=dcol: e.tensor_scalar_mul(
                        s_t1[:], s_xc16[:, db, :], dcol))
                    p.op("vector", lambda e, dbi=dbi: e.tensor_tensor(
                        s_t1[:], s_t1[:], p_k[dbi][:], ALU.add))
                    pg[dbi][0] = p.op("vector",
                                      lambda e, db=db: e.tensor_tensor(
                                          s_ym[:, db, :], s_t1[:], s_g16[:, db, :],
                                          ALU.mult), inc=True)
            ymdone = p.mark("vector")
            p._m2_sp_v = ymdone
            p.wait("gpsimd", ymdone)
            st_ym = p.dma("gpsimd", ym_d[:, c * TC:(c + 1) * TC].rearrange(
                "(db pp) t -> pp db t", pp=128), s_ym[:], stream="ymst")
            war = [st_ym, p.mark("tensor"), xcl, gl]
        return None

    with contextlib.ExitStack() as pc:
        _passM2(pc)
    p.barrier()

    def _passM3(pc):
        def msb(name, shape, dt):
            return pc.enter_context(nc.sbuf_tensor(name, shape, dt))
        s_x = msb("m3_x", [128, NT, D], F32)
        s_z = msb("m3_z", [128, NT, D], F32)
        s_t = msb("m3_t", [128, NT, D], F32)
        s_ym = msb("m3_ym", [128, NDB, TC], BF16)
        s_wop = msb("m3_wop", [128, NDB, D], BF16)
        s_o2 = msb("m3_o2", [128, 2, TC], F32)
        wld = p.dma("gpsimd", s_wop[:], h["wopT"].rearrange("(kc p) n -> p kc n", p=128),
                    stream="wld")
        war = []
        out_tok = None
        for c in range(NCH):
            p.waits("sync", war)
            yl = p.dma("sync", s_ym[:], ym_d[:, c * TC:(c + 1) * TC].rearrange(
                "(db pp) t -> pp db t", pp=128), stream="ymld")
            lv, ztok = ln_z(c, [], s_x, s_z, s_o2[:].rearrange("p a b -> p (a b)"))
            sqt = p._sq_tok
            p.wait("tensor", yl)
            if c == 0:
                p.wait("tensor", wld)
            trv = [None, None]
            for no in range(NJ):
                alt = no % 2
                p.wait("tensor", pg[alt][0])
                mm = None
                for kc in range(NDB):
                    mm = p.op("tensor",
                              lambda e, no=no, kc=kc, alt=alt: e.matmul(
                                  p_k[alt][:], s_wop[:, kc, no * 128:(no + 1) * 128],
                                  s_ym[:, kc, :], start=(kc == 0), stop=(kc == NDB - 1)),
                              inc=(kc == NDB - 1))
                p.wait("vector", mm)
                p.wait("vector", trv[alt])
                if no < 2:
                    p.wait("vector", sqt)
                ev = p.op("vector", lambda e, alt=alt: e.tensor_copy(
                    s_o2[:, alt, :], p_k[alt][:]), inc=True)
                pg[alt][0] = ev
                p.wait("tensor", ev)
                if no == 0:
                    p.wait("tensor", ztok)
                p.wait("tensor", pg[4 + alt][0])
                tr = None
                for i in range(NT):
                    tr = p.op("tensor",
                              lambda e, i=i, alt=alt: e.transpose(
                                  p_k[4 + alt][:, i * 128:(i + 1) * 128],
                                  s_o2[:, alt, i * 128:(i + 1) * 128], s_id32[:]),
                              inc=(i == NT - 1))
                trv[alt] = tr
                p.wait("vector", tr)
                pg[4 + alt][0] = p.op(
                    "vector",
                    lambda e, no=no, alt=alt: e.tensor_tensor(
                        s_t[:, :, no * 128:(no + 1) * 128],
                        p_k[4 + alt][:].rearrange("p (i w) -> p i w", i=NT),
                        s_z[:, :, no * 128:(no + 1) * 128], ALU.add),
                    inc=True)
            pv = p.mark("vector")
            rstd_tok, mu_tok = emit_ln_stats(p, s_t, NT, D, s_mu, s_rstd,
                                             s_o2[:].rearrange("p a b -> p (a b)"),
                                             1e-5, wait_vec=[], wait_sc=[pv])
            emit_ln_apply(p, s_t, s_t, NT, s_mu, s_rstd, rstd_tok, mu_tok)
            ov = p.op("vector", lambda e: e.tensor_tensor(s_t[:], s_t[:], s_x[:],
                                                          ALU.add), inc=True)
            p.wait("gpsimd", ov)
            out_tok = p.dma("gpsimd",
                            h["xout"][c * TC:(c + 1) * TC, :].rearrange(
                                "(i p) d -> p i d", p=128), s_t[:], stream="xst")
            war = [out_tok, p.mark("tensor")]
        return out_tok

    with contextlib.ExitStack() as pc:
        out_tok = _passM3(pc)
    p.wait("gpsimd", out_tok)
    return out_tok


def AP_diag(s_diag, db):
    t = s_diag[:, db, :]
    lst = [list(x) for x in t.ap]
    lst[0][0] = lst[0][0] + 1
    lst[-1] = [1, 1]
    return bass.AP(t.tensor, t.offset, lst)


def build_mamba_nc(debug=False, h_sel=None):
    nc = bass.Bass("TRN2", target_bir_lowering=False, debug=False)
    dbg = None
    h = {}
    h["xin"] = nc.dram_tensor("xin", [T, D], F32, kind="ExternalInput").ap()
    h["xout"] = nc.dram_tensor("xout", [T, D], F32, kind="ExternalOutput").ap()
    h["norm_w"] = nc.dram_tensor("norm_w", [1, D], F32, kind="ExternalInput").ap()
    h["wipT"] = nc.dram_tensor("wipT", [D, 2 * DI], F32, kind="ExternalInput").ap()
    h["wdtT"] = nc.dram_tensor("wdtT", [DI, DI], F32, kind="ExternalInput").ap()
    h["wopT"] = nc.dram_tensor("wopT", [DI, D], F32, kind="ExternalInput").ap()
    h["xpBT"] = nc.dram_tensor("xpBT", [DI, 64], F32, kind="ExternalInput").ap()
    h["A_log"] = nc.dram_tensor("A_log", [DI, 64], F32, kind="ExternalInput").ap()
    h["dt_b"] = nc.dram_tensor("dt_b", [128, NDB], F32, kind="ExternalInput").ap()
    h["conv_b"] = nc.dram_tensor("conv_b", [128, NDB], F32, kind="ExternalInput").ap()
    h["D_param"] = nc.dram_tensor("D_param", [128, NDB], F32, kind="ExternalInput").ap()
    h["conv_w"] = nc.dram_tensor("conv_w", [128, NDB, 4], F32, kind="ExternalInput").ap()
    h["ident16"] = nc.dram_tensor("ident16", [128, 128], BF16, kind="ExternalInput").ap()
    h["ident32"] = nc.dram_tensor("ident32", [128, 128], F32, kind="ExternalInput").ap()
    if debug or h_sel is not None:
        dbg = dict(hdump=nc.dram_tensor("dbg_h", [128, NG, TC], F32,
                                        kind="ExternalOutput").ap(),
                   adump=nc.dram_tensor("dbg_a", [128, NG, TC], F32,
                                        kind="ExternalOutput").ap(),
                   bdump=nc.dram_tensor("dbg_b", [128, NG, TC], F32,
                                        kind="ExternalOutput").ap())
        if h_sel is not None:
            dbg["_h_sel"] = h_sel
    with contextlib.ExitStack() as ctx:
        p = Prog(nc, ctx)
        def sb(name, shape, dt):
            return ctx.enter_context(nc.sbuf_tensor(name, shape, dt))
        def ps(name, shape, dt):
            return ctx.enter_context(nc.psum_tensor(name, shape, dt))
        build_mamba_block(p, sb, ps, h, dbg)
        p.emit()
    return nc


def col16(v):
    return np.ascontiguousarray(v.reshape(NDB, 128).T, np.float32)


def mamba_in_map(x, norm_w, in_proj_w, conv_w, conv_b, x_proj_w, dt_w, dt_b,
                 A_log, D_param, out_proj_w):
    id16, id32 = _ident_np()
    return dict(
        xin=np.ascontiguousarray(x, np.float32),
        norm_w=np.ascontiguousarray(norm_w.reshape(1, D), np.float32),
        wipT=np.ascontiguousarray(in_proj_w.T, np.float32),
        wdtT=np.ascontiguousarray(dt_w.T, np.float32),
        wopT=np.ascontiguousarray(out_proj_w.T, np.float32),
        xpBT=np.ascontiguousarray(x_proj_w[64:128, :].T, np.float32),
        A_log=np.ascontiguousarray(A_log, np.float32),
        dt_b=col16(dt_b), conv_b=col16(conv_b), D_param=col16(D_param),
        conv_w=np.ascontiguousarray(
            conv_w[:, 0, :].reshape(NDB, 128, 4).transpose(1, 0, 2), np.float32),
        ident16=id16, ident32=id32,
    )



def kernel_unfused(**inp):
    x = np.ascontiguousarray(np.asarray(inp["L0_attn_x"], np.float32))
    ncores = x.shape[0]
    core_ids = list(range(ncores))

    nc1 = build_mamba_nc()
    base = mamba_in_map(
        x[0], np.asarray(inp["L0_attn_norm_w"]), np.asarray(inp["L0_attn_in_proj_w"]),
        np.asarray(inp["L0_attn_conv_w"]), np.asarray(inp["L0_attn_conv_b"]),
        np.asarray(inp["L0_attn_x_proj_w"]), np.asarray(inp["L0_attn_dt_w"]),
        np.asarray(inp["L0_attn_dt_b"]), np.asarray(inp["L0_attn_A_log"]),
        np.asarray(inp["L0_attn_D_param"]), np.asarray(inp["L0_attn_out_proj_w"]))
    in_maps = [dict(base, xin=np.ascontiguousarray(x[i])) for i in range(ncores)]
    res = run_bass_kernel_spmd(nc1, in_maps, core_ids=core_ids)
    x = [np.asarray(res.results[i]["xout"]) for i in range(ncores)]

    nc2 = build_ffn_nc()
    base = ffn_in_map(
        x[0], np.asarray(inp["L0_ffn_ln_gamma"]), np.asarray(inp["L0_ffn_ln_beta"]),
        np.asarray(inp["L0_ffn_fc1_q"]), np.asarray(inp["L0_ffn_fc1_scales"]),
        np.asarray(inp["L0_ffn_fc1_bias"]), np.asarray(inp["L0_ffn_fc2_q"]),
        np.asarray(inp["L0_ffn_fc2_scales"]), np.asarray(inp["L0_ffn_fc2_bias"]))
    in_maps = [dict(base, xin=x[i]) for i in range(ncores)]
    res = run_bass_kernel_spmd(nc2, in_maps, core_ids=core_ids)
    x = [np.asarray(res.results[i]["xout"]) for i in range(ncores)]

    nc3 = build_latt_nc()
    base = latt_in_map(
        x[0], np.asarray(inp["L1_attn_Wq"]), np.asarray(inp["L1_attn_bq"]),
        np.asarray(inp["L1_attn_Wk"]), np.asarray(inp["L1_attn_bk"]),
        np.asarray(inp["L1_attn_Wv"]), np.asarray(inp["L1_attn_bv"]),
        np.asarray(inp["L1_attn_Wp"]), np.asarray(inp["L1_attn_bp"]))
    in_maps = [dict(base, xin=x[i]) for i in range(ncores)]
    res = run_bass_kernel_spmd(nc3, in_maps, core_ids=core_ids)
    x = [np.asarray(res.results[i]["xout"]) for i in range(ncores)]

    nc4 = build_lffn_nc()
    base = lffn_in_map(
        x[0], np.asarray(inp["L1_ffn_wqkv"]), np.asarray(inp["L1_ffn_q_scale"]),
        np.asarray(inp["L1_ffn_k_scale"]), np.asarray(inp["L1_ffn_wo"]),
        np.asarray(inp["L1_ffn_ln_g"]), np.asarray(inp["L1_ffn_ln_b"]),
        np.asarray(inp["L1_ffn_w1"]), np.asarray(inp["L1_ffn_b1"]),
        np.asarray(inp["L1_ffn_w2"]), np.asarray(inp["L1_ffn_b2"]))
    in_maps = [dict(base, xin=x[i]) for i in range(ncores)]
    res = run_bass_kernel_spmd(nc4, in_maps, core_ids=core_ids)
    out = np.stack([np.asarray(res.results[i]["xout"]) for i in range(ncores)], 0)
    return out.astype(np.float32)



def build_fused_nc(nblocks=4):
    nc = bass.Bass("TRN2", target_bir_lowering=False, debug=False)
    FF = 4096
    x0 = nc.dram_tensor("x0", [T, D], F32, kind="ExternalInput").ap()
    xout = nc.dram_tensor("xout", [T, D], F32, kind="ExternalOutput").ap()
    x1 = nc.dram_tensor("x1", [T, D], F32).ap()
    x2 = nc.dram_tensor("x2", [T, D], F32).ap()
    x3 = nc.dram_tensor("x3", [T, D], F32).ap()
    id16 = nc.dram_tensor("ident16", [128, 128], BF16, kind="ExternalInput").ap()
    id32 = nc.dram_tensor("ident32", [128, 128], F32, kind="ExternalInput").ap()

    outs = [x1, x2, x3, xout]
    outs[nblocks - 1] = xout
    h1 = dict(xin=x0, xout=outs[0], ident16=id16, ident32=id32)
    h1["norm_w"] = nc.dram_tensor("norm_w", [1, D], F32, kind="ExternalInput").ap()
    h1["wipT"] = nc.dram_tensor("wipT", [D, 2 * DI], F32, kind="ExternalInput").ap()
    h1["wdtT"] = nc.dram_tensor("wdtT", [DI, DI], F32, kind="ExternalInput").ap()
    h1["wopT"] = nc.dram_tensor("wopT", [DI, D], F32, kind="ExternalInput").ap()
    h1["xpBT"] = nc.dram_tensor("xpBT", [DI, 64], F32, kind="ExternalInput").ap()
    h1["A_log"] = nc.dram_tensor("A_log", [DI, 64], F32, kind="ExternalInput").ap()
    for nm in ("dt_b", "conv_b", "D_param"):
        h1[nm] = nc.dram_tensor(nm, [128, NDB], F32, kind="ExternalInput").ap()
    h1["conv_w"] = nc.dram_tensor("conv_w", [128, NDB, 4], F32, kind="ExternalInput").ap()

    h2 = dict(xin=outs[0], xout=outs[1], ident16=id16, ident32=id32)
    h2["ln_gamma"] = nc.dram_tensor("ln_gamma", [1, D], F32, kind="ExternalInput").ap()
    h2["ln_beta"] = nc.dram_tensor("ln_beta", [1, D], F32, kind="ExternalInput").ap()
    h2["fc1_q"] = nc.dram_tensor("fc1_q", [D, FF], I32, kind="ExternalInput").ap()
    h2["fc1_scales"] = nc.dram_tensor("fc1_scales", [D // 128, FF], F32, kind="ExternalInput").ap()
    h2["fc1_bias"] = nc.dram_tensor("fc1_bias", [128, FF // 128], F32, kind="ExternalInput").ap()
    h2["fc2_q"] = nc.dram_tensor("fc2_q", [FF, D], I32, kind="ExternalInput").ap()
    h2["fc2_scales"] = nc.dram_tensor("fc2_scales", [FF // 128, D], F32, kind="ExternalInput").ap()
    h2["fc2_bias"] = nc.dram_tensor("fc2_bias", [128, D // 128], F32, kind="ExternalInput").ap()

    h3 = dict(xin=outs[1], xout=outs[2], ident16=id16, ident32=id32)
    for nm in ("Wq", "Wk", "Wv", "Wp"):
        h3[nm] = nc.dram_tensor(nm, [D, D], F32, kind="ExternalInput").ap()
    for nm in ("bq", "bk", "bv", "bp"):
        h3[nm] = nc.dram_tensor(nm, [128, D // 128], F32, kind="ExternalInput").ap()

    h4 = dict(xin=outs[2], xout=outs[3], ident16=id16, ident32=id32)
    h4["wqkv"] = [nc.dram_tensor(f"wqkv{l}", [D, 3 * INNER], F32, kind="ExternalInput").ap()
                  for l in range(2)]
    h4["wo"] = [nc.dram_tensor(f"wo{l}", [INNER, D], F32, kind="ExternalInput").ap()
                for l in range(2)]
    h4["w1p"] = [nc.dram_tensor(f"w1p{l}", [D, 2 * HID], F32, kind="ExternalInput").ap()
                 for l in range(2)]
    h4["w2"] = [nc.dram_tensor(f"w2_{l}", [HID, D], F32, kind="ExternalInput").ap()
                for l in range(2)]
    for nm, shp, dt in (("qkscale", [128, 4], F32), ("lngb", [128, 32], F32),
                        ("lb1", [128, 88], F32), ("lb2", [128, 16], F32),
                        ("rot128", [128, 128], BF16), ("lmask", [8 * 128, LW], F32),
                        ("selh", [4 * 128, 128], BF16), ("b8", [8, 4 * 128], BF16),
                        ("onesf", [128, 128], F32), ("ones16", [128, 128], BF16)):
        h4[nm] = nc.dram_tensor(nm, shp, dt, kind="ExternalInput").ap()
    h4["tables"] = [nc.dram_tensor(f"tab{i}", [64, T], F32, kind="ExternalInput").ap()
                    for i in range(4)]

    with contextlib.ExitStack() as ctx:
        p = Prog(nc, ctx)
        pool = [ctx.enter_context(nc.psum_tensor(f"pool{i}", [128, 512], F32))
                for i in range(8)]

        def run_block(builder, h):
            with contextlib.ExitStack() as bctx:
                def sb(name, shape, dt):
                    return bctx.enter_context(nc.sbuf_tensor(name, shape, dt))
                nxt = {"i": 0}
                def ps(name, shape, dt):
                    t = pool[nxt["i"]]
                    nxt["i"] += 1
                    return t
                builder(p, sb, ps, h)
            p.barrier()

        blocks = [(build_mamba_block, h1), (build_ffn_block, h2),
                  (build_latt_block, h3), (build_lffn_block, h4)]
        for bi in range(nblocks):
            run_block(*blocks[bi])
        p.emit()
    return nc


def kernel(**inp):
    x = np.ascontiguousarray(np.asarray(inp["L0_attn_x"], np.float32))
    ncores = x.shape[0]
    import os as _os
    nc = build_fused_nc(int(_os.environ.get("MK_NBLOCKS", "4")))
    base = {}
    base.update(mamba_in_map(
        x[0], np.asarray(inp["L0_attn_norm_w"]), np.asarray(inp["L0_attn_in_proj_w"]),
        np.asarray(inp["L0_attn_conv_w"]), np.asarray(inp["L0_attn_conv_b"]),
        np.asarray(inp["L0_attn_x_proj_w"]), np.asarray(inp["L0_attn_dt_w"]),
        np.asarray(inp["L0_attn_dt_b"]), np.asarray(inp["L0_attn_A_log"]),
        np.asarray(inp["L0_attn_D_param"]), np.asarray(inp["L0_attn_out_proj_w"])))
    base.update(ffn_in_map(
        x[0], np.asarray(inp["L0_ffn_ln_gamma"]), np.asarray(inp["L0_ffn_ln_beta"]),
        np.asarray(inp["L0_ffn_fc1_q"]), np.asarray(inp["L0_ffn_fc1_scales"]),
        np.asarray(inp["L0_ffn_fc1_bias"]), np.asarray(inp["L0_ffn_fc2_q"]),
        np.asarray(inp["L0_ffn_fc2_scales"]), np.asarray(inp["L0_ffn_fc2_bias"])))
    base.update(latt_in_map(
        x[0], np.asarray(inp["L1_attn_Wq"]), np.asarray(inp["L1_attn_bq"]),
        np.asarray(inp["L1_attn_Wk"]), np.asarray(inp["L1_attn_bk"]),
        np.asarray(inp["L1_attn_Wv"]), np.asarray(inp["L1_attn_bv"]),
        np.asarray(inp["L1_attn_Wp"]), np.asarray(inp["L1_attn_bp"])))
    base.update(lffn_in_map(
        x[0], np.asarray(inp["L1_ffn_wqkv"]), np.asarray(inp["L1_ffn_q_scale"]),
        np.asarray(inp["L1_ffn_k_scale"]), np.asarray(inp["L1_ffn_wo"]),
        np.asarray(inp["L1_ffn_ln_g"]), np.asarray(inp["L1_ffn_ln_b"]),
        np.asarray(inp["L1_ffn_w1"]), np.asarray(inp["L1_ffn_b1"]),
        np.asarray(inp["L1_ffn_w2"]), np.asarray(inp["L1_ffn_b2"])))
    del base["xin"]
    in_maps = [dict(base, x0=np.ascontiguousarray(x[i])) for i in range(ncores)]
    res = run_bass_kernel_spmd(nc, in_maps, core_ids=list(range(ncores)))
    out = np.stack([np.asarray(res.results[i]["xout"]) for i in range(ncores)], 0)
    return out.astype(np.float32)
